# version 01
# baseline (speedup 1.0000x reference)
import math
import contextlib
import numpy as np
import ml_dtypes
import concourse.bass as bass
import concourse.mybir as mybir
from concourse.bass_utils import run_bass_kernel_spmd

F32 = mybir.dt.float32
BF16 = mybir.dt.bfloat16
U8 = mybir.dt.uint8
ALU = mybir.AluOpType
ACT = mybir.ActivationFunctionType
AXL = mybir.AxisListType

B, S, D = 2, 4096, 4096
NH, HD, NG, KTAP = 128, 64, 8, 4
CHL = 128
INTER = NH * HD
CONVD = INTER + 2 * NG * 128
PROJ = INTER + CONVD + NH
TR = 512
NCHUNK = TR // CHL
NKD = D // 128
NXT = INTER // 128
DT_MIN, DT_MAX, EPS = 1e-3, 100.0, 1e-5
SCALING = 16.0 / 32.0
NCT = 81

STAGE = 6
DEBUG = False


class Pl:
    ENGINES = ("sync", "gpsimd", "scalar", "vector", "tensor")

    def __init__(self, nc):
        self.nc = nc
        self.items = {e: [] for e in self.ENGINES}
        self.cnt = {}
        self.hand = {}
        self.hiw = {}

    def sem(self, stack, name):
        self.hand[name] = stack.enter_context(self.nc.semaphore(name))
        self.cnt[name] = 0
        return name

    def tok(self, sem):
        return (sem, self.cnt[sem])

    def emit(self, eng, fn, waits=(), inc=None, amt=1):
        tokv = None
        if inc is not None:
            self.cnt[inc] += amt
            tokv = (inc, self.cnt[inc])
        wl = []
        for w in waits:
            if w is None:
                continue
            s, v = w
            if v <= 0 or self.hiw.get((eng, s), 0) >= v:
                continue
            self.hiw[(eng, s)] = v
            wl.append((s, v))
        self.items[eng].append((fn, wl, tokv, amt))
        return tokv

    @staticmethod
    def _spans(args):
        out = []
        for a in args:
            name = getattr(getattr(a, "tensor", None), "name", None)
            if name is None:
                memset = getattr(a, "memsetref", None) or ""
                name = memset[:-4] if memset.endswith("_set") else memset
            ap = getattr(a, "ap", None)
            off = getattr(a, "offset", 0)
            if ap is None:
                continue
            span = 1
            for (st_, cn) in list(ap)[1:]:
                span += abs(st_) * (cn - 1)
            out.append((str(name), off, off + span))
        return out

    def _vgaps(self, gapfn):
        nc = self.nc
        gaps = set()
        intra = []
        with nc.discard():
            e = nc.vector
            recs = []
            for idx, (fn, wl, tokv, amt) in enumerate(self.items["vector"]):
                insts = []
                with _InstCatcher(nc, insts):
                    fn(e)
                for j, ins in enumerate(insts):
                    rspan = self._spans(getattr(ins, "ins", []) or [])
                    wspan = self._spans(getattr(ins, "outs", []) or [])
                    recs.append((idx, j, rspan, wspan))
            for k in range(1, len(recs)):
                pidx, pj, pr, pw = recs[k - 1]
                idx, j, rr, rw = recs[k]
                hit = False
                for (wn, wl_, wh) in pw:
                    if wn.startswith("c_gap"):
                        continue
                    for (rn, rl, rh) in rr:
                        if wn == rn and rl < wh and wl_ < rh:
                            hit = True
                if not hit:
                    continue
                if pidx == idx:
                    intra.append((idx, pj, j))
                else:
                    gaps.add(idx)
        if intra:
            raise AssertionError(f"intra-item vector RAW chains (add gaps inside): {intra[:10]}")
        return gaps

    def replay(self, block, gapfn=None):
        hand = self.hand
        vgaps = self._vgaps(gapfn) if gapfn is not None else set()

        def mk(eng):
            items = self.items[eng]

            def go(e):
                for idx, (fn, wl, tokv, amt) in enumerate(items):
                    if eng == "vector" and idx in vgaps:
                        gapfn(e)
                    for (s, v) in wl:
                        e.wait_ge(hand[s], v)
                    inst = fn(e)
                    if tokv is not None:
                        if inst is None:
                            inst = e.nop()
                        inst.then_inc(hand[tokv[0]], amt)
            return go

        for eng in self.ENGINES:
            getattr(block, eng)(mk(eng))


class _InstCatcher:

    def __init__(self, nc, sink):
        self.nc = nc
        self.sink = sink

    def __enter__(self):
        self.cb = lambda ins: self.sink.append(ins)
        self.nc.push_inst_callback(self.cb)
        return self

    def __exit__(self, *a):
        self.nc.remove_inst_callback(self.cb)
        return False


_HC = {}


def fourier_mt():
    if "mt" not in _HC:
        n = D
        j = np.arange(n, dtype=np.float64)
        Mt = np.empty((n, n), dtype=np.float64)
        Mt[0, :] = 1.0 / math.sqrt(n)
        maxf = (n - 1) // 2
        f = np.arange(1, maxf + 1, dtype=np.float64)
        ang = 2.0 * np.pi * np.outer(f, j) / n
        c = math.sqrt(2.0 / n)
        Mt[1:2 * maxf:2, :] = c * np.cos(ang)
        Mt[2:2 * maxf + 1:2, :] = c * np.sin(ang)
        Mt[n - 1, :] = ((-1.0) ** j) / math.sqrt(n)
        _HC["mt"] = np.ascontiguousarray(Mt.astype(np.float32))
    return _HC["mt"]


def _slab_image(wT, c0):
    a = wT[:, c0:c0 + 128]
    return a.reshape(D // 128, 128, 128).transpose(1, 0, 2).reshape(128, D)


def host_shared(inp):
    i = np.arange(128)
    sh = {}
    wti = np.asarray(inp["L0_attn_in_proj_w"]).T
    sh["wti"] = np.stack([_slab_image(wti, c * 128) for c in range(PROJ // 128)])
    wto = np.asarray(inp["L0_attn_out_proj_w"]).T
    im = []
    for dtl in range(NKD):
        a = wto[:, dtl * 128:(dtl + 1) * 128]
        im.append(a.reshape(64, 128, 128).transpose(1, 0, 2).reshape(128, 8192))
    sh["wto"] = np.stack(im)
    wbt = np.asarray(inp["L0_ffn_W_base"]).T
    sh["wbt"] = np.stack([_slab_image(wbt, c * 128) for c in range(NKD)])
    sh["mt"] = fourier_mt()
    cw = np.asarray(inp["L0_attn_conv_w"]).reshape(CONVD, KTAP)
    convd = np.zeros((CONVD // 128, 128, KTAP, 128), np.float32)
    t = np.arange(CONVD)
    for jj in range(KTAP):
        convd[t // 128, t % 128, jj, t % 128] = cw[:, jj]
    sh["convd"] = convd.reshape(CONVD // 128, 128, KTAP * 128)
    sh["convb"] = np.ascontiguousarray(np.asarray(inp["L0_attn_conv_b"]).reshape(CONVD // 128, 128).T)
    sh["dtb"] = np.asarray(inp["L0_attn_dt_bias"]).reshape(128, 1).astype(np.float32)
    sh["acol"] = (-(i + 1).astype(np.float32)).reshape(128, 1)
    sh["dcol"] = np.ascontiguousarray(np.repeat(np.asarray(inp["L0_attn_D"]), HD).reshape(NXT, 128).T)
    sh["nwcol"] = np.ascontiguousarray(np.asarray(inp["L0_attn_norm_w"]).reshape(NXT, 128).T)
    sh["bbcol"] = np.ascontiguousarray(np.asarray(inp["L0_ffn_b_base"]).reshape(NKD, 128).T)
    sh["enct"] = np.ascontiguousarray(
        np.concatenate([np.asarray(inp["L0_ffn_enc0"]).T, np.asarray(inp["L0_ffn_enc1"]).T], axis=1))
    co = np.concatenate([np.transpose(np.asarray(inp["L0_ffn_coefs0"]), (1, 0, 2)),
                         np.transpose(np.asarray(inp["L0_ffn_coefs1"]), (1, 0, 2))],
                        axis=2).reshape(D, 120)
    sh["coef"] = np.ascontiguousarray(co)
    cm = np.concatenate([np.transpose(np.asarray(inp["L0_ffn_mask0"]), (1, 0, 2)),
                         np.transpose(np.asarray(inp["L0_ffn_mask1"]), (1, 0, 2))],
                        axis=2).reshape(D, 120)
    sh["cmask"] = np.ascontiguousarray(cm.astype(np.uint8))
    sh["ident32"] = np.eye(128, dtype=np.float32)
    sh["identbf"] = np.eye(128, dtype=ml_dtypes.bfloat16)
    sh["lt1"] = (i[:, None] <= i[None, :]).astype(np.float32)
    sh["ones128"] = np.ones((128, 128), np.float32)
    sh["utm"] = (i[:, None] <= i[None, :]).astype(ml_dtypes.bfloat16)
    sh["ones1"] = np.ones((1, 128), np.float32)
    gc = np.zeros((8, 128, 8), dtype=ml_dtypes.bfloat16)
    for g in range(8):
        gc[g, :, g] = 1
    sh["gcol"] = gc
    return sh


def host_percore(inp, core):
    b, q = core // 4, core % 4
    x = np.asarray(inp["L0_attn_hidden_states"])[b]
    pc = {}
    for r in range(2):
        t0 = 2048 * r + 512 * q
        main = x[t0:t0 + TR]
        halo = x[t0 - 3:t0] if t0 >= 3 else np.zeros((3, D), np.float32)
        pc[f"x{r}"] = np.ascontiguousarray(np.concatenate([halo, main], axis=0))
    sel = np.zeros((128, 8), np.float32)
    for j in range(3):
        sel[:, j] = 1.0 if j < q else 0.0
        sel[:, 4 + j] = 0.0 if j < q else 1.0
    pc["selm"] = sel
    return pc


def build(stage=STAGE, debug=DEBUG):
    nc = bass.Bass("TRN2", target_bir_lowering=False, detect_race_conditions=False)
    st = contextlib.ExitStack()

    dxr = [nc.dram_tensor(f"x{r}", [3 + TR, D], F32, kind="ExternalInput") for r in range(2)]
    dwti = nc.dram_tensor("wti", [PROJ // 128, 128, D], F32, kind="ExternalInput")
    dwto = nc.dram_tensor("wto", [NKD, 128, INTER], F32, kind="ExternalInput")
    dwbt = nc.dram_tensor("wbt", [NKD, 128, D], F32, kind="ExternalInput")
    dmt = nc.dram_tensor("mt", [D, D], F32, kind="ExternalInput")
    dconvd = nc.dram_tensor("convd", [CONVD // 128, 128, KTAP * 128], F32, kind="ExternalInput")
    dconvb = nc.dram_tensor("convb", [128, CONVD // 128], F32, kind="ExternalInput")
    ddtb = nc.dram_tensor("dtb", [128, 1], F32, kind="ExternalInput")
    dacol = nc.dram_tensor("acol", [128, 1], F32, kind="ExternalInput")
    ddcol = nc.dram_tensor("dcol", [128, NXT], F32, kind="ExternalInput")
    dnwcol = nc.dram_tensor("nwcol", [128, NXT], F32, kind="ExternalInput")
    dbbcol = nc.dram_tensor("bbcol", [128, NKD], F32, kind="ExternalInput")
    denct = nc.dram_tensor("enct", [D, 40], F32, kind="ExternalInput")
    dcoef = nc.dram_tensor("coef", [D, 120], F32, kind="ExternalInput")
    dcmask = nc.dram_tensor("cmask", [D, 120], U8, kind="ExternalInput")
    dselm = nc.dram_tensor("selm", [128, 8], F32, kind="ExternalInput")
    di32 = nc.dram_tensor("ident32", [128, 128], F32, kind="ExternalInput")
    dibf = nc.dram_tensor("identbf", [128, 128], BF16, kind="ExternalInput")
    dlt1 = nc.dram_tensor("lt1", [128, 128], F32, kind="ExternalInput")
    don128 = nc.dram_tensor("ones128", [128, 128], F32, kind="ExternalInput")
    dutm = nc.dram_tensor("utm", [128, 128], BF16, kind="ExternalInput")
    don1 = nc.dram_tensor("ones1", [1, 128], F32, kind="ExternalInput")
    dgcol = nc.dram_tensor("gcol", [8, 128, 8], BF16, kind="ExternalInput")
    dout = nc.dram_tensor("out", [2 * TR, D], F32, kind="ExternalOutput")
    dagin = [nc.dram_tensor(f"agin{r}", [128, INTER + 128], F32) for r in range(2)]
    dagout = [nc.dram_tensor(f"agout{r}", [512, INTER + 128], F32) for r in range(2)]
    dst2048 = nc.dram_tensor("st2048", [128, INTER], F32)

    dbg = {}
    if debug:
        for nm, shp, dt_ in [("xn", [2, D, TR], BF16), ("xc", [2, INTER, TR], BF16),
                             ("bc", [2, 2048, TR], BF16), ("dt", [2, 128, TR], F32),
                             ("cs", [2, 128, TR], F32), ("ya", [2, INTER, TR], BF16),
                             ("pay", [2, 128, INTER + 128], F32),
                             ("raw", [2, 128, 544], BF16),
                             ("yb", [2, INTER, TR], BF16), ("ygn", [2, INTER, TR], BF16)]:
            dbg[nm] = nc.dram_tensor("dbg_" + nm, shp, dt_, kind="ExternalOutput")

    def sb(name, shape, dtype):
        return nc.alloc_sbuf_tensor(name, shape, dtype)

    ident32 = sb("c_i32", [128, 128], F32)
    identbf = sb("c_ibf", [128, 128], BF16)
    lt1 = sb("c_lt1", [128, 128], F32)
    on128 = sb("c_on1x", [128, 128], F32)
    utm = sb("c_utm", [128, 128], BF16)
    on1 = sb("c_on1", [1, 128], F32)
    acol = sb("c_acol", [128, 1], F32)
    dtbc = sb("c_dtb", [128, 1], F32)
    dcolb = sb("c_dcol", [128, NXT], F32)
    nwcolb = sb("c_nw", [128, NXT], F32)
    bbcolb = sb("c_bb", [128, NKD], F32)
    selmb = sb("c_selm", [128, 8], F32)
    gcolb = sb("c_gcol", [128, 64], BF16)
    zero128 = sb("c_zero", [128, 128], F32)
    gapb = sb("c_gap", [1, 8], F32)
    convbb = sb("c_convb", [128, CONVD // 128], F32)
    enctb = sb("c_enct", [128, NKD * 40], BF16)
    vtb = sb("c_vt", [40, D], BF16)
    etall = sb("c_et", [40, TR], BF16)
    drowb = sb("c_drow", [1, 128], F32)
    cdrowb = sb("c_cdrow", [1, 128], F32)
    xnhalo = sb("c_xnh", [128, NKD * 3], BF16)
    statc = sb("c_stat", [128, 16], F32)

    xnb = sb("a_xn", [128, NKD * TR], BF16)
    xfm = sb("a_xfm", [128, NXT * TR], BF16)
    bfmb = sb("a_bfm", [128, NG * TR], BF16)
    cfmb = sb("a_cfm", [128, NG * TR], BF16)

    ov = sb("a_ov", [128, 36 * 1024 + 896], U8)
    ovoff = nc.lookup_mloc(ov).addr

    def at(name, shape, dtype, off):
        return nc.alloc_sbuf_tensor_at(name, shape, dtype, offset=ovoff + off)

    inslab = [at(f"w_in{i}", [128, NKD * 128], BF16, i * 8192) for i in range(2)]
    convdsl = [at(f"w_cd{i}", [128, KTAP * 128], BF16, 16384 + i * 1024) for i in range(2)]
    rawsl = [at(f"w_raw{i}", [128, 544], BF16, 18432 + i * 1088) for i in range(2)]
    rbuf = at("r_buf", [128, INTER], F32, 0)
    ainbuf = rbuf
    wotsl = [at(f"w_ot{i}", [128, NKD * 128], BF16, i * 8192) for i in range(2)]
    wbtsl = [at(f"w_bt{i}", [128, NKD * 128], BF16, 16384 + i * 8192) for i in range(2)]
    gatesb = [at(f"w_gt{i}", [128, 512], BF16, 32768 + i * 1024) for i in range(2)]
    ygsqsb = [at(f"w_yq{i}", [128, 512], BF16, 34816 + i * 1024) for i in range(2)]

    sa = sb("a_str", [128, 24 * 1024], U8)
    saoff = nc.lookup_mloc(sa).addr

    def ats(name, shape, dtype, off):
        return nc.alloc_sbuf_tensor_at(name, shape, dtype, offset=saoff + off)

    xhalf = [ats(f"s_xh{i}", [128, 2048], F32, i * 8192) for i in range(2)]
    xntt = ats("s_xntt", [128, D], BF16, 16384)
    coefw = ats("c_coefw", [128, NKD * 40], BF16, 0)
    coefbuf = ats("c_coefb", [128, 120], F32, 2560)
    cmaskbuf = ats("c_cmkb", [128, 120], F32, 3072)
    vacc = ats("c_vacc", [40, D], F32, 4096)
    xtok = [ats(f"s_xtok{i}", [128, 512], F32, i * 2048) for i in range(4)]
    outsb = [ats(f"s_osb{i}", [128, 512], F32, 8192 + i * 2048) for i in range(2)]
    outst = [ats(f"s_ost{i}", [128, 512], F32, 12288 + i * 2048) for i in range(4)]
    sjbuf = [ats(f"s_sj{i}", [128, 2048], F32, i * 8192) for i in range(2)]

    csb = sb("d_cs", [128, TR], F32)
    negcs = sb("d_ncs", [128, TR], F32)
    dtsb = sb("d_dt", [128, TR], F32)
    dasb = sb("d_da", [128, 128], F32)
    datsb = sb("d_dat", [128, 4 * 128], F32)
    cstsb = sb("d_cst", [128, 128], F32)
    ncstsb = sb("d_ncst", [128, 128], F32)
    dttsb = sb("d_dtt", [128, 128], F32)
    dectsb = sb("d_dect", [128, 128], F32)
    expltsb = sb("d_explt", [128, 128], F32)
    expgtsb = sb("d_expgt", [128, 4 * 128], F32)
    dechsb = sb("d_dech", [128, 128], F32)
    qsb = sb("d_q", [128, 128], F32)
    ebsb = sb("d_eb", [128, 40], F32)
    bcastsb = [ats(f"d_bc{i}", [128, 128], F32, i * 512) for i in range(4)]
    lmsb = [ats(f"d_lm{i}", [128, 128], BF16, 2048 + i * 256) for i in range(4)]
    sclsb = [ats(f"d_scl{i}", [128, 128], BF16, 3072 + i * 256) for i in range(4)]
    scmsb = [ats(f"d_scm{i}", [128, 128], BF16, 4096 + i * 256) for i in range(2)]
    btsb = [ats(f"d_bt{i}", [128, 128], BF16, 4608 + i * 256) for i in range(2)]
    bdtsb = [ats(f"d_bdt{i}", [128, 128], BF16, 5120 + i * 256) for i in range(4)]
    xdttsb = [ats(f"d_xdtt{i}", [128, 128], BF16, 6144 + i * 256) for i in range(4)]
    rcpsb = [ats(f"d_rcp{i}", [128, 128], BF16, 7168 + i * 256) for i in range(2)]
    qdsb = [sb(f"d_qd{i}", [128, 128], F32) for i in range(2)]
    yofsb = [ats(f"d_yof{i}", [128, 128], F32, 7680 + i * 512) for i in range(2)]
    segsb = [ats(f"d_seg{i}", [128, 128], F32, 8704 + i * 512) for i in range(4)]
    ycsb = [ats(f"d_yc{i}", [128, 128], BF16, 10752 + i * 256) for i in range(2)]
    rstd8 = ats("c_rstd8", [8, 512], F32, 20480)
    rstdrow = ats("c_rsrow", [1, 512], F32, 22528)

    assert nc.sbuf_bytes_remaining >= 0, f"SBUF over by {-nc.sbuf_bytes_remaining}"

    PA = [nc.alloc_psum_tensor(f"pa{i}", [128, 512], F32) for i in range(2)]
    PT = [nc.alloc_psum_tensor(f"ptp{i}", [128, 1024], BF16) for i in range(2)]
    PC = [nc.alloc_psum_tensor(f"pcc{i}", [128, 512], F32) for i in range(2)]
    PY = [nc.alloc_psum_tensor(f"pyy{i}", [128, 512], F32) for i in range(2)]

    p = Pl(nc)
    sDW = p.sem(st, "sDW")
    sDX = p.sem(st, "sDX")
    sPE = p.sem(st, "sPE")
    sDV = p.sem(st, "sDV")
    sAC = p.sem(st, "sAC")
    sGP = p.sem(st, "sGP")
    E = p.emit

    bankw = {}

    def consume(bank, tok):
        if tok is not None:
            bankw.setdefault(bank, []).append(tok)

    def bank_free(bank):
        return bankw.pop(bank, [])

    def gdma(out, in_, waits=()):
        return E("gpsimd", (lambda o, i: lambda g: g.dma_start(o, i))(out, in_), waits, sDW, 16)

    def sdma(out, in_, waits=()):
        return E("sync", (lambda o, i: lambda s_: s_.dma_start(o, i))(out, in_), waits, sDX, 16)

    def mm(bank, out, lhsT, rhs, start, stop, waits=(), inc=None, isT=False):
        w = list(waits) + (bank_free(bank) if start else [])
        fn = (lambda o, l, r, s0, s1, it: lambda t: t.matmul(
            o, l, r, start=s0, stop=s1, is_transpose=it))(out, lhsT, rhs, start, stop, isT)
        return E("tensor", fn, w, inc)

    def tpose(bank, out, in_, ident, waits=(), inc=sPE, start=True, stop=True):
        return mm(bank, out, in_, ident, start, stop, waits, inc, isT=True)

    def vcopy(out, in_, waits=(), inc=sDV):
        return E("vector", (lambda o, i: lambda v: v.tensor_copy(o, i))(out, in_), waits, inc)

    for dst, src in [(ident32, di32), (identbf, dibf), (lt1, dlt1), (on128, don128),
                     (utm, dutm), (on1, don1), (acol, dacol), (dtbc, ddtb),
                     (dcolb, ddcol), (nwcolb, dnwcol), (bbcolb, dbbcol), (selmb, dselm),
                     (convbb, dconvb)]:
        sdma(dst[:, :], src[:, :])
    for g in range(8):
        sdma(gcolb[:, g * 8:(g + 1) * 8], dgcol[g, :, :])
    for k in range(NKD):
        gdma(enctb[:, k * 40:(k + 1) * 40], denct[k * 128:(k + 1) * 128, :])
    tgap0 = E("gpsimd", lambda g: g.memset(gapb[:, :], 0.0), (), sGP)
    E("gpsimd", lambda g: g.memset(zero128[:, :], 0.0), (), sGP)
    E("gpsimd", lambda g: g.memset(vtb[:, :], 0.0), (), sGP)
    tzero = E("gpsimd", lambda g: g.memset(etall[:, :], 0.0), (), sGP)
    tconst = p.tok(sDX)
    tconstw = p.tok(sDW)

    twe = [None] * NKD
    for k in range(NKD):
        wl = [twe[k - 1]] if k >= 1 else []
        t1 = sdma(coefbuf[:, :], dcoef[k * 128:(k + 1) * 128, :], wl)
        t2 = gdma(cmaskbuf[:, :], dcmask[k * 128:(k + 1) * 128, :], wl)

        def mkwe(k=k):
            def go(v):
                v.tensor_mul(coefbuf[:, 0:40], coefbuf[:, 0:40], cmaskbuf[:, 0:40])
                v.tensor_mul(coefbuf[:, 40:80], coefbuf[:, 40:80], cmaskbuf[:, 40:80])
                v.tensor_mul(coefbuf[:, 80:120], coefbuf[:, 80:120], cmaskbuf[:, 80:120])
                v.tensor_add(coefbuf[:, 0:40], coefbuf[:, 0:40], coefbuf[:, 40:80])
                v.tensor_copy(gapb[0:1, 0:1], gapb[0:1, 1:2])
                return v.tensor_add(coefw[:, k * 40:(k + 1) * 40], coefbuf[:, 0:40],
                                    coefbuf[:, 80:120])
            return go
        twe[k] = E("vector", mkwe(), [t1, t2], sDV)

    tvz = E("gpsimd", lambda g: g.memset(vacc[:, :], 0.0), (), sGP)
    tmt = [None] * NKD
    tvo = [None] * NKD
    for o in range(2):
        tmt[o] = gdma(inslab[o % 2][:, :], dmt[o * 128:(o + 1) * 128, :], [])
    for o in range(NKD):
        for js in range(8):
            bank = "PC%d" % (js % 2)
            tk = mm(bank, PC[js % 2][0:40, 0:512], coefw[:, o * 40:(o + 1) * 40],
                    inslab[o % 2][:, js * 512:(js + 1) * 512], True, True,
                    [tmt[o], twe[o]], sPE)

            def mkadd(js=js, bank=bank):
                def go(v):
                    return v.tensor_add(vacc[0:40, js * 512:(js + 1) * 512],
                                        vacc[0:40, js * 512:(js + 1) * 512],
                                        PC[js % 2][0:40, 0:512])
                return go
            ta = E("vector", mkadd(), [tk, tvz], sDV)
            consume(bank, ta)
        tvo[o] = tk
        if o + 2 < NKD:
            tmt[o + 2] = gdma(inslab[o % 2][:, :], dmt[(o + 2) * 128:(o + 3) * 128, :], [tk])
    tvt = E("vector", lambda v: v.tensor_scalar(vtb[0:40, :], vacc[0:40, :], SCALING, None,
                                                ALU.mult), [p.tok(sDV)], sDV)
    tsetup_pe = p.tok(sPE)
    tsetup_dv = p.tok(sDV)

    def ln_tile(xdram, row0, nrow, dst_fm, colstride, dstcol0, extra=(), prev=None):
        pw = []
        if prev is not None:
            pw = [prev[0]] + list(prev[1])
        th = [sdma(xhalf[h][0:nrow, :], xdram[row0:row0 + nrow, h * 2048:(h + 1) * 2048],
                   list(extra) + pw) for h in range(2)]

        def mkst(h):
            def go(v):
                return v.tensor_reduce(statc[0:nrow, h:h + 1], xhalf[h][0:nrow, :], AXL.X, ALU.add)
            return go

        def mksq(h):
            def go(sc):
                return sc.activation(xntt[0:nrow, h * 2048:(h + 1) * 2048], xhalf[h][0:nrow, :],
                                     ACT.Square, accum_out=statc[0:nrow, 2 + h:3 + h])
            return go
        ts0 = E("vector", mkst(0), [th[0]], sDV)
        ts1 = E("vector", mkst(1), [th[1]], sDV)
        tq0 = E("scalar", mksq(0), [th[0]], sAC)
        tq1 = E("scalar", mksq(1), [th[1]], sAC)

        def mkmv(v):
            v.tensor_add(statc[0:nrow, 4:5], statc[0:nrow, 0:1], statc[0:nrow, 1:2])
            v.tensor_add(statc[0:nrow, 5:6], statc[0:nrow, 2:3], statc[0:nrow, 3:4])
            v.tensor_scalar(statc[0:nrow, 4:5], statc[0:nrow, 4:5], 1.0 / D, None, ALU.mult)
            v.tensor_scalar(statc[0:nrow, 5:6], statc[0:nrow, 5:6], 1.0 / D, None, ALU.mult)
            v.tensor_mul(statc[0:nrow, 6:7], statc[0:nrow, 4:5], statc[0:nrow, 4:5])
            v.tensor_copy(gapb[0:1, 0:1], gapb[0:1, 1:2])
            v.tensor_sub(statc[0:nrow, 5:6], statc[0:nrow, 5:6], statc[0:nrow, 6:7])
            v.tensor_copy(gapb[0:1, 0:1], gapb[0:1, 1:2])
            return v.tensor_scalar(statc[0:nrow, 5:6], statc[0:nrow, 5:6], EPS, None, ALU.add)
        tmv = E("vector", mkmv, [ts0, ts1, tq0, tq1], sDV)
        tsq = E("scalar", lambda sc: sc.activation(statc[0:nrow, 7:8], statc[0:nrow, 5:6],
                                                   ACT.Sqrt), [tmv], sAC)

        def mknewt(v):
            v.reciprocal(statc[0:nrow, 8:9], statc[0:nrow, 7:8])
            v.tensor_copy(gapb[0:1, 0:1], gapb[0:1, 1:2])
            v.tensor_mul(statc[0:nrow, 9:10], statc[0:nrow, 8:9], statc[0:nrow, 8:9])
            v.tensor_copy(gapb[0:1, 0:1], gapb[0:1, 1:2])
            v.tensor_mul(statc[0:nrow, 9:10], statc[0:nrow, 9:10], statc[0:nrow, 5:6])
            v.tensor_copy(gapb[0:1, 0:1], gapb[0:1, 1:2])
            v.tensor_scalar(statc[0:nrow, 9:10], statc[0:nrow, 9:10], -0.5, 1.5, ALU.mult, ALU.add)
            v.tensor_copy(gapb[0:1, 0:1], gapb[0:1, 1:2])
            return v.tensor_mul(statc[0:nrow, 8:9], statc[0:nrow, 8:9], statc[0:nrow, 9:10])
        tnw = E("vector", mknewt, [tsq], sDV)
        tn = []
        for h in range(2):
            def mknm(h=h):
                def go(v):
                    return v.tensor_scalar(xntt[0:nrow, h * 2048:(h + 1) * 2048],
                                           xhalf[h][0:nrow, :], statc[0:nrow, 4:5],
                                           statc[0:nrow, 8:9], ALU.subtract, ALU.mult)
                return go
            tn.append(E("vector", mknm(h), [tnw], sDV))
        tlast = None
        tt = None
        for k in range(NKD):
            bank = "PT%d" % (k % 2)
            slot = PT[k % 2][:, 128 * ((k // 2) % 2):128 * ((k // 2) % 2) + nrow]
            tt = tpose(bank, slot, xntt[0:nrow, k * 128:(k + 1) * 128],
                       identbf[0:nrow, 0:nrow], tn)
            te = vcopy(dst_fm[:, k * colstride + dstcol0:k * colstride + dstcol0 + nrow],
                       slot, [tt])
            consume(bank, te)
            tlast = te
        return (tt, [tn[0], tn[1], tq0, tq1], tlast)

    def round_r(r, prev_end=None):
        xdram = dxr[r]
        lnprev = None
        for tt_ in range(4):
            ex = [tconst, tsetup_pe, tsetup_dv] if (r == 0 and tt_ == 0) else []
            if prev_end is not None and tt_ == 0:
                ex = ex + list(prev_end)
            lnprev = ln_tile(xdram, 3 + tt_ * 128, 128, xnb, TR, tt_ * 128,
                             ex, prev=lnprev)
        ln_tile(xdram, 0, 3, xnhalo, 3, 0, prev=lnprev)
        txn = p.tok(sDV)

        if debug:
            for k in range(NKD):
                sdma(dbg["xn"][r, k * 128:(k + 1) * 128, :], xnb[:, k * TR:(k + 1) * TR], [txn])
        if stage < 2:
            return

        def ct_info(ci):
            if ci == 0:
                return ("dt", PROJ // 128 - 1, None, None)
            if ci <= 16:
                gg = ci - 1
                half, g = gg // 8, gg % 8
                cti = (2 * INTER + half * NG * 128 + g * 128) // 128
                return ("bc", cti, cfmb if half else bfmb, g)
            k = ci - 17
            return ("x", (INTER + k * 128) // 128, xfm, k)

        tslab = [None] * NCT
        tcdsl = [None] * NCT
        tmmend = [None] * NCT
        tconvend = [None] * NCT
        tevac = [None] * NCT

        def slabdma(ci, wl, wl2):
            kind, cti, dst, idx = ct_info(ci)
            tslab[ci] = gdma(inslab[ci % 2][:, :], dwti[cti, :, :], wl)
            if kind != "dt":
                tcdsl[ci] = gdma(convdsl[ci % 2][:, :], dconvd[cti - INTER // 128, :, :], wl2)
        pe_guard = [tsetup_pe] + (list(prev_end) if prev_end is not None else [])
        slabdma(0, pe_guard, [])
        slabdma(1, pe_guard, [])
        for ci in range(NCT):
            kind, cti, dst, idx = ct_info(ci)
            bank, ps = "PA%d" % (ci % 2), PA[ci % 2]
            hbank = "PY%d" % (ci % 2)
            tk = None
            for k in range(NKD):
                tk = mm(bank, ps[:, :], inslab[ci % 2][:, k * 128:(k + 1) * 128],
                        xnb[:, k * TR:(k + 1) * TR], k == 0, k == NKD - 1,
                        [tslab[ci], txn], sPE if k == NKD - 1 else None)
                if kind != "dt":
                    hinc = sPE if k == NKD - 1 else None
                    thl = mm(hbank, PY[ci % 2][:, 0:3],
                             inslab[ci % 2][:, k * 128:(k + 1) * 128],
                             xnhalo[:, k * 3:(k + 1) * 3], k == 0, k == NKD - 1, [], hinc)
            tmmend[ci] = tk
            if kind == "dt":
                def mkz(ci=ci):
                    def go(v):
                        return v.tensor_scalar(dtsb[:, :], PA[ci % 2][:, :], dtbc[:, 0:1],
                                               None, ALU.add)
                    return go
                tz = E("vector", mkz(), [tmmend[ci]], sDV)
                consume(bank, tz)

                def mkabs(sc):
                    return sc.activation(negcs[:, :], dtsb[:, :], ACT.Abs)

                def mkexp(sc):
                    return sc.activation(negcs[:, :], negcs[:, :], ACT.Exp, scale=-1.0)

                def mkln(sc):
                    return sc.activation(negcs[:, :], negcs[:, :], ACT.Ln, bias=1.0)
                ta1 = E("scalar", mkabs, [tz], sAC)
                ta2 = E("scalar", mkexp, [ta1], sAC)
                ta3 = E("scalar", mkln, [ta2], sAC)

                def mkfin(v):
                    v.tensor_scalar(dtsb[:, :], dtsb[:, :], 0.0, None, ALU.max)
                    v.tensor_copy(gapb[0:1, 0:1], gapb[0:1, 1:2])
                    v.tensor_add(dtsb[:, :], dtsb[:, :], negcs[:, :])
                    v.tensor_copy(gapb[0:1, 0:1], gapb[0:1, 1:2])
                    return v.tensor_scalar(dtsb[:, :], dtsb[:, :], DT_MIN, DT_MAX,
                                           ALU.max, ALU.min)
                tevac[ci] = E("vector", mkfin, [ta3], sDV)
            else:
                wr = [tmmend[ci]] + ([tconvend[ci - 2]] if ci >= 2 else [])

                def mkraw(ci=ci):
                    def go(v):
                        return v.tensor_copy(rawsl[ci % 2][:, 3:3 + 512], PA[ci % 2][:, :])
                    return go
                tr0 = E("vector", mkraw(), wr, sDV)
                consume(bank, tr0)

                def mkrh(ci=ci):
                    def go(v):
                        return v.tensor_copy(rawsl[ci % 2][:, 0:3], PY[ci % 2][:, 0:3])
                    return go
                tr1 = E("vector", mkrh(), [thl], sDV)
                consume(hbank, tr1)
                cbank = "PC%d" % (ci % 2)
                tc = None
                for j in range(KTAP):
                    tc = mm(cbank, PC[ci % 2][:, :], convdsl[ci % 2][:, j * 128:(j + 1) * 128],
                            rawsl[ci % 2][:, j:j + 512], j == 0, j == KTAP - 1,
                            [tr0, tr1, tcdsl[ci]], sPE if j == KTAP - 1 else None)
                tconvend[ci] = tc
                if debug and ci in (17, 79):
                    sdma(dbg["raw"][0 if ci == 17 else 1, :, 0:515], rawsl[ci % 2][:, 0:515],
                         [tr0, tr1])

                def mksilu(ci=ci, dst=dst, idx=idx, cti=cti):
                    def go(sc):
                        return sc.activation(dst[:, idx * TR:(idx + 1) * TR], PC[ci % 2][:, :],
                                             ACT.Silu, bias=convbb[:, cti - 64:cti - 63])
                    return go
                tevac[ci] = E("scalar", mksilu(), [tc], sAC)
                consume(cbank, tevac[ci])
            if ci + 2 < NCT:
                slabdma(ci + 2, [tmmend[ci]], [tconvend[ci]] if tconvend[ci] else [])
        tph1ac = p.tok(sAC)
        tdtok = tevac[0]

        if debug:
            for k in range(NXT):
                sdma(dbg["xc"][r, k * 128:(k + 1) * 128, :], xfm[:, k * TR:(k + 1) * TR],
                     [tevac[17 + k]])
            for g in range(8):
                sdma(dbg["bc"][r, g * 128:(g + 1) * 128, :], bfmb[:, g * TR:(g + 1) * TR],
                     [tevac[1 + g]])
                sdma(dbg["bc"][r, 1024 + g * 128:1024 + (g + 1) * 128, :],
                     cfmb[:, g * TR:(g + 1) * TR], [tevac[9 + g]])
            sdma(dbg["dt"][r, :, :], dtsb[:, :], [tdtok])
        if stage < 3:
            return

        twl = [p.tok(sDW), tmmend[NCT - 1], tconvend[NCT - 1]]
        tRinit = E("gpsimd", lambda g: g.memset(rbuf[:, :], 0.0), twl, sGP)
        ptok = {}

        for c in range(NCHUNK):
            c0 = c * CHL

            def mkda(c=c):
                def go(v):
                    return v.tensor_scalar(dasb[:, :], dtsb[:, c * CHL:(c + 1) * CHL],
                                           acol[:, 0:1], None, ALU.mult)
                return go
            tda = E("vector", mkda(), [tdtok], sDV)
            ttda = tpose("PC0", PC[0][:, 0:128], dasb[:, :], ident32[:, :], [tda])
            tdat = vcopy(datsb[:, c * 128:(c + 1) * 128], PC[0][:, 0:128], [ttda])
            consume("PC0", tdat)
            tcst = None
            for cc in range(c + 1):
                tcst = mm("PC1", PC[1][:, 0:128], (lt1 if cc == c else on128)[:, :],
                          datsb[:, cc * 128:(cc + 1) * 128], cc == 0, cc == c,
                          [tdat], sPE if cc == c else None)

            def mkcst(v):
                v.tensor_copy(cstsb[:, :], PC[1][:, 0:128])
                return v.tensor_scalar(ncstsb[:, :], PC[1][:, 0:128], -1.0, None, ALU.mult)
            tcste = E("vector", mkcst, [tcst], sDV)
            consume("PC1", tcste)
            tcs1 = tpose("PC0", PC[0][:, 0:128], cstsb[:, :], ident32[:, :], [tcste])

            def mkcs(c=c):
                def go(v):
                    v.tensor_copy(csb[:, c * CHL:(c + 1) * CHL], PC[0][:, 0:128])
                    return v.tensor_scalar(negcs[:, c * CHL:(c + 1) * CHL], PC[0][:, 0:128],
                                           -1.0, None, ALU.mult)
                return go
            tcse = E("vector", mkcs(), [tcs1], sDV)
            consume("PC0", tcse)

            tdt1 = tpose("PC1", PC[1][:, 0:128], dtsb[:, c * CHL:(c + 1) * CHL], ident32[:, :],
                         [tdtok, tcse])
            tdtt = vcopy(dttsb[:, :], PC[1][:, 0:128], [tdt1])
            consume("PC1", tdtt)

            def mkdec(c=c):
                def go(sc):
                    return sc.activation(dechsb[:, :], csb[:, c * CHL:(c + 1) * CHL], ACT.Exp,
                                         bias=csb[:, (c + 1) * CHL - 1:(c + 1) * CHL], scale=-1.0)
                return go
            tdec = E("scalar", mkdec(), [tcse], sAC)
            tdec1 = tpose("PC0", PC[0][:, 0:128], dechsb[:, :], ident32[:, :], [tdec])
            tdect = vcopy(dectsb[:, :], PC[0][:, 0:128], [tdec1])
            consume("PC0", tdect)

            def mkexpl(c=c):
                def go(sc):
                    if c == 0:
                        return sc.activation(dechsb[:, :], csb[:, 0:CHL], ACT.Exp)
                    return sc.activation(dechsb[:, :], csb[:, c * CHL:(c + 1) * CHL], ACT.Exp,
                                         bias=negcs[:, c * CHL - 1:c * CHL])
                return go
            texl = E("scalar", mkexpl(), [tcse, tdec1], sAC)
            texl1 = tpose("PC1", PC[1][:, 0:128], dechsb[:, :], ident32[:, :], [texl])
            texlt = vcopy(expltsb[:, :], PC[1][:, 0:128], [texl1])
            consume("PC1", texlt)

            def mkexpg(c=c):
                def go(sc):
                    return sc.activation(dechsb[:, :], csb[:, c * CHL:(c + 1) * CHL], ACT.Exp)
                return go
            texg = E("scalar", mkexpg(), [texl1], sAC)
            texg1 = tpose("PC0", PC[0][:, 0:128], dechsb[:, :], ident32[:, :], [texg])
            texgt = vcopy(expgtsb[:, c * 128:(c + 1) * 128], PC[0][:, 0:128], [texg1])
            consume("PC0", texgt)

            def mkcd(c=c):
                def go(sc):
                    if c == 0:
                        return sc.activation(statc[:, 10:11], csb[:, CHL - 1:CHL], ACT.Exp)
                    return sc.activation(statc[:, 10:11], csb[:, (c + 1) * CHL - 1:(c + 1) * CHL],
                                         ACT.Exp, bias=negcs[:, c * CHL - 1:c * CHL])
                return go
            tcd = E("scalar", mkcd(), [tcse], sAC)
            tcd1 = tpose("PC1", PC[1][0:1, 0:128], statc[:, 10:11], ident32[:, :], [tcd])
            tcdr = vcopy(cdrowb[0:1, :], PC[1][0:1, 0:128], [tcd1])
            consume("PC1", tcdr)
            tq1 = mm("PC0", PC[0][:, 0:128], on1[0:1, :], cdrowb[0:1, :], True, True, [tcdr], sPE)
            tq = vcopy(qsb[:, :], PC[0][:, 0:128], [tq1])
            consume("PC0", tq)

            for pr in range(NXT):
                pi = c * NXT + pr
                g = pr // 8
                par = pi % 2
                tks = {}
                ptok[pi] = tks
                prev2 = ptok.get(pi - 2, {})
                prev4 = ptok.get(pi - 4, {})

                if pr % 8 == 0:
                    gb = "PC%d" % (g % 2)
                    tsc1 = mm(gb, PC[g % 2][:, 128:256], bfmb[:, g * TR + c0:g * TR + c0 + CHL],
                              cfmb[:, g * TR + c0:g * TR + c0 + CHL], True, True,
                              [tevac[1 + g], tevac[9 + g]], sPE)

                    def mkscm(g=g):
                        def go(v):
                            return v.tensor_tensor(scmsb[g % 2][:, :], PC[g % 2][:, 128:256],
                                                   utm[:, :], ALU.mult)
                        return go
                    tscm_g = E("vector", mkscm(), [tsc1], sDV)
                    consume(gb, tscm_g)
                    tb1 = tpose("PT%d" % (g % 2), PT[g % 2][:, 512:640],
                                bfmb[:, g * TR + c0:g * TR + c0 + CHL], identbf[:, :],
                                [tevac[1 + g]])
                    tbt_g = vcopy(btsb[g % 2][:, :], PT[g % 2][:, 512:640], [tb1])
                    consume("PT%d" % (g % 2), tbt_g)
                    if g == 0 or True:
                        pass
                    grp_tok = (tscm_g, tbt_g)
                tscm_g, tbt_g = grp_tok

                xbank = "PT%d" % (pi % 2)
                xslot = PT[pi % 2][:, 128 * ((pi // 2) % 2):128 * ((pi // 2) % 2) + 128]
                ttx = tpose(xbank, xslot, xfm[:, pr * TR + c0:pr * TR + c0 + CHL], identbf[:, :],
                            [tevac[17 + pr]])

                def mkxdt(pi=pi, xslot=xslot, pr=pr):
                    def go(v):
                        v.tensor_scalar(xdttsb[pi % 4][:, 0:64], xslot[:, 0:64],
                                        dttsb[:, 2 * pr:2 * pr + 1], None, ALU.mult)
                        return v.tensor_scalar(xdttsb[pi % 4][:, 64:128], xslot[:, 64:128],
                                               dttsb[:, 2 * pr + 1:2 * pr + 2], None, ALU.mult)
                    return go
                txdt = E("vector", mkxdt(), [ttx, tdtt, prev4.get("states")], sDV)
                consume(xbank, txdt)

                tscl = [None, None]
                for hi in range(2):
                    h = 2 * pr + hi
                    m = (2 * pi + hi) % 4
                    rb = "PC%d" % ((2 * pi + hi) % 2)
                    rbp = PC[(2 * pi + hi) % 2]

                    def mkbc(h=h, m=m):
                        def go(v):
                            return v.tensor_scalar(bcastsb[m][:, :], zero128[:, :],
                                                   cstsb[:, h:h + 1], None, ALU.add)
                        return go
                    tbc = E("vector", mkbc(), [tcste, prev2.get("rep%d" % hi)], sDV)
                    trep = tpose(rb, rbp[:, 256:384], bcastsb[m][:, :], ident32[:, :], [tbc])
                    tks["rep%d" % hi] = trep

                    def mkseg(h=h, m=m, rbp=rbp):
                        def go(v):
                            return v.tensor_scalar(segsb[m][:, :], rbp[:, 256:384],
                                                   ncstsb[:, h:h + 1], 0.0, ALU.add, ALU.min)
                        return go
                    tseg = E("vector", mkseg(), [trep, prev2.get("lm%d" % hi)], sDV)
                    consume(rb, tseg)

                    def mklm(m=m):
                        def go(sc):
                            return sc.activation(lmsb[m][:, :], segsb[m][:, :], ACT.Exp)
                        return go
                    tlm = E("scalar", mklm(), [tseg, prev2.get("scl%d" % hi)], sAC)
                    tks["lm%d" % hi] = tlm

                    def mkscl(m=m, g=g):
                        def go(v):
                            return v.tensor_mul(sclsb[m][:, :], lmsb[m][:, :], scmsb[g % 2][:, :])
                        return go
                    tscl[hi] = E("vector", mkscl(), [tlm, tscm_g, prev2.get("ydiagend")], sDV)
                    tks["scl%d" % hi] = tscl[hi]

                ybank = "PY%d" % (pi % 2)
                yps = PY[pi % 2]
                m0, m1 = (2 * pi) % 4, (2 * pi + 1) % 4
                ty1 = mm(ybank, yps[:, 0:64], sclsb[m0][:, :], xdttsb[pi % 4][:, 0:64],
                         True, False, [txdt, tscl[0], prev2.get("ycomb")])
                ty2 = mm(ybank, yps[:, 64:128], sclsb[m1][:, :], xdttsb[pi % 4][:, 64:128],
                         False, True, [tscl[1]], sPE)
                tks["ydiagend"] = ty2

                tbd = [None, None]
                for hi in range(2):
                    h = 2 * pr + hi

                    def mkbdt(h=h, m=(2 * pi + hi) % 4, g=g):
                        def go(v):
                            return v.tensor_scalar(bdtsb[m][:, :], btsb[g % 2][:, :],
                                                   dectsb[:, h:h + 1], None, ALU.mult)
                        return go
                    tbd[hi] = E("vector", mkbdt(), [tbt_g, tdect, prev2.get("states")], sDV)
                sbank = "PA%d" % par
                sps = PA[par]
                tst1 = mm(sbank, sps[:, 0:64], bdtsb[m0][:, :], xdttsb[pi % 4][:, 0:64],
                          True, False, [tbd[0], tbd[1], txdt, tscl[1]])
                tst2 = mm(sbank, sps[:, 64:128], bdtsb[m1][:, :], xdttsb[pi % 4][:, 64:128],
                          False, True, [], sPE)
                tks["states"] = tst2

                if c > 0:
                    trcp = E("vector", (lambda pr=pr, pi=pi: lambda v: v.tensor_copy(
                        rcpsb[pi % 2][:, :], rbuf[:, pr * 128:(pr + 1) * 128]))(),
                        [prev2.get("yoffmm"), ptok.get(pi - 64, {}).get("walk")], sDV)
                    tyo1 = mm(sbank, sps[:, 128:256], cfmb[:, g * TR + c0:g * TR + c0 + CHL],
                              rcpsb[pi % 2][:, :], True, True, [trcp, tevac[9 + g]], sPE)
                    tks["yoffmm"] = tyo1

                    def mkyoe(pi=pi, pr=pr, sps=sps):
                        def go(v):
                            v.tensor_scalar(yofsb[pi % 2][:, 0:64], sps[:, 128:192],
                                            expltsb[:, 2 * pr:2 * pr + 1], None, ALU.mult)
                            return v.tensor_scalar(yofsb[pi % 2][:, 64:128], sps[:, 192:256],
                                                   expltsb[:, 2 * pr + 1:2 * pr + 2], None,
                                                   ALU.mult)
                        return go
                    tyoe = E("vector", mkyoe(), [tyo1, texlt, prev2.get("ycomb")], sDV)
                    consume(sbank, tyoe)

                def mkwalk(pr=pr, par=par, sps=sps):
                    def go(v):
                        v.scalar_tensor_tensor(rbuf[:, pr * 128:pr * 128 + 64],
                                               rbuf[:, pr * 128:pr * 128 + 64],
                                               qsb[:, 2 * pr:2 * pr + 1], sps[:, 0:64],
                                               ALU.mult, ALU.add)
                        return v.scalar_tensor_tensor(rbuf[:, pr * 128 + 64:(pr + 1) * 128],
                                                      rbuf[:, pr * 128 + 64:(pr + 1) * 128],
                                                      qsb[:, 2 * pr + 1:2 * pr + 2],
                                                      sps[:, 64:128], ALU.mult, ALU.add)
                    return go
                wk = [tst2, tq, tRinit]
                if c > 0:
                    wk.append(tks["yoffmm"])
                twalk = E("vector", mkwalk(), wk, sDV)
                consume(sbank, twalk)
                tks["walk"] = twalk

                if c > 0:
                    def mkcomb(pi=pi, yps=yps):
                        def go(v):
                            return v.tensor_add(ycsb[pi % 2][:, :], yps[:, 0:128],
                                                yofsb[pi % 2][:, :])
                        return go
                    tcomb = E("vector", mkcomb(), [ty2, tyoe, prev2.get("ytr")], sDV)
                else:
                    def mkcomb(pi=pi, yps=yps):
                        def go(v):
                            return v.tensor_copy(ycsb[pi % 2][:, :], yps[:, 0:128])
                        return go
                    tcomb = E("vector", mkcomb(), [ty2, prev2.get("ytr")], sDV)
                consume(ybank, tcomb)
                tks["ycomb"] = tcomb
                ytslot = PT[pi % 2][:, 640:768]
                tytr = tpose("PT%d" % (pi % 2), ytslot, ycsb[pi % 2][:, :], identbf[:, :],
                             [tcomb])
                tks["ytr"] = tytr

                def mkyev(pr=pr, pi=pi, c0=c0, ytslot=ytslot):
                    def go(v):
                        return v.scalar_tensor_tensor(
                            xfm[:, pr * TR + c0:pr * TR + c0 + CHL],
                            xfm[:, pr * TR + c0:pr * TR + c0 + CHL],
                            dcolb[:, pr:pr + 1], ytslot, ALU.mult, ALU.add)
                    return go
                tyev = E("vector", mkyev(), [tytr, txdt], sDV)
                consume("PT%d" % (pi % 2), tyev)
                tks["yevac"] = tyev

        tlastwalk = ptok[NCHUNK * NXT - 1]["walk"]
        tpay = sdma(dagin[r][:, 0:INTER], rbuf[:, :], [tlastwalk])

        def mkdtot(sc):
            return sc.activation(statc[:, 11:12], csb[:, TR - 1:TR], ACT.Exp)
        tdt_ = E("scalar", mkdtot, [p.tok(sDV)], sAC)
        tdr1 = tpose("PC1", PC[1][0:1, 0:128], statc[:, 11:12], ident32[:, :], [tdt_])
        tdrow = vcopy(drowb[0:1, :], PC[1][0:1, 0:128], [tdr1])
        consume("PC1", tdrow)
        tpz = sdma(dagin[r][:, INTER:INTER + 128], zero128[:, :], [tconst])
        tpay2 = sdma(dagin[r][0:1, INTER:INTER + 128], drowb[0:1, :], [tdrow, tpz])

        tdump = []
        if debug and stage == 3:
            for k in range(NXT):
                sdma(dbg["ya"][r, k * 128:(k + 1) * 128, :], xfm[:, k * TR:(k + 1) * TR],
                     [ptok[(NCHUNK - 1) * NXT + k]["yevac"]])
            tdump.append(sdma(dbg["pay"][r, :, 0:INTER], rbuf[:, :], [tlastwalk]))
            sdma(dbg["pay"][r, 0:1, INTER:INTER + 128], drowb[0:1, :], [tdrow])
            sdma(dbg["cs"][r, :, :], csb[:, :], [p.tok(sDV)])
        if stage < 4:
            return None

        tag = E("gpsimd", (lambda r=r: lambda g: g.collective_compute(
            "AllGather", ALU.bypass, replica_groups=[[0, 1, 2, 3], [4, 5, 6, 7]],
            ins=[dagin[r].ap().opt()], outs=[dagout[r].ap().opt()]))(),
            [tpay, tpay2], sGP)

        for ts_ in range(4):
            te_mm = None
            for k in range(NKD):
                te_mm = mm("PC%d" % (ts_ % 2), PC[ts_ % 2][:, 0:40],
                           xnb[:, k * TR + ts_ * 128:k * TR + (ts_ + 1) * 128],
                           enctb[:, k * 40:(k + 1) * 40], k == 0, k == NKD - 1,
                           [txn], sPE if k == NKD - 1 else None)
            tecp = vcopy(ebsb[:, :], PC[ts_ % 2][:, 0:40], [te_mm])
            consume("PC%d" % (ts_ % 2), tecp)
            tet = tpose("PC%d" % ((ts_ + 1) % 2), PC[(ts_ + 1) % 2][0:40, 0:128],
                        ebsb[:, :], ident32[:, :], [tecp])
            teev = vcopy(etall[0:40, ts_ * 128:(ts_ + 1) * 128],
                         PC[(ts_ + 1) % 2][0:40, 0:128], [tet])
            consume("PC%d" % ((ts_ + 1) % 2), teev)
        tet_done = p.tok(sDV)

        def combine_pass2(masked, seed_from_dram, out_extra_wait):
            jrange = list(range(3)) if masked else list(range(4))
            tqd = [None] * 4
            if seed_from_dram:
                tseed = sdma(ainbuf[:, :], dst2048[:, :], list(out_extra_wait) + [tag])
            else:
                tseed = E("gpsimd", (lambda: lambda g: g.memset(ainbuf[:, :], 0.0))(),
                          list(out_extra_wait) + [tag], sGP)
            for j in jrange:
                tdr = sdma(drowb[0:1, :], dagout[r][128 * j:128 * j + 1, INTER:INTER + 128],
                           [tag] + ([tqd[j - 2]] if j >= 2 else []))
                tqm = mm("PC%d" % (j % 2), PC[j % 2][:, 0:128], on1[0:1, :], drowb[0:1, :],
                         True, True, [tdr], sPE)

                def mkqd(j=j, masked=masked):
                    def go(v):
                        if masked:
                            return v.tensor_scalar(qdsb[j % 2][:, :], PC[j % 2][:, 0:128],
                                                   selmb[:, j:j + 1], selmb[:, 4 + j:5 + j],
                                                   ALU.mult, ALU.add)
                        return v.tensor_copy(qdsb[j % 2][:, :], PC[j % 2][:, 0:128])
                    return go
                tqd[j] = E("vector", mkqd(), [tqm], sDV)
                consume("PC%d" % (j % 2), tqd[j])
            tlast = None
            tprevq = {}
            for quarter in range(4):
                for ji, j in enumerate(jrange):
                    qi = (quarter * len(jrange) + ji) % 2
                    tsj_new = sdma(sjbuf[qi][:, :],
                                   dagout[r][128 * j:128 * (j + 1),
                                             quarter * 2048:(quarter + 1) * 2048],
                                   [tag] + ([tprevq[qi]] if qi in tprevq else []))
                    tscl_s = None
                    if masked:
                        def mksm(j=j, qi=qi):
                            def go(v):
                                return v.tensor_scalar(sjbuf[qi][:, :], sjbuf[qi][:, :],
                                                       selmb[:, j:j + 1], None, ALU.mult)
                            return go
                        tscl_s = E("vector", mksm(), [tsj_new], sDV)
                    for hb in range(32):
                        h = quarter * 32 + hb

                        def mkc(j=j, qi=qi, h=h, hb=hb):
                            def go(v):
                                return v.scalar_tensor_tensor(
                                    ainbuf[:, h * 64:(h + 1) * 64],
                                    ainbuf[:, h * 64:(h + 1) * 64],
                                    qdsb[j % 2][:, h:h + 1],
                                    sjbuf[qi][:, hb * 64:(hb + 1) * 64],
                                    ALU.mult, ALU.add)
                            return go
                        tlast = E("vector", mkc(), [tsj_new, tscl_s, tqd[j], tseed], sDV)
                    tprevq[qi] = tlast
            return tlast

        textra = list(tdump)
        if r == 0:
            tfull = combine_pass2(False, False, textra + [tpay])
            tst_dma = sdma(dst2048[:, :], ainbuf[:, :], [tfull])
            tain = combine_pass2(True, False, [tst_dma])
        else:
            tain = combine_pass2(True, True, textra + [tpay])

        btok = {}
        for c in range(NCHUNK):
            c0 = c * CHL
            for pr in range(NXT):
                pib = c * NXT + pr
                g = pr // 8
                par = pib % 2
                prevb = btok.get(pib - 2, {})
                tks2 = {}
                btok[pib] = tks2

                def mkacp(pr=pr, pib=pib):
                    def go(v):
                        return v.tensor_copy(rcpsb[pib % 2][:, :],
                                             rbuf[:, pr * 128:(pr + 1) * 128])
                    return go
                tacp = E("vector", mkacp(), [tain, prevb.get("bmm")], sDV)
                tbmm = mm("PA%d" % par, PA[par][:, 128:256],
                          cfmb[:, g * TR + c0:g * TR + c0 + CHL],
                          rcpsb[pib % 2][:, :], True, True,
                          [tacp, tevac[9 + g]], sPE)
                tks2["bmm"] = tbmm

                def mkbsc(pib=pib, pr=pr, par=par, c=c):
                    def go(v):
                        v.tensor_scalar(yofsb[pib % 2][:, 0:64], PA[par][:, 128:192],
                                        expgtsb[:, c * 128 + 2 * pr:c * 128 + 2 * pr + 1],
                                        None, ALU.mult)
                        return v.tensor_scalar(yofsb[pib % 2][:, 64:128], PA[par][:, 192:256],
                                               expgtsb[:, c * 128 + 2 * pr + 1:c * 128 + 2 * pr + 2],
                                               None, ALU.mult)
                    return go
                tbsc = E("vector", mkbsc(), [tbmm, prevb.get("btr")], sDV)
                consume("PA%d" % par, tbsc)
                tbtr = tpose("PY%d" % par, PY[par][:, 0:128], yofsb[pib % 2][:, :],
                             ident32[:, :], [tbsc])
                tks2["btr"] = tbtr

                def mkbadd(pr=pr, c0=c0, par=par):
                    def go(v):
                        return v.tensor_add(xfm[:, pr * TR + c0:pr * TR + c0 + CHL],
                                            xfm[:, pr * TR + c0:pr * TR + c0 + CHL],
                                            PY[par][:, 0:128])
                    return go
                tbadd = E("vector", mkbadd(),
                          [tbtr, ptok[pib]["yevac"]], sDV)
                consume("PY%d" % par, tbadd)
                tks2["badd"] = tbadd
        tphB = p.tok(sDV)

        if debug:
            for k in range(NXT):
                sdma(dbg["yb"][r, k * 128:(k + 1) * 128, :], xfm[:, k * TR:(k + 1) * TR],
                     [btok[(NCHUNK - 1) * NXT + k]["badd"]])
        if stage < 5:
            return None

        tgsl = [None] * NXT
        tgmm = [None] * NXT
        tgate = [None] * NXT
        tsq = [None] * NXT
        tnmm = [None] * NXT

        def gslab(kk, wl):
            tgsl[kk] = gdma(inslab[kk % 2][:, :], dwti[kk, :, :], wl)
        gslab(0, [tphB])
        gslab(1, [tphB])
        for kk in range(NXT):
            bank, ps = "PA%d" % (kk % 2), PA[kk % 2]
            tk = None
            for k in range(NKD):
                tk = mm(bank, ps[:, :], inslab[kk % 2][:, k * 128:(k + 1) * 128],
                        xnb[:, k * TR:(k + 1) * TR], k == 0, k == NKD - 1,
                        [tgsl[kk]], sPE if k == NKD - 1 else None)
            tgmm[kk] = tk

            def mkgs(kk=kk):
                def go(sc):
                    return sc.activation(gatesb[kk % 2][:, :], PA[kk % 2][:, :], ACT.Silu)
                return go
            wg = [tk] + ([tgate[kk - 2]] if kk >= 2 else [])
            tsilu = E("scalar", mkgs(), wg, sAC)
            consume(bank, tsilu)

            def mkgm(kk=kk):
                def go(v):
                    return v.tensor_mul(xfm[:, kk * TR:(kk + 1) * TR],
                                        xfm[:, kk * TR:(kk + 1) * TR],
                                        gatesb[kk % 2][:, :])
                return go
            tgate[kk] = E("vector", mkgm(), [tsilu, btok[3 * NXT + kk]["badd"]], sDV)

            def mksqg(kk=kk):
                def go(sc):
                    return sc.activation(ygsqsb[kk % 2][:, :], xfm[:, kk * TR:(kk + 1) * TR],
                                         ACT.Square)
                return go
            ws = [tgate[kk]] + ([tnmm[kk - 2]] if kk >= 2 else [])
            tsq[kk] = E("scalar", mksqg(), ws, sAC)
            tnmm[kk] = mm("PY0", PY[0][0:8, 0:512], gcolb[:, (kk // 8) * 8:(kk // 8) * 8 + 8],
                          ygsqsb[kk % 2][:, :], kk == 0, kk == NXT - 1, [tsq[kk]],
                          sPE if kk == NXT - 1 else None)
            if kk + 2 < NXT:
                gslab(kk + 2, [tgmm[kk]])

        def mkms(v):
            return v.tensor_scalar(rstd8[:, :], PY[0][0:8, 0:512], 1.0 / 1024, EPS,
                                   ALU.mult, ALU.add)
        tms = E("vector", mkms, [tnmm[NXT - 1]], sDV)
        consume("PY0", tms)
        tsq8 = E("scalar", lambda sc: sc.activation(rstd8[:, :], rstd8[:, :], ACT.Sqrt),
                 [tms], sAC)

        def mkrec(v):
            return v.reciprocal(rstd8[:, :], rstd8[:, :])
        trec = E("vector", mkrec, [tsq8], sDV)

        tscale = [None] * NXT
        tgrep = None
        for gg in range(8):
            trow = sdma(rstdrow[0:1, :], rstd8[gg:gg + 1, :],
                        [trec] + ([tgrep] if tgrep else []))
            tgrep_mm = mm("PC0", PC[0][:, 0:512], on1[0:1, :], rstdrow[0:1, :], True, True,
                          [trow], sPE)
            for kk in range(gg * 8, gg * 8 + 8):
                def mksc2(kk=kk):
                    def go(v):
                        return v.scalar_tensor_tensor(xfm[:, kk * TR:(kk + 1) * TR],
                                                      xfm[:, kk * TR:(kk + 1) * TR],
                                                      nwcolb[:, kk:kk + 1],
                                                      PC[0][:, 0:512], ALU.mult, ALU.mult)
                    return go
                tscale[kk] = E("vector", mksc2(), [tgrep_mm, tsq[kk]], sDV)
                consume("PC0", tscale[kk])
            tgrep = tscale[gg * 8 + 7]

        if debug:
            for k in range(NXT):
                sdma(dbg["ygn"][r, k * 128:(k + 1) * 128, :], xfm[:, k * TR:(k + 1) * TR],
                     [tscale[k]])
        if stage < 6:
            return None

        twsl = [None] * (2 * NKD)
        twb = [None] * NKD
        tdmm = [None] * NKD
        tdev = [None] * NKD
        txk = {}
        tflush = {}
        tadds = {}

        def wdma(dtl, half, wl):
            twsl[2 * dtl + half] = gdma(wotsl[half][:, :],
                                        dwto[dtl, :, half * 4096:(half + 1) * 4096], wl)

        def wbdma(dtl, wl):
            twb[dtl] = gdma(wbtsl[dtl % 2][:, :], dwbt[dtl, :, :], wl)

        wdma(0, 0, [tgmm[NXT - 2]])
        wdma(0, 1, [tgmm[NXT - 1]])
        wbdma(0, [tphB])
        wbdma(1, [tphB])

        def xtdma(dg, ts_, wl):
            txk[(dg, ts_)] = sdma(xtok[ts_][:, :],
                                  xdram[3 + ts_ * 128:3 + (ts_ + 1) * 128,
                                        dg * 512:(dg + 1) * 512], wl)
        for ts_ in range(4):
            xtdma(0, ts_, [tphB])

        for dtl in range(NKD):
            bank, ps = "PA%d" % (dtl % 2), PA[dtl % 2]
            dg = dtl // 4
            tk = None
            for k in range(NKD):
                tk = mm(bank, ps[:, :], wbtsl[dtl % 2][:, k * 128:(k + 1) * 128],
                        xnb[:, k * TR:(k + 1) * TR], k == 0, False,
                        [twb[dtl]], None)
            tk = mm(bank, ps[:, :], vtb[0:40, dtl * 128:(dtl + 1) * 128], etall[0:40, :],
                    False, False, [tet_done], None)
            for half in range(2):
                for k2 in range(NKD):
                    kk = half * NKD + k2
                    last = (half == 1 and k2 == NKD - 1)
                    inc2 = sPE if (last or k2 == NKD - 1) else None
                    tk = mm(bank, ps[:, :],
                            wotsl[half][:, k2 * 128:(k2 + 1) * 128],
                            xfm[:, kk * TR:(kk + 1) * TR], False, last,
                            [twsl[2 * dtl + half], tscale[kk]], inc2)
                if half == 0 and dtl + 1 < NKD:
                    wdma(dtl + 1, 0, [tk])
            tdmm[dtl] = tk
            if dtl + 2 < NKD:
                wbdma(dtl + 2, [tdmm[dtl]])
            if dtl + 1 < NKD:
                wdma(dtl + 1, 1, [tdmm[dtl]])

            def mkoev(dtl=dtl, par=dtl % 2):
                def go(v):
                    return v.tensor_scalar(outsb[dtl % 2][:, :], PA[par][:, :],
                                           bbcolb[:, dtl:dtl + 1], None, ALU.add)
                return go
            wv = [tdmm[dtl]] + ([tdev[dtl - 2]] if dtl >= 2 else [])
            tev_ = E("vector", mkoev(), wv, sDV)
            consume(bank, tev_)
            for ts_ in range(4):
                tb = "PY%d" % (ts_ % 2)
                tt_o = tpose(tb, PY[ts_ % 2][:, 128:256], outsb[dtl % 2][:, ts_ * 128:(ts_ + 1) * 128],
                             ident32[:, :], [tev_])

                def mkres(dtl=dtl, ts_=ts_, dg=dg):
                    def go(v):
                        return v.tensor_add(outst[ts_][:, (dtl % 4) * 128:(dtl % 4 + 1) * 128],
                                            PY[ts_ % 2][:, 128:256],
                                            xtok[ts_][:, (dtl % 4) * 128:(dtl % 4 + 1) * 128])
                    return go
                wl3 = [tt_o, txk[(dg, ts_)]]
                if (dg - 1, ts_) in tflush:
                    wl3.append(tflush[(dg - 1, ts_)])
                tadds[(dtl, ts_)] = E("vector", mkres(), wl3, sDV)
                consume(tb, tadds[(dtl, ts_)])
            tdev[dtl] = tadds[(dtl, 3)]
            if dtl % 4 == 3:
                for ts_ in range(4):
                    tflush[(dg, ts_)] = sdma(
                        dout[r * TR + ts_ * 128:r * TR + (ts_ + 1) * 128,
                             dg * 512:(dg + 1) * 512],
                        outst[ts_][:, :], [tadds[(dtl, ts_)]])
                if dg + 1 < 8:
                    for ts_ in range(4):
                        xtdma(dg + 1, ts_, [tadds[(dtl, ts_)]])
        tend_pe = p.tok(sPE)
        tend_dv = p.tok(sDV)
        tend_dx = p.tok(sDX)
        tend_dw = p.tok(sDW)
        tend_ac = p.tok(sAC)
        return (tend_pe, tend_dv, tend_dx, tend_dw, tend_ac)

    prev_end = None
    for r in range(2 if stage >= 6 else 1):
        prev_end = round_r(r, prev_end)

    E("sync", lambda s_: s_.nop(), [p.tok(sDX)])
    E("gpsimd", lambda g: g.nop(), [p.tok(sDW)])

    def gapfn(e):
        e.tensor_copy(gapb[0:1, 0:1], gapb[0:1, 1:2])

    with nc.Block() as block:
        p.replay(block, gapfn=gapfn)
    st.close()
    return nc


def kernel(**inputs):
    sh = host_shared(inputs)
    nc = build()
    in_maps = []
    for core in range(8):
        m = dict(sh)
        m.update(host_percore(inputs, core))
        in_maps.append(m)
    res = run_bass_kernel_spmd(nc, in_maps, core_ids=list(range(8)))
    out = np.empty((B, S, D), np.float32)
    for core in range(8):
        b, q = core // 4, core % 4
        o = res.results[core]["out"]
        out[b, 512 * q:512 * q + 512] = o[0:512]
        out[b, 2048 + 512 * q:2048 + 512 * q + 512] = o[512:1024]
    return out


# version 02
# speedup vs baseline: 1.0420x; 1.0420x over previous
import math
import contextlib
import numpy as np
import ml_dtypes
import concourse.bass as bass
import concourse.mybir as mybir
from concourse.bass_utils import run_bass_kernel_spmd

F32 = mybir.dt.float32
BF16 = mybir.dt.bfloat16
U8 = mybir.dt.uint8
ALU = mybir.AluOpType
ACT = mybir.ActivationFunctionType
AXL = mybir.AxisListType

B, S, D = 2, 4096, 4096
NH, HD, NG, KTAP = 128, 64, 8, 4
CHL = 128
INTER = NH * HD
CONVD = INTER + 2 * NG * 128
PROJ = INTER + CONVD + NH
TR = 512
NCHUNK = TR // CHL
NKD = D // 128
NXT = INTER // 128
DT_MIN, DT_MAX, EPS = 1e-3, 100.0, 1e-5
SCALING = 16.0 / 32.0
NCT = 81

STAGE = 6
DEBUG = False


class Pl:
    ENGINES = ("sync", "gpsimd", "scalar", "vector", "tensor")

    def __init__(self, nc):
        self.nc = nc
        self.items = {e: [] for e in self.ENGINES}
        self.cnt = {}
        self.hand = {}
        self.hiw = {}

    def sem(self, stack, name):
        self.hand[name] = stack.enter_context(self.nc.semaphore(name))
        self.cnt[name] = 0
        return name

    def tok(self, sem):
        return (sem, self.cnt[sem])

    def emit(self, eng, fn, waits=(), inc=None, amt=1):
        tokv = None
        if inc is not None:
            self.cnt[inc] += amt
            tokv = (inc, self.cnt[inc])
        wl = []
        for w in waits:
            if w is None:
                continue
            s, v = w
            if v <= 0 or self.hiw.get((eng, s), 0) >= v:
                continue
            self.hiw[(eng, s)] = v
            wl.append((s, v))
        self.items[eng].append((fn, wl, tokv, amt))
        return tokv

    @staticmethod
    def _spans(args):
        out = []
        for a in args:
            name = getattr(getattr(a, "tensor", None), "name", None)
            if name is None:
                memset = getattr(a, "memsetref", None) or ""
                name = memset[:-4] if memset.endswith("_set") else memset
            ap = getattr(a, "ap", None)
            off = getattr(a, "offset", 0)
            if ap is None:
                continue
            span = 1
            for (st_, cn) in list(ap)[1:]:
                span += abs(st_) * (cn - 1)
            out.append((str(name), off, off + span))
        return out

    def _vgaps(self, gapfn):
        nc = self.nc
        gaps = set()
        intra = []
        with nc.discard():
            e = nc.vector
            recs = []
            for idx, (fn, wl, tokv, amt) in enumerate(self.items["vector"]):
                insts = []
                with _InstCatcher(nc, insts):
                    fn(e)
                for j, ins in enumerate(insts):
                    rspan = self._spans(getattr(ins, "ins", []) or [])
                    wspan = self._spans(getattr(ins, "outs", []) or [])
                    recs.append((idx, j, rspan, wspan))
            for k in range(1, len(recs)):
                pidx, pj, pr, pw = recs[k - 1]
                idx, j, rr, rw = recs[k]
                hit = False
                for (wn, wl_, wh) in pw:
                    if wn.startswith("c_gap"):
                        continue
                    for (rn, rl, rh) in rr:
                        if wn == rn and rl < wh and wl_ < rh:
                            hit = True
                if not hit:
                    continue
                if pidx == idx:
                    intra.append((idx, pj, j))
                else:
                    gaps.add(idx)
        if intra:
            raise AssertionError(f"intra-item vector RAW chains (add gaps inside): {intra[:10]}")
        return gaps

    def replay(self, block, gapfn=None):
        hand = self.hand
        vgaps = self._vgaps(gapfn) if gapfn is not None else set()

        def mk(eng):
            items = self.items[eng]

            def go(e):
                for idx, (fn, wl, tokv, amt) in enumerate(items):
                    if eng == "vector" and idx in vgaps:
                        gapfn(e)
                    for (s, v) in wl:
                        e.wait_ge(hand[s], v)
                    inst = fn(e)
                    if tokv is not None:
                        if inst is None:
                            inst = e.nop()
                        inst.then_inc(hand[tokv[0]], amt)
            return go

        for eng in self.ENGINES:
            getattr(block, eng)(mk(eng))


class _InstCatcher:

    def __init__(self, nc, sink):
        self.nc = nc
        self.sink = sink

    def __enter__(self):
        self.cb = lambda ins: self.sink.append(ins)
        self.nc.push_inst_callback(self.cb)
        return self

    def __exit__(self, *a):
        self.nc.remove_inst_callback(self.cb)
        return False


_HC = {}


def fourier_mt():
    if "mt" not in _HC:
        n = D
        j = np.arange(n, dtype=np.float64)
        Mt = np.empty((n, n), dtype=np.float64)
        Mt[0, :] = 1.0 / math.sqrt(n)
        maxf = (n - 1) // 2
        f = np.arange(1, maxf + 1, dtype=np.float64)
        ang = 2.0 * np.pi * np.outer(f, j) / n
        c = math.sqrt(2.0 / n)
        Mt[1:2 * maxf:2, :] = c * np.cos(ang)
        Mt[2:2 * maxf + 1:2, :] = c * np.sin(ang)
        Mt[n - 1, :] = ((-1.0) ** j) / math.sqrt(n)
        _HC["mt"] = np.ascontiguousarray(Mt.astype(np.float32))
    return _HC["mt"]


def _slab_image(wT, c0):
    a = wT[:, c0:c0 + 128]
    return a.reshape(D // 128, 128, 128).transpose(1, 0, 2).reshape(128, D)


def host_shared(inp):
    i = np.arange(128)
    sh = {}
    wti = np.asarray(inp["L0_attn_in_proj_w"]).T
    sh["wti"] = np.stack([_slab_image(wti, c * 128) for c in range(PROJ // 128)])
    wto = np.asarray(inp["L0_attn_out_proj_w"]).T
    im = []
    for dtl in range(NKD):
        a = wto[:, dtl * 128:(dtl + 1) * 128]
        im.append(a.reshape(64, 128, 128).transpose(1, 0, 2).reshape(128, 8192))
    sh["wto"] = np.stack(im)
    wbt = np.asarray(inp["L0_ffn_W_base"]).T
    sh["wbt"] = np.stack([_slab_image(wbt, c * 128) for c in range(NKD)])
    sh["mt"] = fourier_mt()
    cw = np.asarray(inp["L0_attn_conv_w"]).reshape(CONVD, KTAP)
    convd = np.zeros((CONVD // 128, 128, KTAP, 128), np.float32)
    t = np.arange(CONVD)
    for jj in range(KTAP):
        convd[t // 128, t % 128, jj, t % 128] = cw[:, jj]
    sh["convd"] = convd.reshape(CONVD // 128, 128, KTAP * 128)
    sh["convb"] = np.ascontiguousarray(np.asarray(inp["L0_attn_conv_b"]).reshape(CONVD // 128, 128).T)
    sh["dtb"] = np.asarray(inp["L0_attn_dt_bias"]).reshape(128, 1).astype(np.float32)
    sh["acol"] = (-(i + 1).astype(np.float32)).reshape(128, 1)
    sh["dcol"] = np.ascontiguousarray(np.repeat(np.asarray(inp["L0_attn_D"]), HD).reshape(NXT, 128).T)
    sh["nwcol"] = np.ascontiguousarray(np.asarray(inp["L0_attn_norm_w"]).reshape(NXT, 128).T)
    sh["bbcol"] = np.ascontiguousarray(np.asarray(inp["L0_ffn_b_base"]).reshape(NKD, 128).T)
    sh["enct"] = np.ascontiguousarray(
        np.concatenate([np.asarray(inp["L0_ffn_enc0"]).T, np.asarray(inp["L0_ffn_enc1"]).T], axis=1))
    co = np.concatenate([np.transpose(np.asarray(inp["L0_ffn_coefs0"]), (1, 0, 2)),
                         np.transpose(np.asarray(inp["L0_ffn_coefs1"]), (1, 0, 2))],
                        axis=2).reshape(D, 120)
    sh["coef"] = np.ascontiguousarray(co)
    cm = np.concatenate([np.transpose(np.asarray(inp["L0_ffn_mask0"]), (1, 0, 2)),
                         np.transpose(np.asarray(inp["L0_ffn_mask1"]), (1, 0, 2))],
                        axis=2).reshape(D, 120)
    sh["cmask"] = np.ascontiguousarray(cm.astype(np.uint8))
    sh["ident32"] = np.eye(128, dtype=np.float32)
    sh["identbf"] = np.eye(128, dtype=ml_dtypes.bfloat16)
    sh["lt1"] = (i[:, None] <= i[None, :]).astype(np.float32)
    sh["ones128"] = np.ones((128, 128), np.float32)
    sh["utm"] = (i[:, None] <= i[None, :]).astype(ml_dtypes.bfloat16)
    sh["ones1"] = np.ones((1, 128), np.float32)
    gc = np.zeros((8, 128, 8), dtype=ml_dtypes.bfloat16)
    for g in range(8):
        gc[g, :, g] = 1
    sh["gcol"] = gc
    return sh


def host_percore(inp, core):
    b, q = core // 4, core % 4
    x = np.asarray(inp["L0_attn_hidden_states"])[b]
    pc = {}
    for r in range(2):
        t0 = 2048 * r + 512 * q
        main = x[t0:t0 + TR]
        halo = x[t0 - 3:t0] if t0 >= 3 else np.zeros((3, D), np.float32)
        pc[f"x{r}"] = np.ascontiguousarray(np.concatenate([halo, main], axis=0))
    sel = np.zeros((128, 8), np.float32)
    for j in range(3):
        sel[:, j] = 1.0 if j < q else 0.0
        sel[:, 4 + j] = 0.0 if j < q else 1.0
    pc["selm"] = sel
    return pc


def build(stage=STAGE, debug=DEBUG):
    nc = bass.Bass("TRN2", target_bir_lowering=False, detect_race_conditions=False)
    st = contextlib.ExitStack()

    dxr = [nc.dram_tensor(f"x{r}", [3 + TR, D], F32, kind="ExternalInput") for r in range(2)]
    dwti = nc.dram_tensor("wti", [PROJ // 128, 128, D], F32, kind="ExternalInput")
    dwto = nc.dram_tensor("wto", [NKD, 128, INTER], F32, kind="ExternalInput")
    dwbt = nc.dram_tensor("wbt", [NKD, 128, D], F32, kind="ExternalInput")
    dmt = nc.dram_tensor("mt", [D, D], F32, kind="ExternalInput")
    dconvd = nc.dram_tensor("convd", [CONVD // 128, 128, KTAP * 128], F32, kind="ExternalInput")
    dconvb = nc.dram_tensor("convb", [128, CONVD // 128], F32, kind="ExternalInput")
    ddtb = nc.dram_tensor("dtb", [128, 1], F32, kind="ExternalInput")
    dacol = nc.dram_tensor("acol", [128, 1], F32, kind="ExternalInput")
    ddcol = nc.dram_tensor("dcol", [128, NXT], F32, kind="ExternalInput")
    dnwcol = nc.dram_tensor("nwcol", [128, NXT], F32, kind="ExternalInput")
    dbbcol = nc.dram_tensor("bbcol", [128, NKD], F32, kind="ExternalInput")
    denct = nc.dram_tensor("enct", [D, 40], F32, kind="ExternalInput")
    dcoef = nc.dram_tensor("coef", [D, 120], F32, kind="ExternalInput")
    dcmask = nc.dram_tensor("cmask", [D, 120], U8, kind="ExternalInput")
    dselm = nc.dram_tensor("selm", [128, 8], F32, kind="ExternalInput")
    di32 = nc.dram_tensor("ident32", [128, 128], F32, kind="ExternalInput")
    dibf = nc.dram_tensor("identbf", [128, 128], BF16, kind="ExternalInput")
    dlt1 = nc.dram_tensor("lt1", [128, 128], F32, kind="ExternalInput")
    don128 = nc.dram_tensor("ones128", [128, 128], F32, kind="ExternalInput")
    dutm = nc.dram_tensor("utm", [128, 128], BF16, kind="ExternalInput")
    don1 = nc.dram_tensor("ones1", [1, 128], F32, kind="ExternalInput")
    dgcol = nc.dram_tensor("gcol", [8, 128, 8], BF16, kind="ExternalInput")
    dout = nc.dram_tensor("out", [2 * TR, D], F32, kind="ExternalOutput")
    dagin = [nc.dram_tensor(f"agin{r}", [128, INTER + 128], F32) for r in range(2)]
    dagout = [nc.dram_tensor(f"agout{r}", [512, INTER + 128], F32) for r in range(2)]
    dst2048 = nc.dram_tensor("st2048", [128, INTER], F32)

    dbg = {}
    if debug:
        for nm, shp, dt_ in [("xn", [2, D, TR], BF16), ("xc", [2, INTER, TR], BF16),
                             ("bc", [2, 2048, TR], BF16), ("dt", [2, 128, TR], F32),
                             ("cs", [2, 128, TR], F32), ("ya", [2, INTER, TR], BF16),
                             ("pay", [2, 128, INTER + 128], F32),
                             ("raw", [2, 128, 544], BF16),
                             ("yb", [2, INTER, TR], BF16), ("ygn", [2, INTER, TR], BF16)]:
            dbg[nm] = nc.dram_tensor("dbg_" + nm, shp, dt_, kind="ExternalOutput")

    def sb(name, shape, dtype):
        return nc.alloc_sbuf_tensor(name, shape, dtype)

    ident32 = sb("c_i32", [128, 128], F32)
    identbf = sb("c_ibf", [128, 128], BF16)
    lt1 = sb("c_lt1", [128, 128], F32)
    on128 = sb("c_on1x", [128, 128], F32)
    utm = sb("c_utm", [128, 128], BF16)
    on1 = sb("c_on1", [1, 128], F32)
    acol = sb("c_acol", [128, 1], F32)
    dtbc = sb("c_dtb", [128, 1], F32)
    dcolb = sb("c_dcol", [128, NXT], F32)
    nwcolb = sb("c_nw", [128, NXT], F32)
    bbcolb = sb("c_bb", [128, NKD], F32)
    selmb = sb("c_selm", [128, 8], F32)
    gcolb = sb("c_gcol", [128, 64], BF16)
    zero128 = sb("c_zero", [128, 128], F32)
    gapb = sb("c_gap", [1, 8], F32)
    convbb = sb("c_convb", [128, CONVD // 128], F32)
    enctb = sb("c_enct", [128, NKD * 40], BF16)
    vtb = sb("c_vt", [40, D], BF16)
    etall = sb("c_et", [40, TR], BF16)
    drowb = sb("c_drow", [1, 128], F32)
    cdrowb = sb("c_cdrow", [1, 128], F32)
    xnhalo = sb("c_xnh", [128, NKD * 3], BF16)
    statc = sb("c_stat", [128, 16], F32)

    xnb = sb("a_xn", [128, NKD * TR], BF16)
    xfm = sb("a_xfm", [128, NXT * TR], BF16)
    bfmb = sb("a_bfm", [128, NG * TR], BF16)
    cfmb = sb("a_cfm", [128, NG * TR], BF16)

    ov = sb("a_ov", [128, 36 * 1024 + 896], U8)
    ovoff = nc.lookup_mloc(ov).addr

    def at(name, shape, dtype, off):
        return nc.alloc_sbuf_tensor_at(name, shape, dtype, offset=ovoff + off)

    inslab = [at(f"w_in{i}", [128, NKD * 128], BF16, i * 8192) for i in range(2)]
    convdsl = [at(f"w_cd{i}", [128, KTAP * 128], BF16, 16384 + i * 1024) for i in range(2)]
    rawsl = [at(f"w_raw{i}", [128, 544], BF16, 18432 + i * 1088) for i in range(2)]
    rbuf = at("r_buf", [128, INTER], F32, 0)
    ainbuf = rbuf
    wotsl = [at(f"w_ot{i}", [128, NKD * 128], BF16, i * 8192) for i in range(2)]
    wbtsl = [at(f"w_bt{i}", [128, NKD * 128], BF16, 16384 + i * 8192) for i in range(2)]
    gatesb = [at(f"w_gt{i}", [128, 512], BF16, 32768 + i * 1024) for i in range(2)]
    ygsqsb = [at(f"w_yq{i}", [128, 512], BF16, 34816 + i * 1024) for i in range(2)]

    sa = sb("a_str", [128, 24 * 1024], U8)
    saoff = nc.lookup_mloc(sa).addr

    def ats(name, shape, dtype, off):
        return nc.alloc_sbuf_tensor_at(name, shape, dtype, offset=saoff + off)

    xhalf = [ats(f"s_xh{i}", [128, 2048], F32, i * 8192) for i in range(2)]
    xntt = ats("s_xntt", [128, D], BF16, 16384)
    coefw = ats("c_coefw", [128, NKD * 40], BF16, 0)
    coefbuf = ats("c_coefb", [128, 120], F32, 2560)
    cmaskbuf = ats("c_cmkb", [128, 120], F32, 3072)
    vacc = ats("c_vacc", [40, D], F32, 4096)
    xtok = [ats(f"s_xtok{i}", [128, 512], F32, i * 2048) for i in range(4)]
    outsb = [ats(f"s_osb{i}", [128, 512], F32, 8192 + i * 2048) for i in range(2)]
    outst = [ats(f"s_ost{i}", [128, 512], F32, 12288 + i * 2048) for i in range(4)]
    sjbuf = [ats(f"s_sj{i}", [128, 2048], F32, i * 8192) for i in range(2)]

    csb = sb("d_cs", [128, TR], F32)
    negcs = sb("d_ncs", [128, TR], F32)
    dtsb = sb("d_dt", [128, TR], F32)
    dasb = sb("d_da", [128, 128], F32)
    datsb = sb("d_dat", [128, 4 * 128], F32)
    cstsb = sb("d_cst", [128, 128], F32)
    ncstsb = sb("d_ncst", [128, 128], F32)
    dttsb = sb("d_dtt", [128, 128], F32)
    dectsb = sb("d_dect", [128, 128], F32)
    expltsb = sb("d_explt", [128, 128], F32)
    expgtsb = sb("d_expgt", [128, 4 * 128], F32)
    dechsb = sb("d_dech", [128, 128], F32)
    qsb = sb("d_q", [128, 128], F32)
    ebsb = sb("d_eb", [128, 40], F32)
    bcastsb = [ats(f"d_bc{i}", [128, 128], F32, i * 512) for i in range(4)]
    lmsb = [ats(f"d_lm{i}", [128, 128], BF16, 2048 + i * 256) for i in range(4)]
    sclsb = [ats(f"d_scl{i}", [128, 128], BF16, 3072 + i * 256) for i in range(4)]
    scmsb = [ats(f"d_scm{i}", [128, 128], BF16, 4096 + i * 256) for i in range(2)]
    btsb = [ats(f"d_bt{i}", [128, 128], BF16, 4608 + i * 256) for i in range(2)]
    bdtsb = [ats(f"d_bdt{i}", [128, 128], BF16, 5120 + i * 256) for i in range(4)]
    xdttsb = [ats(f"d_xdtt{i}", [128, 128], BF16, 6144 + i * 256) for i in range(4)]
    rcpsb = [ats(f"d_rcp{i}", [128, 128], BF16, 7168 + i * 256) for i in range(2)]
    qdsb = [sb(f"d_qd{i}", [128, 128], F32) for i in range(2)]
    yofsb = [ats(f"d_yof{i}", [128, 128], F32, 7680 + i * 512) for i in range(2)]
    segsb = [ats(f"d_seg{i}", [128, 128], F32, 8704 + i * 512) for i in range(4)]
    ycsb = [ats(f"d_yc{i}", [128, 128], BF16, 10752 + i * 256) for i in range(2)]
    rstd8 = ats("c_rstd8", [8, 512], F32, 20480)
    rstdrow = ats("c_rsrow", [1, 512], F32, 22528)

    assert nc.sbuf_bytes_remaining >= 0, f"SBUF over by {-nc.sbuf_bytes_remaining}"

    PA = [nc.alloc_psum_tensor(f"pa{i}", [128, 512], F32) for i in range(2)]
    PT = [nc.alloc_psum_tensor(f"ptp{i}", [128, 1024], BF16) for i in range(2)]
    PC = [nc.alloc_psum_tensor(f"pcc{i}", [128, 512], F32) for i in range(2)]
    PY = [nc.alloc_psum_tensor(f"pyy{i}", [128, 512], F32) for i in range(2)]

    p = Pl(nc)
    sDW = p.sem(st, "sDW")
    sDX = p.sem(st, "sDX")
    sPE = p.sem(st, "sPE")
    sDV = p.sem(st, "sDV")
    sAC = p.sem(st, "sAC")
    sGP = p.sem(st, "sGP")
    E = p.emit

    bankw = {}

    def consume(bank, tok):
        if tok is not None:
            bankw.setdefault(bank, []).append(tok)

    def bank_free(bank):
        return bankw.pop(bank, [])

    def gdma(out, in_, waits=()):
        return E("gpsimd", (lambda o, i: lambda g: g.dma_start(o, i))(out, in_), waits, sDW, 16)

    def sdma(out, in_, waits=()):
        return E("sync", (lambda o, i: lambda s_: s_.dma_start(o, i))(out, in_), waits, sDX, 16)

    def mm(bank, out, lhsT, rhs, start, stop, waits=(), inc=None, isT=False):
        w = list(waits) + (bank_free(bank) if start else [])
        fn = (lambda o, l, r, s0, s1, it: lambda t: t.matmul(
            o, l, r, start=s0, stop=s1, is_transpose=it))(out, lhsT, rhs, start, stop, isT)
        return E("tensor", fn, w, inc)

    def tpose(bank, out, in_, ident, waits=(), inc=sPE, start=True, stop=True):
        return mm(bank, out, in_, ident, start, stop, waits, inc, isT=True)

    def vcopy(out, in_, waits=(), inc=sDV):
        return E("vector", (lambda o, i: lambda v: v.tensor_copy(o, i))(out, in_), waits, inc)

    for dst, src in [(ident32, di32), (identbf, dibf), (lt1, dlt1), (on128, don128),
                     (utm, dutm), (on1, don1), (acol, dacol), (dtbc, ddtb),
                     (dcolb, ddcol), (nwcolb, dnwcol), (bbcolb, dbbcol), (selmb, dselm),
                     (convbb, dconvb)]:
        sdma(dst[:, :], src[:, :])
    for g in range(8):
        sdma(gcolb[:, g * 8:(g + 1) * 8], dgcol[g, :, :])
    for k in range(NKD):
        gdma(enctb[:, k * 40:(k + 1) * 40], denct[k * 128:(k + 1) * 128, :])
    tgap0 = E("gpsimd", lambda g: g.memset(gapb[:, :], 0.0), (), sGP)
    E("gpsimd", lambda g: g.memset(zero128[:, :], 0.0), (), sGP)
    E("gpsimd", lambda g: g.memset(vtb[:, :], 0.0), (), sGP)
    tzero = E("gpsimd", lambda g: g.memset(etall[:, :], 0.0), (), sGP)
    tconst = p.tok(sDX)
    tconstw = p.tok(sDW)

    twe = [None] * NKD
    for k in range(NKD):
        wl = [twe[k - 1]] if k >= 1 else []
        t1 = sdma(coefbuf[:, :], dcoef[k * 128:(k + 1) * 128, :], wl)
        t2 = gdma(cmaskbuf[:, :], dcmask[k * 128:(k + 1) * 128, :], wl)

        def mkwe(k=k):
            def go(v):
                v.tensor_mul(coefbuf[:, 0:40], coefbuf[:, 0:40], cmaskbuf[:, 0:40])
                v.tensor_mul(coefbuf[:, 40:80], coefbuf[:, 40:80], cmaskbuf[:, 40:80])
                v.tensor_mul(coefbuf[:, 80:120], coefbuf[:, 80:120], cmaskbuf[:, 80:120])
                v.tensor_add(coefbuf[:, 0:40], coefbuf[:, 0:40], coefbuf[:, 40:80])
                v.tensor_copy(gapb[0:1, 0:1], gapb[0:1, 1:2])
                return v.tensor_add(coefw[:, k * 40:(k + 1) * 40], coefbuf[:, 0:40],
                                    coefbuf[:, 80:120])
            return go
        twe[k] = E("vector", mkwe(), [t1, t2], sDV)

    tvz = E("gpsimd", lambda g: g.memset(vacc[:, :], 0.0), (), sGP)
    tmt = [None] * NKD
    tvo = [None] * NKD
    for o in range(2):
        tmt[o] = gdma(inslab[o % 2][:, :], dmt[o * 128:(o + 1) * 128, :], [])
    for o in range(NKD):
        for js in range(8):
            bank = "PC%d" % (js % 2)
            tk = mm(bank, PC[js % 2][0:40, 0:512], coefw[:, o * 40:(o + 1) * 40],
                    inslab[o % 2][:, js * 512:(js + 1) * 512], True, True,
                    [tmt[o], twe[o]], sPE)

            def mkadd(js=js, bank=bank):
                def go(v):
                    return v.tensor_add(vacc[0:40, js * 512:(js + 1) * 512],
                                        vacc[0:40, js * 512:(js + 1) * 512],
                                        PC[js % 2][0:40, 0:512])
                return go
            ta = E("vector", mkadd(), [tk, tvz], sDV)
            consume(bank, ta)
        tvo[o] = tk
        if o + 2 < NKD:
            tmt[o + 2] = gdma(inslab[o % 2][:, :], dmt[(o + 2) * 128:(o + 3) * 128, :], [tk])
    tvt = E("vector", lambda v: v.tensor_scalar(vtb[0:40, :], vacc[0:40, :], SCALING, None,
                                                ALU.mult), [p.tok(sDV)], sDV)
    tsetup_pe = p.tok(sPE)
    tsetup_dv = p.tok(sDV)

    def ln_tile(xdram, row0, nrow, dst_fm, colstride, dstcol0, extra=(), prev=None):
        pw = []
        if prev is not None:
            pw = [prev[0]] + list(prev[1])
        th = [sdma(xhalf[h][0:nrow, :], xdram[row0:row0 + nrow, h * 2048:(h + 1) * 2048],
                   list(extra) + pw) for h in range(2)]

        def mkst(h):
            def go(v):
                return v.tensor_reduce(statc[0:nrow, h:h + 1], xhalf[h][0:nrow, :], AXL.X, ALU.add)
            return go

        def mksq(h):
            def go(sc):
                return sc.activation(xntt[0:nrow, h * 2048:(h + 1) * 2048], xhalf[h][0:nrow, :],
                                     ACT.Square, accum_out=statc[0:nrow, 2 + h:3 + h])
            return go
        ts0 = E("vector", mkst(0), [th[0]], sDV)
        ts1 = E("vector", mkst(1), [th[1]], sDV)
        tq0 = E("scalar", mksq(0), [th[0]], sAC)
        tq1 = E("scalar", mksq(1), [th[1]], sAC)

        def mkmv(v):
            v.tensor_add(statc[0:nrow, 4:5], statc[0:nrow, 0:1], statc[0:nrow, 1:2])
            v.tensor_add(statc[0:nrow, 5:6], statc[0:nrow, 2:3], statc[0:nrow, 3:4])
            v.tensor_scalar(statc[0:nrow, 4:5], statc[0:nrow, 4:5], 1.0 / D, None, ALU.mult)
            v.tensor_scalar(statc[0:nrow, 5:6], statc[0:nrow, 5:6], 1.0 / D, None, ALU.mult)
            v.tensor_mul(statc[0:nrow, 6:7], statc[0:nrow, 4:5], statc[0:nrow, 4:5])
            v.tensor_copy(gapb[0:1, 0:1], gapb[0:1, 1:2])
            v.tensor_sub(statc[0:nrow, 5:6], statc[0:nrow, 5:6], statc[0:nrow, 6:7])
            v.tensor_copy(gapb[0:1, 0:1], gapb[0:1, 1:2])
            return v.tensor_scalar(statc[0:nrow, 5:6], statc[0:nrow, 5:6], EPS, None, ALU.add)
        tmv = E("vector", mkmv, [ts0, ts1, tq0, tq1], sDV)
        tsq = E("scalar", lambda sc: sc.activation(statc[0:nrow, 7:8], statc[0:nrow, 5:6],
                                                   ACT.Sqrt), [tmv], sAC)

        def mknewt(v):
            v.reciprocal(statc[0:nrow, 8:9], statc[0:nrow, 7:8])
            v.tensor_copy(gapb[0:1, 0:1], gapb[0:1, 1:2])
            v.tensor_mul(statc[0:nrow, 9:10], statc[0:nrow, 8:9], statc[0:nrow, 8:9])
            v.tensor_copy(gapb[0:1, 0:1], gapb[0:1, 1:2])
            v.tensor_mul(statc[0:nrow, 9:10], statc[0:nrow, 9:10], statc[0:nrow, 5:6])
            v.tensor_copy(gapb[0:1, 0:1], gapb[0:1, 1:2])
            v.tensor_scalar(statc[0:nrow, 9:10], statc[0:nrow, 9:10], -0.5, 1.5, ALU.mult, ALU.add)
            v.tensor_copy(gapb[0:1, 0:1], gapb[0:1, 1:2])
            return v.tensor_mul(statc[0:nrow, 8:9], statc[0:nrow, 8:9], statc[0:nrow, 9:10])
        tnw = E("vector", mknewt, [tsq], sDV)
        tn = []
        for h in range(2):
            def mknm(h=h):
                def go(v):
                    return v.tensor_scalar(xntt[0:nrow, h * 2048:(h + 1) * 2048],
                                           xhalf[h][0:nrow, :], statc[0:nrow, 4:5],
                                           statc[0:nrow, 8:9], ALU.subtract, ALU.mult)
                return go
            tn.append(E("vector", mknm(h), [tnw], sDV))
        tlast = None
        tt = None
        for k in range(NKD):
            bank = "PT%d" % (k % 2)
            slot = PT[k % 2][:, 128 * ((k // 2) % 2):128 * ((k // 2) % 2) + nrow]
            tt = tpose(bank, slot, xntt[0:nrow, k * 128:(k + 1) * 128],
                       identbf[0:nrow, 0:nrow], tn)
            te = vcopy(dst_fm[:, k * colstride + dstcol0:k * colstride + dstcol0 + nrow],
                       slot, [tt])
            consume(bank, te)
            tlast = te
        return (tt, [tn[0], tn[1], tq0, tq1], tlast)

    def round_r(r, prev_end=None):
        xdram = dxr[r]
        lnprev = None
        for tt_ in range(4):
            ex = [tconst, tsetup_pe, tsetup_dv] if (r == 0 and tt_ == 0) else []
            if prev_end is not None and tt_ == 0:
                ex = ex + list(prev_end)
            lnprev = ln_tile(xdram, 3 + tt_ * 128, 128, xnb, TR, tt_ * 128,
                             ex, prev=lnprev)
        ln_tile(xdram, 0, 3, xnhalo, 3, 0, prev=lnprev)
        txn = p.tok(sDV)

        if debug:
            for k in range(NKD):
                sdma(dbg["xn"][r, k * 128:(k + 1) * 128, :], xnb[:, k * TR:(k + 1) * TR], [txn])
        if stage < 2:
            return

        def ct_info(ci):
            if ci == 0:
                return ("dt", PROJ // 128 - 1, None, None)
            if ci <= 16:
                gg = ci - 1
                half, g = gg // 8, gg % 8
                cti = (2 * INTER + half * NG * 128 + g * 128) // 128
                return ("bc", cti, cfmb if half else bfmb, g)
            k = ci - 17
            return ("x", (INTER + k * 128) // 128, xfm, k)

        tslab = [None] * NCT
        tcdsl = [None] * NCT
        tmmend = [None] * NCT
        tconvend = [None] * NCT
        tevac = [None] * NCT

        def slabdma(ci, wl, wl2):
            kind, cti, dst, idx = ct_info(ci)
            tslab[ci] = gdma(inslab[ci % 2][:, :], dwti[cti, :, :], wl)
            if kind != "dt":
                tcdsl[ci] = gdma(convdsl[ci % 2][:, :], dconvd[cti - INTER // 128, :, :], wl2)
        pe_guard = [tsetup_pe] + (list(prev_end) if prev_end is not None else [])
        slabdma(0, pe_guard, [])
        slabdma(1, pe_guard, [])
        for ci in range(NCT):
            kind, cti, dst, idx = ct_info(ci)
            bank, ps = "PA%d" % (ci % 2), PA[ci % 2]
            hbank = "PY%d" % (ci % 2)
            tk = None
            for k in range(NKD):
                tk = mm(bank, ps[:, :], inslab[ci % 2][:, k * 128:(k + 1) * 128],
                        xnb[:, k * TR:(k + 1) * TR], k == 0, k == NKD - 1,
                        [tslab[ci], txn], sPE if k == NKD - 1 else None)
                if kind != "dt":
                    hinc = sPE if k == NKD - 1 else None
                    thl = mm(hbank, PY[ci % 2][:, 0:3],
                             inslab[ci % 2][:, k * 128:(k + 1) * 128],
                             xnhalo[:, k * 3:(k + 1) * 3], k == 0, k == NKD - 1, [], hinc)
            tmmend[ci] = tk
            if kind == "dt":
                def mkz(ci=ci):
                    def go(v):
                        return v.tensor_scalar(dtsb[:, :], PA[ci % 2][:, :], dtbc[:, 0:1],
                                               None, ALU.add)
                    return go
                tz = E("vector", mkz(), [tmmend[ci]], sDV)
                consume(bank, tz)

                def mkabs(sc):
                    return sc.activation(negcs[:, :], dtsb[:, :], ACT.Abs)

                def mkexp(sc):
                    return sc.activation(negcs[:, :], negcs[:, :], ACT.Exp, scale=-1.0)

                def mkln(sc):
                    return sc.activation(negcs[:, :], negcs[:, :], ACT.Ln, bias=1.0)
                ta1 = E("scalar", mkabs, [tz], sAC)
                ta2 = E("scalar", mkexp, [ta1], sAC)
                ta3 = E("scalar", mkln, [ta2], sAC)

                def mkfin(v):
                    v.tensor_scalar(dtsb[:, :], dtsb[:, :], 0.0, None, ALU.max)
                    v.tensor_copy(gapb[0:1, 0:1], gapb[0:1, 1:2])
                    v.tensor_add(dtsb[:, :], dtsb[:, :], negcs[:, :])
                    v.tensor_copy(gapb[0:1, 0:1], gapb[0:1, 1:2])
                    return v.tensor_scalar(dtsb[:, :], dtsb[:, :], DT_MIN, DT_MAX,
                                           ALU.max, ALU.min)
                tevac[ci] = E("vector", mkfin, [ta3], sDV)
            else:
                wr = [tmmend[ci]] + ([tconvend[ci - 2]] if ci >= 2 else [])

                def mkraw(ci=ci):
                    def go(v):
                        return v.tensor_copy(rawsl[ci % 2][:, 3:3 + 512], PA[ci % 2][:, :])
                    return go
                tr0 = E("vector", mkraw(), wr, sDV)
                consume(bank, tr0)

                def mkrh(ci=ci):
                    def go(v):
                        return v.tensor_copy(rawsl[ci % 2][:, 0:3], PY[ci % 2][:, 0:3])
                    return go
                tr1 = E("vector", mkrh(), [thl], sDV)
                consume(hbank, tr1)
                cbank = "PC%d" % (ci % 2)
                tc = None
                for j in range(KTAP):
                    tc = mm(cbank, PC[ci % 2][:, :], convdsl[ci % 2][:, j * 128:(j + 1) * 128],
                            rawsl[ci % 2][:, j:j + 512], j == 0, j == KTAP - 1,
                            [tr0, tr1, tcdsl[ci]], sPE if j == KTAP - 1 else None)
                tconvend[ci] = tc
                if debug and ci in (17, 79):
                    sdma(dbg["raw"][0 if ci == 17 else 1, :, 0:515], rawsl[ci % 2][:, 0:515],
                         [tr0, tr1])

                def mksilu(ci=ci, dst=dst, idx=idx, cti=cti):
                    def go(sc):
                        return sc.activation(dst[:, idx * TR:(idx + 1) * TR], PC[ci % 2][:, :],
                                             ACT.Silu, bias=convbb[:, cti - 64:cti - 63])
                    return go
                tevac[ci] = E("scalar", mksilu(), [tc], sAC)
                consume(cbank, tevac[ci])
            if ci + 2 < NCT:
                slabdma(ci + 2, [tmmend[ci]], [tconvend[ci]] if tconvend[ci] else [])
        tph1ac = p.tok(sAC)
        tdtok = tevac[0]

        if debug:
            for k in range(NXT):
                sdma(dbg["xc"][r, k * 128:(k + 1) * 128, :], xfm[:, k * TR:(k + 1) * TR],
                     [tevac[17 + k]])
            for g in range(8):
                sdma(dbg["bc"][r, g * 128:(g + 1) * 128, :], bfmb[:, g * TR:(g + 1) * TR],
                     [tevac[1 + g]])
                sdma(dbg["bc"][r, 1024 + g * 128:1024 + (g + 1) * 128, :],
                     cfmb[:, g * TR:(g + 1) * TR], [tevac[9 + g]])
            sdma(dbg["dt"][r, :, :], dtsb[:, :], [tdtok])
        if stage < 3:
            return

        twl = [p.tok(sDW), tmmend[NCT - 1], tconvend[NCT - 1]]
        tRinit = E("gpsimd", lambda g: g.memset(rbuf[:, :], 0.0), twl, sGP)
        ptok = {}

        for c in range(NCHUNK):
            c0 = c * CHL

            def mkda(c=c):
                def go(v):
                    return v.tensor_scalar(dasb[:, :], dtsb[:, c * CHL:(c + 1) * CHL],
                                           acol[:, 0:1], None, ALU.mult)
                return go
            tda = E("vector", mkda(), [tdtok], sDV)
            ttda = tpose("PC0", PC[0][:, 0:128], dasb[:, :], ident32[:, :], [tda])
            tdat = vcopy(datsb[:, c * 128:(c + 1) * 128], PC[0][:, 0:128], [ttda])
            consume("PC0", tdat)
            tcst = None
            for cc in range(c + 1):
                tcst = mm("PC1", PC[1][:, 0:128], (lt1 if cc == c else on128)[:, :],
                          datsb[:, cc * 128:(cc + 1) * 128], cc == 0, cc == c,
                          [tdat], sPE if cc == c else None)

            def mkcst(v):
                v.tensor_copy(cstsb[:, :], PC[1][:, 0:128])
                return v.tensor_scalar(ncstsb[:, :], PC[1][:, 0:128], -1.0, None, ALU.mult)
            tcste = E("vector", mkcst, [tcst], sDV)
            consume("PC1", tcste)
            tcs1 = tpose("PC0", PC[0][:, 0:128], cstsb[:, :], ident32[:, :], [tcste])

            def mkcs(c=c):
                def go(v):
                    v.tensor_copy(csb[:, c * CHL:(c + 1) * CHL], PC[0][:, 0:128])
                    return v.tensor_scalar(negcs[:, c * CHL:(c + 1) * CHL], PC[0][:, 0:128],
                                           -1.0, None, ALU.mult)
                return go
            tcse = E("vector", mkcs(), [tcs1], sDV)
            consume("PC0", tcse)

            tdt1 = tpose("PC1", PC[1][:, 0:128], dtsb[:, c * CHL:(c + 1) * CHL], ident32[:, :],
                         [tdtok, tcse])
            tdtt = vcopy(dttsb[:, :], PC[1][:, 0:128], [tdt1])
            consume("PC1", tdtt)

            def mkdec(c=c):
                def go(sc):
                    return sc.activation(dechsb[:, :], csb[:, c * CHL:(c + 1) * CHL], ACT.Exp,
                                         bias=csb[:, (c + 1) * CHL - 1:(c + 1) * CHL], scale=-1.0)
                return go
            tdec = E("scalar", mkdec(), [tcse], sAC)
            tdec1 = tpose("PC0", PC[0][:, 0:128], dechsb[:, :], ident32[:, :], [tdec])
            tdect = vcopy(dectsb[:, :], PC[0][:, 0:128], [tdec1])
            consume("PC0", tdect)

            def mkexpl(c=c):
                def go(sc):
                    if c == 0:
                        return sc.activation(dechsb[:, :], csb[:, 0:CHL], ACT.Exp)
                    return sc.activation(dechsb[:, :], csb[:, c * CHL:(c + 1) * CHL], ACT.Exp,
                                         bias=negcs[:, c * CHL - 1:c * CHL])
                return go
            texl = E("scalar", mkexpl(), [tcse, tdec1], sAC)
            texl1 = tpose("PC1", PC[1][:, 0:128], dechsb[:, :], ident32[:, :], [texl])
            texlt = vcopy(expltsb[:, :], PC[1][:, 0:128], [texl1])
            consume("PC1", texlt)

            def mkexpg(c=c):
                def go(sc):
                    return sc.activation(dechsb[:, :], csb[:, c * CHL:(c + 1) * CHL], ACT.Exp)
                return go
            texg = E("scalar", mkexpg(), [texl1], sAC)
            texg1 = tpose("PC0", PC[0][:, 0:128], dechsb[:, :], ident32[:, :], [texg])
            texgt = vcopy(expgtsb[:, c * 128:(c + 1) * 128], PC[0][:, 0:128], [texg1])
            consume("PC0", texgt)

            def mkcd(c=c):
                def go(sc):
                    if c == 0:
                        return sc.activation(statc[:, 10:11], csb[:, CHL - 1:CHL], ACT.Exp)
                    return sc.activation(statc[:, 10:11], csb[:, (c + 1) * CHL - 1:(c + 1) * CHL],
                                         ACT.Exp, bias=negcs[:, c * CHL - 1:c * CHL])
                return go
            tcd = E("scalar", mkcd(), [tcse], sAC)
            tcd1 = tpose("PC1", PC[1][0:1, 0:128], statc[:, 10:11], ident32[:, :], [tcd])
            tcdr = vcopy(cdrowb[0:1, :], PC[1][0:1, 0:128], [tcd1])
            consume("PC1", tcdr)
            tq1 = mm("PC0", PC[0][:, 0:128], on1[0:1, :], cdrowb[0:1, :], True, True, [tcdr], sPE)
            tq = vcopy(qsb[:, :], PC[0][:, 0:128], [tq1])
            consume("PC0", tq)

            for pr in range(NXT):
                pi = c * NXT + pr
                g = pr // 8
                par = pi % 2
                tks = {}
                ptok[pi] = tks
                prev2 = ptok.get(pi - 2, {})
                prev4 = ptok.get(pi - 4, {})

                if pr % 8 == 0:
                    gb = "PC%d" % (g % 2)
                    tsc1 = mm(gb, PC[g % 2][:, 128:256], bfmb[:, g * TR + c0:g * TR + c0 + CHL],
                              cfmb[:, g * TR + c0:g * TR + c0 + CHL], True, True,
                              [tevac[1 + g], tevac[9 + g]], sPE)

                    def mkscm(g=g):
                        def go(v):
                            return v.tensor_tensor(scmsb[g % 2][:, :], PC[g % 2][:, 128:256],
                                                   utm[:, :], ALU.mult)
                        return go
                    tscm_g = E("vector", mkscm(), [tsc1], sDV)
                    consume(gb, tscm_g)
                    tb1 = tpose("PT%d" % (g % 2), PT[g % 2][:, 512:640],
                                bfmb[:, g * TR + c0:g * TR + c0 + CHL], identbf[:, :],
                                [tevac[1 + g]])
                    tbt_g = vcopy(btsb[g % 2][:, :], PT[g % 2][:, 512:640], [tb1])
                    consume("PT%d" % (g % 2), tbt_g)
                    if g == 0 or True:
                        pass
                    grp_tok = (tscm_g, tbt_g)
                tscm_g, tbt_g = grp_tok

                xbank = "PT%d" % (pi % 2)
                xslot = PT[pi % 2][:, 128 * ((pi // 2) % 2):128 * ((pi // 2) % 2) + 128]
                ttx = tpose(xbank, xslot, xfm[:, pr * TR + c0:pr * TR + c0 + CHL], identbf[:, :],
                            [tevac[17 + pr]])

                def mkxdt(pi=pi, xslot=xslot, pr=pr):
                    def go(v):
                        v.tensor_scalar(xdttsb[pi % 4][:, 0:64], xslot[:, 0:64],
                                        dttsb[:, 2 * pr:2 * pr + 1], None, ALU.mult)
                        return v.tensor_scalar(xdttsb[pi % 4][:, 64:128], xslot[:, 64:128],
                                               dttsb[:, 2 * pr + 1:2 * pr + 2], None, ALU.mult)
                    return go
                txdt = E("vector", mkxdt(), [ttx, tdtt, prev4.get("states")], sDV)
                consume(xbank, txdt)

                tscl = [None, None]
                for hi in range(2):
                    h = 2 * pr + hi
                    m = (2 * pi + hi) % 4
                    rb = "PC%d" % ((2 * pi + hi) % 2)
                    rbp = PC[(2 * pi + hi) % 2]

                    def mkbc(h=h, m=m):
                        def go(sc):
                            return sc.activation(bcastsb[m][:, :], zero128[:, :], ACT.Identity,
                                                 bias=cstsb[:, h:h + 1], scale=0.0)
                        return go
                    tbc = E("scalar", mkbc(), [tcste, prev2.get("rep%d" % hi)], sAC)
                    trep = tpose(rb, rbp[:, 256:384], bcastsb[m][:, :], ident32[:, :], [tbc])
                    tks["rep%d" % hi] = trep

                    def mkseg(h=h, m=m, rbp=rbp):
                        def go(v):
                            return v.tensor_scalar(segsb[m][:, :], rbp[:, 256:384],
                                                   ncstsb[:, h:h + 1], 0.0, ALU.add, ALU.min)
                        return go
                    tseg = E("vector", mkseg(), [trep, prev2.get("lm%d" % hi)], sDV)
                    consume(rb, tseg)

                    def mklm(m=m):
                        def go(sc):
                            return sc.activation(lmsb[m][:, :], segsb[m][:, :], ACT.Exp)
                        return go
                    tlm = E("scalar", mklm(), [tseg, prev2.get("scl%d" % hi)], sAC)
                    tks["lm%d" % hi] = tlm

                    def mkscl(m=m, g=g):
                        def go(v):
                            return v.tensor_mul(sclsb[m][:, :], lmsb[m][:, :], scmsb[g % 2][:, :])
                        return go
                    tscl[hi] = E("vector", mkscl(), [tlm, tscm_g, prev2.get("ydiagend")], sDV)
                    tks["scl%d" % hi] = tscl[hi]

                ybank = "PY%d" % (pi % 2)
                yps = PY[pi % 2]
                m0, m1 = (2 * pi) % 4, (2 * pi + 1) % 4
                ty1 = mm(ybank, yps[:, 0:64], sclsb[m0][:, :], xdttsb[pi % 4][:, 0:64],
                         True, False, [txdt, tscl[0], prev2.get("ycomb")])
                ty2 = mm(ybank, yps[:, 64:128], sclsb[m1][:, :], xdttsb[pi % 4][:, 64:128],
                         False, True, [tscl[1]], sPE)
                tks["ydiagend"] = ty2

                tbd = [None, None]
                for hi in range(2):
                    h = 2 * pr + hi

                    def mkbdt(h=h, m=(2 * pi + hi) % 4, g=g):
                        def go(v):
                            return v.tensor_scalar(bdtsb[m][:, :], btsb[g % 2][:, :],
                                                   dectsb[:, h:h + 1], None, ALU.mult)
                        return go
                    tbd[hi] = E("vector", mkbdt(), [tbt_g, tdect, prev2.get("states")], sDV)
                sbank = "PA%d" % par
                sps = PA[par]
                tst1 = mm(sbank, sps[:, 0:64], bdtsb[m0][:, :], xdttsb[pi % 4][:, 0:64],
                          True, False, [tbd[0], tbd[1], txdt, tscl[1]])
                tst2 = mm(sbank, sps[:, 64:128], bdtsb[m1][:, :], xdttsb[pi % 4][:, 64:128],
                          False, True, [], sPE)
                tks["states"] = tst2

                if c > 0:
                    trcp = E("vector", (lambda pr=pr, pi=pi: lambda v: v.tensor_copy(
                        rcpsb[pi % 2][:, :], rbuf[:, pr * 128:(pr + 1) * 128]))(),
                        [prev2.get("yoffmm"), ptok.get(pi - 64, {}).get("walk")], sDV)
                    tyo1 = mm(sbank, sps[:, 128:256], cfmb[:, g * TR + c0:g * TR + c0 + CHL],
                              rcpsb[pi % 2][:, :], True, True, [trcp, tevac[9 + g]], sPE)
                    tks["yoffmm"] = tyo1

                    def mkyoe(pi=pi, pr=pr, sps=sps):
                        def go(sc):
                            sc.activation(yofsb[pi % 2][:, 0:64], sps[:, 128:192], ACT.Identity,
                                          scale=expltsb[:, 2 * pr:2 * pr + 1])
                            return sc.activation(yofsb[pi % 2][:, 64:128], sps[:, 192:256],
                                                 ACT.Identity,
                                                 scale=expltsb[:, 2 * pr + 1:2 * pr + 2])
                        return go
                    tyoe = E("scalar", mkyoe(), [tyo1, texlt, prev2.get("ycomb")], sAC)
                    consume(sbank, tyoe)

                def mkwalk(pr=pr, par=par, sps=sps):
                    def go(v):
                        v.scalar_tensor_tensor(rbuf[:, pr * 128:pr * 128 + 64],
                                               rbuf[:, pr * 128:pr * 128 + 64],
                                               qsb[:, 2 * pr:2 * pr + 1], sps[:, 0:64],
                                               ALU.mult, ALU.add)
                        return v.scalar_tensor_tensor(rbuf[:, pr * 128 + 64:(pr + 1) * 128],
                                                      rbuf[:, pr * 128 + 64:(pr + 1) * 128],
                                                      qsb[:, 2 * pr + 1:2 * pr + 2],
                                                      sps[:, 64:128], ALU.mult, ALU.add)
                    return go
                wk = [tst2, tq, tRinit]
                if c > 0:
                    wk.append(tks["yoffmm"])
                    wk.append(tyoe)
                twalk = E("vector", mkwalk(), wk, sDV)
                consume(sbank, twalk)
                tks["walk"] = twalk

                if c > 0:
                    def mkcomb(pi=pi, yps=yps):
                        def go(v):
                            return v.tensor_add(ycsb[pi % 2][:, :], yps[:, 0:128],
                                                yofsb[pi % 2][:, :])
                        return go
                    tcomb = E("vector", mkcomb(), [ty2, tyoe, prev2.get("ytr")], sDV)
                else:
                    def mkcomb(pi=pi, yps=yps):
                        def go(v):
                            return v.tensor_copy(ycsb[pi % 2][:, :], yps[:, 0:128])
                        return go
                    tcomb = E("vector", mkcomb(), [ty2, prev2.get("ytr")], sDV)
                consume(ybank, tcomb)
                tks["ycomb"] = tcomb
                ytslot = PT[pi % 2][:, 640:768]
                tytr = tpose("PT%d" % (pi % 2), ytslot, ycsb[pi % 2][:, :], identbf[:, :],
                             [tcomb])
                tks["ytr"] = tytr

                def mkyev(pr=pr, pi=pi, c0=c0, ytslot=ytslot):
                    def go(v):
                        return v.scalar_tensor_tensor(
                            xfm[:, pr * TR + c0:pr * TR + c0 + CHL],
                            xfm[:, pr * TR + c0:pr * TR + c0 + CHL],
                            dcolb[:, pr:pr + 1], ytslot, ALU.mult, ALU.add)
                    return go
                tyev = E("vector", mkyev(), [tytr, txdt], sDV)
                consume("PT%d" % (pi % 2), tyev)
                tks["yevac"] = tyev

        tlastwalk = ptok[NCHUNK * NXT - 1]["walk"]
        tpay = sdma(dagin[r][:, 0:INTER], rbuf[:, :], [tlastwalk])

        def mkdtot(sc):
            return sc.activation(statc[:, 11:12], csb[:, TR - 1:TR], ACT.Exp)
        tdt_ = E("scalar", mkdtot, [p.tok(sDV)], sAC)
        tdr1 = tpose("PC1", PC[1][0:1, 0:128], statc[:, 11:12], ident32[:, :], [tdt_])
        tdrow = vcopy(drowb[0:1, :], PC[1][0:1, 0:128], [tdr1])
        consume("PC1", tdrow)
        tpz = sdma(dagin[r][:, INTER:INTER + 128], zero128[:, :], [tconst])
        tpay2 = sdma(dagin[r][0:1, INTER:INTER + 128], drowb[0:1, :], [tdrow, tpz])

        tdump = []
        if debug and stage == 3:
            for k in range(NXT):
                sdma(dbg["ya"][r, k * 128:(k + 1) * 128, :], xfm[:, k * TR:(k + 1) * TR],
                     [ptok[(NCHUNK - 1) * NXT + k]["yevac"]])
            tdump.append(sdma(dbg["pay"][r, :, 0:INTER], rbuf[:, :], [tlastwalk]))
            sdma(dbg["pay"][r, 0:1, INTER:INTER + 128], drowb[0:1, :], [tdrow])
            sdma(dbg["cs"][r, :, :], csb[:, :], [p.tok(sDV)])
        if stage < 4:
            return None

        tag = E("gpsimd", (lambda r=r: lambda g: g.collective_compute(
            "AllGather", ALU.bypass, replica_groups=[[0, 1, 2, 3], [4, 5, 6, 7]],
            ins=[dagin[r].ap().opt()], outs=[dagout[r].ap().opt()]))(),
            [tpay, tpay2], sGP)

        for ts_ in range(4):
            te_mm = None
            for k in range(NKD):
                te_mm = mm("PC%d" % (ts_ % 2), PC[ts_ % 2][:, 0:40],
                           xnb[:, k * TR + ts_ * 128:k * TR + (ts_ + 1) * 128],
                           enctb[:, k * 40:(k + 1) * 40], k == 0, k == NKD - 1,
                           [txn], sPE if k == NKD - 1 else None)
            tecp = vcopy(ebsb[:, :], PC[ts_ % 2][:, 0:40], [te_mm])
            consume("PC%d" % (ts_ % 2), tecp)
            tet = tpose("PC%d" % ((ts_ + 1) % 2), PC[(ts_ + 1) % 2][0:40, 0:128],
                        ebsb[:, :], ident32[:, :], [tecp])
            teev = vcopy(etall[0:40, ts_ * 128:(ts_ + 1) * 128],
                         PC[(ts_ + 1) % 2][0:40, 0:128], [tet])
            consume("PC%d" % ((ts_ + 1) % 2), teev)
        tet_done = p.tok(sDV)

        def combine_pass2(masked, seed_from_dram, out_extra_wait):
            jrange = list(range(3)) if masked else list(range(4))
            tqd = [None] * 4
            if seed_from_dram:
                tseed = sdma(ainbuf[:, :], dst2048[:, :], list(out_extra_wait) + [tag])
            else:
                tseed = E("gpsimd", (lambda: lambda g: g.memset(ainbuf[:, :], 0.0))(),
                          list(out_extra_wait) + [tag], sGP)
            for j in jrange:
                tdr = sdma(drowb[0:1, :], dagout[r][128 * j:128 * j + 1, INTER:INTER + 128],
                           [tag] + ([tqd[j - 2]] if j >= 2 else []))
                tqm = mm("PC%d" % (j % 2), PC[j % 2][:, 0:128], on1[0:1, :], drowb[0:1, :],
                         True, True, [tdr], sPE)

                def mkqd(j=j, masked=masked):
                    def go(v):
                        if masked:
                            return v.tensor_scalar(qdsb[j % 2][:, :], PC[j % 2][:, 0:128],
                                                   selmb[:, j:j + 1], selmb[:, 4 + j:5 + j],
                                                   ALU.mult, ALU.add)
                        return v.tensor_copy(qdsb[j % 2][:, :], PC[j % 2][:, 0:128])
                    return go
                tqd[j] = E("vector", mkqd(), [tqm], sDV)
                consume("PC%d" % (j % 2), tqd[j])
            tlast = None
            tprevq = {}
            for quarter in range(4):
                for ji, j in enumerate(jrange):
                    qi = (quarter * len(jrange) + ji) % 2
                    tsj_new = sdma(sjbuf[qi][:, :],
                                   dagout[r][128 * j:128 * (j + 1),
                                             quarter * 2048:(quarter + 1) * 2048],
                                   [tag] + ([tprevq[qi]] if qi in tprevq else []))
                    tscl_s = None
                    if masked:
                        def mksm(j=j, qi=qi):
                            def go(v):
                                return v.tensor_scalar(sjbuf[qi][:, :], sjbuf[qi][:, :],
                                                       selmb[:, j:j + 1], None, ALU.mult)
                            return go
                        tscl_s = E("vector", mksm(), [tsj_new], sDV)
                    for hb in range(32):
                        h = quarter * 32 + hb

                        def mkc(j=j, qi=qi, h=h, hb=hb):
                            def go(v):
                                return v.scalar_tensor_tensor(
                                    ainbuf[:, h * 64:(h + 1) * 64],
                                    ainbuf[:, h * 64:(h + 1) * 64],
                                    qdsb[j % 2][:, h:h + 1],
                                    sjbuf[qi][:, hb * 64:(hb + 1) * 64],
                                    ALU.mult, ALU.add)
                            return go
                        tlast = E("vector", mkc(), [tsj_new, tscl_s, tqd[j], tseed], sDV)
                    tprevq[qi] = tlast
            return tlast

        textra = list(tdump)
        if r == 0:
            tfull = combine_pass2(False, False, textra + [tpay])
            tst_dma = sdma(dst2048[:, :], ainbuf[:, :], [tfull])
            tain = combine_pass2(True, False, [tst_dma])
        else:
            tain = combine_pass2(True, True, textra + [tpay])

        btok = {}
        for c in range(NCHUNK):
            c0 = c * CHL
            for pr in range(NXT):
                pib = c * NXT + pr
                g = pr // 8
                par = pib % 2
                prevb = btok.get(pib - 2, {})
                tks2 = {}
                btok[pib] = tks2

                def mkacp(pr=pr, pib=pib):
                    def go(v):
                        return v.tensor_copy(rcpsb[pib % 2][:, :],
                                             rbuf[:, pr * 128:(pr + 1) * 128])
                    return go
                tacp = E("vector", mkacp(), [tain, prevb.get("bmm")], sDV)
                tbmm = mm("PA%d" % par, PA[par][:, 128:256],
                          cfmb[:, g * TR + c0:g * TR + c0 + CHL],
                          rcpsb[pib % 2][:, :], True, True,
                          [tacp, tevac[9 + g]], sPE)
                tks2["bmm"] = tbmm

                def mkbsc(pib=pib, pr=pr, par=par, c=c):
                    def go(v):
                        v.tensor_scalar(yofsb[pib % 2][:, 0:64], PA[par][:, 128:192],
                                        expgtsb[:, c * 128 + 2 * pr:c * 128 + 2 * pr + 1],
                                        None, ALU.mult)
                        return v.tensor_scalar(yofsb[pib % 2][:, 64:128], PA[par][:, 192:256],
                                               expgtsb[:, c * 128 + 2 * pr + 1:c * 128 + 2 * pr + 2],
                                               None, ALU.mult)
                    return go
                tbsc = E("vector", mkbsc(), [tbmm, prevb.get("btr")], sDV)
                consume("PA%d" % par, tbsc)
                tbtr = tpose("PY%d" % par, PY[par][:, 0:128], yofsb[pib % 2][:, :],
                             ident32[:, :], [tbsc])
                tks2["btr"] = tbtr

                def mkbadd(pr=pr, c0=c0, par=par):
                    def go(v):
                        return v.tensor_add(xfm[:, pr * TR + c0:pr * TR + c0 + CHL],
                                            xfm[:, pr * TR + c0:pr * TR + c0 + CHL],
                                            PY[par][:, 0:128])
                    return go
                tbadd = E("vector", mkbadd(),
                          [tbtr, ptok[pib]["yevac"]], sDV)
                consume("PY%d" % par, tbadd)
                tks2["badd"] = tbadd
        tphB = p.tok(sDV)

        if debug:
            for k in range(NXT):
                sdma(dbg["yb"][r, k * 128:(k + 1) * 128, :], xfm[:, k * TR:(k + 1) * TR],
                     [btok[(NCHUNK - 1) * NXT + k]["badd"]])
        if stage < 5:
            return None

        tgsl = [None] * NXT
        tgmm = [None] * NXT
        tgate = [None] * NXT
        tsq = [None] * NXT
        tnmm = [None] * NXT

        def gslab(kk, wl):
            tgsl[kk] = gdma(inslab[kk % 2][:, :], dwti[kk, :, :], wl)
        gslab(0, [tphB])
        gslab(1, [tphB])
        for kk in range(NXT):
            bank, ps = "PA%d" % (kk % 2), PA[kk % 2]
            tk = None
            for k in range(NKD):
                tk = mm(bank, ps[:, :], inslab[kk % 2][:, k * 128:(k + 1) * 128],
                        xnb[:, k * TR:(k + 1) * TR], k == 0, k == NKD - 1,
                        [tgsl[kk]], sPE if k == NKD - 1 else None)
            tgmm[kk] = tk

            def mkgs(kk=kk):
                def go(sc):
                    return sc.activation(gatesb[kk % 2][:, :], PA[kk % 2][:, :], ACT.Silu)
                return go
            wg = [tk] + ([tgate[kk - 2]] if kk >= 2 else [])
            tsilu = E("scalar", mkgs(), wg, sAC)
            consume(bank, tsilu)

            def mkgm(kk=kk):
                def go(v):
                    return v.tensor_mul(xfm[:, kk * TR:(kk + 1) * TR],
                                        xfm[:, kk * TR:(kk + 1) * TR],
                                        gatesb[kk % 2][:, :])
                return go
            tgate[kk] = E("vector", mkgm(), [tsilu, btok[3 * NXT + kk]["badd"]], sDV)

            def mksqg(kk=kk):
                def go(sc):
                    return sc.activation(ygsqsb[kk % 2][:, :], xfm[:, kk * TR:(kk + 1) * TR],
                                         ACT.Square)
                return go
            ws = [tgate[kk]] + ([tnmm[kk - 2]] if kk >= 2 else [])
            tsq[kk] = E("scalar", mksqg(), ws, sAC)
            tnmm[kk] = mm("PY0", PY[0][0:8, 0:512], gcolb[:, (kk // 8) * 8:(kk // 8) * 8 + 8],
                          ygsqsb[kk % 2][:, :], kk == 0, kk == NXT - 1, [tsq[kk]],
                          sPE if kk == NXT - 1 else None)
            if kk + 2 < NXT:
                gslab(kk + 2, [tgmm[kk]])

        def mkms(v):
            return v.tensor_scalar(rstd8[:, :], PY[0][0:8, 0:512], 1.0 / 1024, EPS,
                                   ALU.mult, ALU.add)
        tms = E("vector", mkms, [tnmm[NXT - 1]], sDV)
        consume("PY0", tms)
        tsq8 = E("scalar", lambda sc: sc.activation(rstd8[:, :], rstd8[:, :], ACT.Sqrt),
                 [tms], sAC)

        def mkrec(v):
            return v.reciprocal(rstd8[:, :], rstd8[:, :])
        trec = E("vector", mkrec, [tsq8], sDV)

        tscale = [None] * NXT
        tgrep = None
        for gg in range(8):
            trow = sdma(rstdrow[0:1, :], rstd8[gg:gg + 1, :],
                        [trec] + ([tgrep] if tgrep else []))
            tgrep_mm = mm("PC0", PC[0][:, 0:512], on1[0:1, :], rstdrow[0:1, :], True, True,
                          [trow], sPE)
            for kk in range(gg * 8, gg * 8 + 8):
                def mksc2(kk=kk):
                    def go(v):
                        return v.scalar_tensor_tensor(xfm[:, kk * TR:(kk + 1) * TR],
                                                      xfm[:, kk * TR:(kk + 1) * TR],
                                                      nwcolb[:, kk:kk + 1],
                                                      PC[0][:, 0:512], ALU.mult, ALU.mult)
                    return go
                tscale[kk] = E("vector", mksc2(), [tgrep_mm, tsq[kk]], sDV)
                consume("PC0", tscale[kk])
            tgrep = tscale[gg * 8 + 7]

        if debug:
            for k in range(NXT):
                sdma(dbg["ygn"][r, k * 128:(k + 1) * 128, :], xfm[:, k * TR:(k + 1) * TR],
                     [tscale[k]])
        if stage < 6:
            return None

        twsl = [None] * (2 * NKD)
        twb = [None] * NKD
        tdmm = [None] * NKD
        tdev = [None] * NKD
        txk = {}
        tflush = {}
        tadds = {}

        def wdma(dtl, half, wl):
            twsl[2 * dtl + half] = gdma(wotsl[half][:, :],
                                        dwto[dtl, :, half * 4096:(half + 1) * 4096], wl)

        def wbdma(dtl, wl):
            twb[dtl] = gdma(wbtsl[dtl % 2][:, :], dwbt[dtl, :, :], wl)

        wdma(0, 0, [tgmm[NXT - 2]])
        wdma(0, 1, [tgmm[NXT - 1]])
        wbdma(0, [tphB])
        wbdma(1, [tphB])

        def xtdma(dg, ts_, wl):
            txk[(dg, ts_)] = sdma(xtok[ts_][:, :],
                                  xdram[3 + ts_ * 128:3 + (ts_ + 1) * 128,
                                        dg * 512:(dg + 1) * 512], wl)
        for ts_ in range(4):
            xtdma(0, ts_, [tphB])

        for dtl in range(NKD):
            bank, ps = "PA%d" % (dtl % 2), PA[dtl % 2]
            dg = dtl // 4
            tk = None
            for k in range(NKD):
                tk = mm(bank, ps[:, :], wbtsl[dtl % 2][:, k * 128:(k + 1) * 128],
                        xnb[:, k * TR:(k + 1) * TR], k == 0, False,
                        [twb[dtl]], None)
            tk = mm(bank, ps[:, :], vtb[0:40, dtl * 128:(dtl + 1) * 128], etall[0:40, :],
                    False, False, [tet_done], None)
            for half in range(2):
                for k2 in range(NKD):
                    kk = half * NKD + k2
                    last = (half == 1 and k2 == NKD - 1)
                    inc2 = sPE if (last or k2 == NKD - 1) else None
                    tk = mm(bank, ps[:, :],
                            wotsl[half][:, k2 * 128:(k2 + 1) * 128],
                            xfm[:, kk * TR:(kk + 1) * TR], False, last,
                            [twsl[2 * dtl + half], tscale[kk]], inc2)
                if half == 0 and dtl + 1 < NKD:
                    wdma(dtl + 1, 0, [tk])
            tdmm[dtl] = tk
            if dtl + 2 < NKD:
                wbdma(dtl + 2, [tdmm[dtl]])
            if dtl + 1 < NKD:
                wdma(dtl + 1, 1, [tdmm[dtl]])

            def mkoev(dtl=dtl, par=dtl % 2):
                def go(v):
                    return v.tensor_scalar(outsb[dtl % 2][:, :], PA[par][:, :],
                                           bbcolb[:, dtl:dtl + 1], None, ALU.add)
                return go
            wv = [tdmm[dtl]] + ([tdev[dtl - 2]] if dtl >= 2 else [])
            tev_ = E("vector", mkoev(), wv, sDV)
            consume(bank, tev_)
            for ts_ in range(4):
                tb = "PY%d" % (ts_ % 2)
                tt_o = tpose(tb, PY[ts_ % 2][:, 128:256], outsb[dtl % 2][:, ts_ * 128:(ts_ + 1) * 128],
                             ident32[:, :], [tev_])

                def mkres(dtl=dtl, ts_=ts_, dg=dg):
                    def go(v):
                        return v.tensor_add(outst[ts_][:, (dtl % 4) * 128:(dtl % 4 + 1) * 128],
                                            PY[ts_ % 2][:, 128:256],
                                            xtok[ts_][:, (dtl % 4) * 128:(dtl % 4 + 1) * 128])
                    return go
                wl3 = [tt_o, txk[(dg, ts_)]]
                if (dg - 1, ts_) in tflush:
                    wl3.append(tflush[(dg - 1, ts_)])
                tadds[(dtl, ts_)] = E("vector", mkres(), wl3, sDV)
                consume(tb, tadds[(dtl, ts_)])
            tdev[dtl] = tadds[(dtl, 3)]
            if dtl % 4 == 3:
                for ts_ in range(4):
                    tflush[(dg, ts_)] = sdma(
                        dout[r * TR + ts_ * 128:r * TR + (ts_ + 1) * 128,
                             dg * 512:(dg + 1) * 512],
                        outst[ts_][:, :], [tadds[(dtl, ts_)]])
                if dg + 1 < 8:
                    for ts_ in range(4):
                        xtdma(dg + 1, ts_, [tadds[(dtl, ts_)]])
        tend_pe = p.tok(sPE)
        tend_dv = p.tok(sDV)
        tend_dx = p.tok(sDX)
        tend_dw = p.tok(sDW)
        tend_ac = p.tok(sAC)
        return (tend_pe, tend_dv, tend_dx, tend_dw, tend_ac)

    prev_end = None
    for r in range(2 if stage >= 6 else 1):
        prev_end = round_r(r, prev_end)

    E("sync", lambda s_: s_.nop(), [p.tok(sDX)])
    E("gpsimd", lambda g: g.nop(), [p.tok(sDW)])

    def gapfn(e):
        e.tensor_copy(gapb[0:1, 0:1], gapb[0:1, 1:2])

    with nc.Block() as block:
        p.replay(block, gapfn=gapfn)
    st.close()
    return nc


def kernel(**inputs):
    sh = host_shared(inputs)
    nc = build()
    in_maps = []
    for core in range(8):
        m = dict(sh)
        m.update(host_percore(inputs, core))
        in_maps.append(m)
    res = run_bass_kernel_spmd(nc, in_maps, core_ids=list(range(8)))
    out = np.empty((B, S, D), np.float32)
    for core in range(8):
        b, q = core // 4, core % 4
        o = res.results[core]["out"]
        out[b, 512 * q:512 * q + 512] = o[0:512]
        out[b, 2048 + 512 * q:2048 + 512 * q + 512] = o[512:1024]
    return out


# version 03 (final)
# speedup vs baseline: 1.0435x; 1.0015x over previous
import math
import contextlib
import numpy as np
import ml_dtypes
import concourse.bass as bass
import concourse.mybir as mybir
from concourse.bass_utils import run_bass_kernel_spmd

F32 = mybir.dt.float32
BF16 = mybir.dt.bfloat16
U8 = mybir.dt.uint8
ALU = mybir.AluOpType
ACT = mybir.ActivationFunctionType
AXL = mybir.AxisListType

B, S, D = 2, 4096, 4096
NH, HD, NG, KTAP = 128, 64, 8, 4
CHL = 128
INTER = NH * HD
CONVD = INTER + 2 * NG * 128
PROJ = INTER + CONVD + NH
TR = 512
NCHUNK = TR // CHL
NKD = D // 128
NXT = INTER // 128
DT_MIN, DT_MAX, EPS = 1e-3, 100.0, 1e-5
SCALING = 16.0 / 32.0
NCT = 81

STAGE = 6
DEBUG = False


class Pl:
    ENGINES = ("sync", "gpsimd", "scalar", "vector", "tensor")

    def __init__(self, nc):
        self.nc = nc
        self.items = {e: [] for e in self.ENGINES}
        self.cnt = {}
        self.hand = {}
        self.hiw = {}

    def sem(self, stack, name):
        self.hand[name] = stack.enter_context(self.nc.semaphore(name))
        self.cnt[name] = 0
        return name

    def tok(self, sem):
        return (sem, self.cnt[sem])

    def emit(self, eng, fn, waits=(), inc=None, amt=1):
        tokv = None
        if inc is not None:
            self.cnt[inc] += amt
            tokv = (inc, self.cnt[inc])
        wl = []
        for w in waits:
            if w is None:
                continue
            s, v = w
            if v <= 0 or self.hiw.get((eng, s), 0) >= v:
                continue
            self.hiw[(eng, s)] = v
            wl.append((s, v))
        self.items[eng].append((fn, wl, tokv, amt))
        return tokv

    @staticmethod
    def _spans(args):
        out = []
        for a in args:
            name = getattr(getattr(a, "tensor", None), "name", None)
            if name is None:
                memset = getattr(a, "memsetref", None) or ""
                name = memset[:-4] if memset.endswith("_set") else memset
            ap = getattr(a, "ap", None)
            off = getattr(a, "offset", 0)
            if ap is None:
                continue
            span = 1
            for (st_, cn) in list(ap)[1:]:
                span += abs(st_) * (cn - 1)
            out.append((str(name), off, off + span))
        return out

    def _vgaps(self, gapfn):
        nc = self.nc
        gaps = set()
        intra = []
        with nc.discard():
            e = nc.vector
            recs = []
            for idx, (fn, wl, tokv, amt) in enumerate(self.items["vector"]):
                insts = []
                with _InstCatcher(nc, insts):
                    fn(e)
                for j, ins in enumerate(insts):
                    rspan = self._spans(getattr(ins, "ins", []) or [])
                    wspan = self._spans(getattr(ins, "outs", []) or [])
                    recs.append((idx, j, rspan, wspan))
            for k in range(1, len(recs)):
                pidx, pj, pr, pw = recs[k - 1]
                idx, j, rr, rw = recs[k]
                hit = False
                for (wn, wl_, wh) in pw:
                    if wn.startswith("c_gap"):
                        continue
                    for (rn, rl, rh) in rr:
                        if wn == rn and rl < wh and wl_ < rh:
                            hit = True
                if not hit:
                    continue
                if pidx == idx:
                    intra.append((idx, pj, j))
                else:
                    gaps.add(idx)
        if intra:
            raise AssertionError(f"intra-item vector RAW chains (add gaps inside): {intra[:10]}")
        return gaps

    def replay(self, block, gapfn=None):
        hand = self.hand
        vgaps = self._vgaps(gapfn) if gapfn is not None else set()

        def mk(eng):
            items = self.items[eng]

            def go(e):
                for idx, (fn, wl, tokv, amt) in enumerate(items):
                    if eng == "vector" and idx in vgaps:
                        gapfn(e)
                    for (s, v) in wl:
                        e.wait_ge(hand[s], v)
                    inst = fn(e)
                    if tokv is not None:
                        if inst is None:
                            inst = e.nop()
                        inst.then_inc(hand[tokv[0]], amt)
            return go

        for eng in self.ENGINES:
            getattr(block, eng)(mk(eng))


class _InstCatcher:

    def __init__(self, nc, sink):
        self.nc = nc
        self.sink = sink

    def __enter__(self):
        self.cb = lambda ins: self.sink.append(ins)
        self.nc.push_inst_callback(self.cb)
        return self

    def __exit__(self, *a):
        self.nc.remove_inst_callback(self.cb)
        return False


_HC = {}


def fourier_mt():
    if "mt" not in _HC:
        n = D
        j = np.arange(n, dtype=np.float64)
        Mt = np.empty((n, n), dtype=np.float64)
        Mt[0, :] = 1.0 / math.sqrt(n)
        maxf = (n - 1) // 2
        f = np.arange(1, maxf + 1, dtype=np.float64)
        ang = 2.0 * np.pi * np.outer(f, j) / n
        c = math.sqrt(2.0 / n)
        Mt[1:2 * maxf:2, :] = c * np.cos(ang)
        Mt[2:2 * maxf + 1:2, :] = c * np.sin(ang)
        Mt[n - 1, :] = ((-1.0) ** j) / math.sqrt(n)
        _HC["mt"] = np.ascontiguousarray(Mt.astype(np.float32))
    return _HC["mt"]


def _slab_image(wT, c0):
    a = wT[:, c0:c0 + 128]
    return a.reshape(D // 128, 128, 128).transpose(1, 0, 2).reshape(128, D)


def host_shared(inp):
    i = np.arange(128)
    sh = {}
    wti = np.asarray(inp["L0_attn_in_proj_w"]).T
    sh["wti"] = np.stack([_slab_image(wti, c * 128) for c in range(PROJ // 128)])
    wto = np.asarray(inp["L0_attn_out_proj_w"]).T
    im = []
    for dtl in range(NKD):
        a = wto[:, dtl * 128:(dtl + 1) * 128]
        im.append(a.reshape(64, 128, 128).transpose(1, 0, 2).reshape(128, 8192))
    sh["wto"] = np.stack(im)
    wbt = np.asarray(inp["L0_ffn_W_base"]).T
    sh["wbt"] = np.stack([_slab_image(wbt, c * 128) for c in range(NKD)])
    sh["mt"] = fourier_mt()
    cw = np.asarray(inp["L0_attn_conv_w"]).reshape(CONVD, KTAP)
    convd = np.zeros((CONVD // 128, 128, KTAP, 128), np.float32)
    t = np.arange(CONVD)
    for jj in range(KTAP):
        convd[t // 128, t % 128, jj, t % 128] = cw[:, jj]
    sh["convd"] = convd.reshape(CONVD // 128, 128, KTAP * 128)
    sh["convb"] = np.ascontiguousarray(np.asarray(inp["L0_attn_conv_b"]).reshape(CONVD // 128, 128).T)
    sh["dtb"] = np.asarray(inp["L0_attn_dt_bias"]).reshape(128, 1).astype(np.float32)
    sh["acol"] = (-(i + 1).astype(np.float32)).reshape(128, 1)
    sh["dcol"] = np.ascontiguousarray(np.repeat(np.asarray(inp["L0_attn_D"]), HD).reshape(NXT, 128).T)
    sh["nwcol"] = np.ascontiguousarray(np.asarray(inp["L0_attn_norm_w"]).reshape(NXT, 128).T)
    sh["bbcol"] = np.ascontiguousarray(np.asarray(inp["L0_ffn_b_base"]).reshape(NKD, 128).T)
    sh["enct"] = np.ascontiguousarray(
        np.concatenate([np.asarray(inp["L0_ffn_enc0"]).T, np.asarray(inp["L0_ffn_enc1"]).T], axis=1))
    co = np.concatenate([np.transpose(np.asarray(inp["L0_ffn_coefs0"]), (1, 0, 2)),
                         np.transpose(np.asarray(inp["L0_ffn_coefs1"]), (1, 0, 2))],
                        axis=2).reshape(D, 120)
    sh["coef"] = np.ascontiguousarray(co)
    cm = np.concatenate([np.transpose(np.asarray(inp["L0_ffn_mask0"]), (1, 0, 2)),
                         np.transpose(np.asarray(inp["L0_ffn_mask1"]), (1, 0, 2))],
                        axis=2).reshape(D, 120)
    sh["cmask"] = np.ascontiguousarray(cm.astype(np.uint8))
    sh["ident32"] = np.eye(128, dtype=np.float32)
    sh["identbf"] = np.eye(128, dtype=ml_dtypes.bfloat16)
    sh["lt1"] = (i[:, None] <= i[None, :]).astype(np.float32)
    sh["ones128"] = np.ones((128, 128), np.float32)
    sh["utm"] = (i[:, None] <= i[None, :]).astype(ml_dtypes.bfloat16)
    sh["ones1"] = np.ones((1, 128), np.float32)
    gc = np.zeros((8, 128, 8), dtype=ml_dtypes.bfloat16)
    for g in range(8):
        gc[g, :, g] = 1
    sh["gcol"] = gc
    return sh


def host_percore(inp, core):
    b, q = core // 4, core % 4
    x = np.asarray(inp["L0_attn_hidden_states"])[b]
    pc = {}
    for r in range(2):
        t0 = 2048 * r + 512 * q
        main = x[t0:t0 + TR]
        halo = x[t0 - 3:t0] if t0 >= 3 else np.zeros((3, D), np.float32)
        pc[f"x{r}"] = np.ascontiguousarray(np.concatenate([halo, main], axis=0))
    sel = np.zeros((128, 8), np.float32)
    for j in range(3):
        sel[:, j] = 1.0 if j < q else 0.0
        sel[:, 4 + j] = 0.0 if j < q else 1.0
    pc["selm"] = sel
    return pc


def build(stage=STAGE, debug=DEBUG):
    nc = bass.Bass("TRN2", target_bir_lowering=False, detect_race_conditions=False)
    st = contextlib.ExitStack()

    dxr = [nc.dram_tensor(f"x{r}", [3 + TR, D], F32, kind="ExternalInput") for r in range(2)]
    dwti = nc.dram_tensor("wti", [PROJ // 128, 128, D], F32, kind="ExternalInput")
    dwto = nc.dram_tensor("wto", [NKD, 128, INTER], F32, kind="ExternalInput")
    dwbt = nc.dram_tensor("wbt", [NKD, 128, D], F32, kind="ExternalInput")
    dmt = nc.dram_tensor("mt", [D, D], F32, kind="ExternalInput")
    dconvd = nc.dram_tensor("convd", [CONVD // 128, 128, KTAP * 128], F32, kind="ExternalInput")
    dconvb = nc.dram_tensor("convb", [128, CONVD // 128], F32, kind="ExternalInput")
    ddtb = nc.dram_tensor("dtb", [128, 1], F32, kind="ExternalInput")
    dacol = nc.dram_tensor("acol", [128, 1], F32, kind="ExternalInput")
    ddcol = nc.dram_tensor("dcol", [128, NXT], F32, kind="ExternalInput")
    dnwcol = nc.dram_tensor("nwcol", [128, NXT], F32, kind="ExternalInput")
    dbbcol = nc.dram_tensor("bbcol", [128, NKD], F32, kind="ExternalInput")
    denct = nc.dram_tensor("enct", [D, 40], F32, kind="ExternalInput")
    dcoef = nc.dram_tensor("coef", [D, 120], F32, kind="ExternalInput")
    dcmask = nc.dram_tensor("cmask", [D, 120], U8, kind="ExternalInput")
    dselm = nc.dram_tensor("selm", [128, 8], F32, kind="ExternalInput")
    di32 = nc.dram_tensor("ident32", [128, 128], F32, kind="ExternalInput")
    dibf = nc.dram_tensor("identbf", [128, 128], BF16, kind="ExternalInput")
    dlt1 = nc.dram_tensor("lt1", [128, 128], F32, kind="ExternalInput")
    don128 = nc.dram_tensor("ones128", [128, 128], F32, kind="ExternalInput")
    dutm = nc.dram_tensor("utm", [128, 128], BF16, kind="ExternalInput")
    don1 = nc.dram_tensor("ones1", [1, 128], F32, kind="ExternalInput")
    dgcol = nc.dram_tensor("gcol", [8, 128, 8], BF16, kind="ExternalInput")
    dout = nc.dram_tensor("out", [2 * TR, D], F32, kind="ExternalOutput")
    dagin = [nc.dram_tensor(f"agin{r}", [128, INTER + 128], F32) for r in range(2)]
    dagout = [nc.dram_tensor(f"agout{r}", [512, INTER + 128], F32) for r in range(2)]
    dst2048 = nc.dram_tensor("st2048", [128, INTER], F32)

    dbg = {}
    if debug:
        for nm, shp, dt_ in [("xn", [2, D, TR], BF16), ("xc", [2, INTER, TR], BF16),
                             ("bc", [2, 2048, TR], BF16), ("dt", [2, 128, TR], F32),
                             ("cs", [2, 128, TR], F32), ("ya", [2, INTER, TR], BF16),
                             ("pay", [2, 128, INTER + 128], F32),
                             ("raw", [2, 128, 544], BF16),
                             ("yb", [2, INTER, TR], BF16), ("ygn", [2, INTER, TR], BF16)]:
            dbg[nm] = nc.dram_tensor("dbg_" + nm, shp, dt_, kind="ExternalOutput")

    def sb(name, shape, dtype):
        return nc.alloc_sbuf_tensor(name, shape, dtype)

    ident32 = sb("c_i32", [128, 128], F32)
    identbf = sb("c_ibf", [128, 128], BF16)
    lt1 = sb("c_lt1", [128, 128], F32)
    on128 = sb("c_on1x", [128, 128], F32)
    utm = sb("c_utm", [128, 128], BF16)
    on1 = sb("c_on1", [1, 128], F32)
    acol = sb("c_acol", [128, 1], F32)
    dtbc = sb("c_dtb", [128, 1], F32)
    dcolb = sb("c_dcol", [128, NXT], F32)
    nwcolb = sb("c_nw", [128, NXT], F32)
    bbcolb = sb("c_bb", [128, NKD], F32)
    selmb = sb("c_selm", [128, 8], F32)
    gcolb = sb("c_gcol", [128, 64], BF16)
    zero128 = sb("c_zero", [128, 128], F32)
    gapb = sb("c_gap", [1, 8], F32)
    convbb = sb("c_convb", [128, CONVD // 128], F32)
    enctb = sb("c_enct", [128, NKD * 40], BF16)
    vtb = sb("c_vt", [40, D], BF16)
    etall = sb("c_et", [40, TR], BF16)
    drowb = sb("c_drow", [1, 128], F32)
    cdrowb = sb("c_cdrow", [1, 128], F32)
    xnhalo = sb("c_xnh", [128, NKD * 3], BF16)
    statc = sb("c_stat", [128, 16], F32)

    xnb = sb("a_xn", [128, NKD * TR], BF16)
    xfm = sb("a_xfm", [128, NXT * TR], BF16)
    bfmb = sb("a_bfm", [128, NG * TR], BF16)
    cfmb = sb("a_cfm", [128, NG * TR], BF16)

    ov = sb("a_ov", [128, 36 * 1024 + 896], U8)
    ovoff = nc.lookup_mloc(ov).addr

    def at(name, shape, dtype, off):
        return nc.alloc_sbuf_tensor_at(name, shape, dtype, offset=ovoff + off)

    inslab = [at(f"w_in{i}", [128, NKD * 128], BF16, i * 8192) for i in range(2)]
    convdsl = [at(f"w_cd{i}", [128, KTAP * 128], BF16, 16384 + i * 1024) for i in range(2)]
    rawsl = [at(f"w_raw{i}", [128, 544], BF16, 18432 + i * 1088) for i in range(2)]
    rbuf = at("r_buf", [128, INTER], F32, 0)
    ainbuf = rbuf
    wotsl = [at(f"w_ot{i}", [128, NKD * 128], BF16, i * 8192) for i in range(2)]
    wbtsl = [at(f"w_bt{i}", [128, NKD * 128], BF16, 16384 + i * 8192) for i in range(2)]
    gatesb = [at(f"w_gt{i}", [128, 512], BF16, 32768 + i * 1024) for i in range(2)]
    ygsqsb = [at(f"w_yq{i}", [128, 512], BF16, 34816 + i * 1024) for i in range(2)]

    sa = sb("a_str", [128, 24 * 1024], U8)
    saoff = nc.lookup_mloc(sa).addr

    def ats(name, shape, dtype, off):
        return nc.alloc_sbuf_tensor_at(name, shape, dtype, offset=saoff + off)

    xhalf = [ats(f"s_xh{i}", [128, 2048], F32, i * 8192) for i in range(2)]
    xntt = ats("s_xntt", [128, D], BF16, 16384)
    coefw = ats("c_coefw", [128, NKD * 40], BF16, 0)
    coefbuf = ats("c_coefb", [128, 120], F32, 2560)
    cmaskbuf = ats("c_cmkb", [128, 120], F32, 3072)
    vacc = ats("c_vacc", [40, D], F32, 4096)
    xtok = [ats(f"s_xtok{i}", [128, 512], F32, i * 2048) for i in range(4)]
    outsb = [ats(f"s_osb{i}", [128, 512], F32, 8192 + i * 2048) for i in range(2)]
    outst = [ats(f"s_ost{i}", [128, 512], F32, 12288 + i * 2048) for i in range(4)]
    sjbuf = [ats(f"s_sj{i}", [128, 2048], F32, i * 8192) for i in range(2)]

    csb = sb("d_cs", [128, TR], F32)
    negcs = sb("d_ncs", [128, TR], F32)
    dtsb = sb("d_dt", [128, TR], F32)
    dasb = sb("d_da", [128, 128], F32)
    datsb = sb("d_dat", [128, 4 * 128], F32)
    cstsb = sb("d_cst", [128, 128], F32)
    ncstsb = sb("d_ncst", [128, 128], F32)
    dttsb = sb("d_dtt", [128, 128], F32)
    dectsb = sb("d_dect", [128, 128], F32)
    expltsb = sb("d_explt", [128, 128], F32)
    expgtsb = sb("d_expgt", [128, 4 * 128], F32)
    dechsb = sb("d_dech", [128, 128], F32)
    qsb = sb("d_q", [128, 128], F32)
    ebsb = sb("d_eb", [128, 40], F32)
    bcastsb = [ats(f"d_bc{i}", [128, 128], F32, i * 512) for i in range(4)]
    lmsb = [ats(f"d_lm{i}", [128, 128], BF16, 2048 + i * 256) for i in range(4)]
    sclsb = [ats(f"d_scl{i}", [128, 128], BF16, 3072 + i * 256) for i in range(4)]
    scmsb = [ats(f"d_scm{i}", [128, 128], BF16, 4096 + i * 256) for i in range(2)]
    btsb = [ats(f"d_bt{i}", [128, 128], BF16, 4608 + i * 256) for i in range(2)]
    bdtsb = [ats(f"d_bdt{i}", [128, 128], BF16, 5120 + i * 256) for i in range(4)]
    xdttsb = [ats(f"d_xdtt{i}", [128, 128], BF16, 6144 + i * 256) for i in range(4)]
    rcpsb = [ats(f"d_rcp{i}", [128, 128], BF16, 7168 + i * 256) for i in range(2)]
    qdsb = [sb(f"d_qd{i}", [128, 128], F32) for i in range(2)]
    yofsb = [ats(f"d_yof{i}", [128, 128], F32, 7680 + i * 512) for i in range(2)]
    segsb = [ats(f"d_seg{i}", [128, 128], F32, 8704 + i * 512) for i in range(4)]
    ycsb = [ats(f"d_yc{i}", [128, 128], BF16, 10752 + i * 256) for i in range(2)]
    rstd8 = ats("c_rstd8", [8, 512], F32, 20480)
    rstdrow = ats("c_rsrow", [1, 512], F32, 22528)

    assert nc.sbuf_bytes_remaining >= 0, f"SBUF over by {-nc.sbuf_bytes_remaining}"

    PA = [nc.alloc_psum_tensor(f"pa{i}", [128, 512], F32) for i in range(2)]
    PT = [nc.alloc_psum_tensor(f"ptp{i}", [128, 1024], BF16) for i in range(2)]
    PC = [nc.alloc_psum_tensor(f"pcc{i}", [128, 512], F32) for i in range(2)]
    PY = [nc.alloc_psum_tensor(f"pyy{i}", [128, 512], F32) for i in range(2)]

    p = Pl(nc)
    sDW = p.sem(st, "sDW")
    sDX = p.sem(st, "sDX")
    sPE = p.sem(st, "sPE")
    sDV = p.sem(st, "sDV")
    sAC = p.sem(st, "sAC")
    sGP = p.sem(st, "sGP")
    E = p.emit

    bankw = {}

    def consume(bank, tok):
        if tok is not None:
            bankw.setdefault(bank, []).append(tok)

    def bank_free(bank):
        return bankw.pop(bank, [])

    def gdma(out, in_, waits=()):
        return E("gpsimd", (lambda o, i: lambda g: g.dma_start(o, i))(out, in_), waits, sDW, 16)

    def sdma(out, in_, waits=()):
        return E("sync", (lambda o, i: lambda s_: s_.dma_start(o, i))(out, in_), waits, sDX, 16)

    def mm(bank, out, lhsT, rhs, start, stop, waits=(), inc=None, isT=False):
        w = list(waits) + (bank_free(bank) if start else [])
        fn = (lambda o, l, r, s0, s1, it: lambda t: t.matmul(
            o, l, r, start=s0, stop=s1, is_transpose=it))(out, lhsT, rhs, start, stop, isT)
        return E("tensor", fn, w, inc)

    def tpose(bank, out, in_, ident, waits=(), inc=sPE, start=True, stop=True):
        return mm(bank, out, in_, ident, start, stop, waits, inc, isT=True)

    def vcopy(out, in_, waits=(), inc=sDV):
        return E("vector", (lambda o, i: lambda v: v.tensor_copy(o, i))(out, in_), waits, inc)

    for dst, src in [(ident32, di32), (identbf, dibf), (lt1, dlt1), (on128, don128),
                     (utm, dutm), (on1, don1), (acol, dacol), (dtbc, ddtb),
                     (dcolb, ddcol), (nwcolb, dnwcol), (bbcolb, dbbcol), (selmb, dselm),
                     (convbb, dconvb)]:
        sdma(dst[:, :], src[:, :])
    for g in range(8):
        sdma(gcolb[:, g * 8:(g + 1) * 8], dgcol[g, :, :])
    for k in range(NKD):
        gdma(enctb[:, k * 40:(k + 1) * 40], denct[k * 128:(k + 1) * 128, :])
    tgap0 = E("gpsimd", lambda g: g.memset(gapb[:, :], 0.0), (), sGP)
    E("gpsimd", lambda g: g.memset(zero128[:, :], 0.0), (), sGP)
    E("gpsimd", lambda g: g.memset(vtb[:, :], 0.0), (), sGP)
    tzero = E("gpsimd", lambda g: g.memset(etall[:, :], 0.0), (), sGP)
    tconst = p.tok(sDX)
    tconstw = p.tok(sDW)

    twe = [None] * NKD
    for k in range(NKD):
        wl = [twe[k - 1]] if k >= 1 else []
        t1 = sdma(coefbuf[:, :], dcoef[k * 128:(k + 1) * 128, :], wl)
        t2 = gdma(cmaskbuf[:, :], dcmask[k * 128:(k + 1) * 128, :], wl)

        def mkwe(k=k):
            def go(v):
                v.tensor_mul(coefbuf[:, 0:40], coefbuf[:, 0:40], cmaskbuf[:, 0:40])
                v.tensor_mul(coefbuf[:, 40:80], coefbuf[:, 40:80], cmaskbuf[:, 40:80])
                v.tensor_mul(coefbuf[:, 80:120], coefbuf[:, 80:120], cmaskbuf[:, 80:120])
                v.tensor_add(coefbuf[:, 0:40], coefbuf[:, 0:40], coefbuf[:, 40:80])
                v.tensor_copy(gapb[0:1, 0:1], gapb[0:1, 1:2])
                return v.tensor_add(coefw[:, k * 40:(k + 1) * 40], coefbuf[:, 0:40],
                                    coefbuf[:, 80:120])
            return go
        twe[k] = E("vector", mkwe(), [t1, t2], sDV)

    tvz = E("gpsimd", lambda g: g.memset(vacc[:, :], 0.0), (), sGP)
    tmt = [None] * NKD
    tvo = [None] * NKD
    for o in range(2):
        tmt[o] = gdma(inslab[o % 2][:, :], dmt[o * 128:(o + 1) * 128, :], [])
    for o in range(NKD):
        for js in range(8):
            bank = "PC%d" % (js % 2)
            tk = mm(bank, PC[js % 2][0:40, 0:512], coefw[:, o * 40:(o + 1) * 40],
                    inslab[o % 2][:, js * 512:(js + 1) * 512], True, True,
                    [tmt[o], twe[o]], sPE)

            def mkadd(js=js, bank=bank):
                def go(v):
                    return v.tensor_add(vacc[0:40, js * 512:(js + 1) * 512],
                                        vacc[0:40, js * 512:(js + 1) * 512],
                                        PC[js % 2][0:40, 0:512])
                return go
            ta = E("vector", mkadd(), [tk, tvz], sDV)
            consume(bank, ta)
        tvo[o] = tk
        if o + 2 < NKD:
            tmt[o + 2] = gdma(inslab[o % 2][:, :], dmt[(o + 2) * 128:(o + 3) * 128, :], [tk])
    tvt = E("vector", lambda v: v.tensor_scalar(vtb[0:40, :], vacc[0:40, :], SCALING, None,
                                                ALU.mult), [p.tok(sDV)], sDV)
    tsetup_pe = p.tok(sPE)
    tsetup_dv = p.tok(sDV)

    def ln_tile(xdram, row0, nrow, dst_fm, colstride, dstcol0, extra=(), prev=None):
        pw = []
        if prev is not None:
            pw = [prev[0]] + list(prev[1])
        th = [sdma(xhalf[h][0:nrow, :], xdram[row0:row0 + nrow, h * 2048:(h + 1) * 2048],
                   list(extra) + pw) for h in range(2)]

        def mkst(h):
            def go(v):
                return v.tensor_reduce(statc[0:nrow, h:h + 1], xhalf[h][0:nrow, :], AXL.X, ALU.add)
            return go

        def mksq(h):
            def go(sc):
                return sc.activation(xntt[0:nrow, h * 2048:(h + 1) * 2048], xhalf[h][0:nrow, :],
                                     ACT.Square, accum_out=statc[0:nrow, 2 + h:3 + h])
            return go
        ts0 = E("vector", mkst(0), [th[0]], sDV)
        ts1 = E("vector", mkst(1), [th[1]], sDV)
        tq0 = E("scalar", mksq(0), [th[0]], sAC)
        tq1 = E("scalar", mksq(1), [th[1]], sAC)

        def mkmv(v):
            v.tensor_add(statc[0:nrow, 4:5], statc[0:nrow, 0:1], statc[0:nrow, 1:2])
            v.tensor_add(statc[0:nrow, 5:6], statc[0:nrow, 2:3], statc[0:nrow, 3:4])
            v.tensor_scalar(statc[0:nrow, 4:5], statc[0:nrow, 4:5], 1.0 / D, None, ALU.mult)
            v.tensor_scalar(statc[0:nrow, 5:6], statc[0:nrow, 5:6], 1.0 / D, None, ALU.mult)
            v.tensor_mul(statc[0:nrow, 6:7], statc[0:nrow, 4:5], statc[0:nrow, 4:5])
            v.tensor_copy(gapb[0:1, 0:1], gapb[0:1, 1:2])
            v.tensor_sub(statc[0:nrow, 5:6], statc[0:nrow, 5:6], statc[0:nrow, 6:7])
            v.tensor_copy(gapb[0:1, 0:1], gapb[0:1, 1:2])
            return v.tensor_scalar(statc[0:nrow, 5:6], statc[0:nrow, 5:6], EPS, None, ALU.add)
        tmv = E("vector", mkmv, [ts0, ts1, tq0, tq1], sDV)
        tsq = E("scalar", lambda sc: sc.activation(statc[0:nrow, 7:8], statc[0:nrow, 5:6],
                                                   ACT.Sqrt), [tmv], sAC)

        def mknewt(v):
            v.reciprocal(statc[0:nrow, 8:9], statc[0:nrow, 7:8])
            v.tensor_copy(gapb[0:1, 0:1], gapb[0:1, 1:2])
            v.tensor_mul(statc[0:nrow, 9:10], statc[0:nrow, 8:9], statc[0:nrow, 8:9])
            v.tensor_copy(gapb[0:1, 0:1], gapb[0:1, 1:2])
            v.tensor_mul(statc[0:nrow, 9:10], statc[0:nrow, 9:10], statc[0:nrow, 5:6])
            v.tensor_copy(gapb[0:1, 0:1], gapb[0:1, 1:2])
            v.tensor_scalar(statc[0:nrow, 9:10], statc[0:nrow, 9:10], -0.5, 1.5, ALU.mult, ALU.add)
            v.tensor_copy(gapb[0:1, 0:1], gapb[0:1, 1:2])
            return v.tensor_mul(statc[0:nrow, 8:9], statc[0:nrow, 8:9], statc[0:nrow, 9:10])
        tnw = E("vector", mknewt, [tsq], sDV)
        tn = []
        for h in range(2):
            def mknm(h=h):
                def go(v):
                    return v.tensor_scalar(xntt[0:nrow, h * 2048:(h + 1) * 2048],
                                           xhalf[h][0:nrow, :], statc[0:nrow, 4:5],
                                           statc[0:nrow, 8:9], ALU.subtract, ALU.mult)
                return go
            tn.append(E("vector", mknm(h), [tnw], sDV))
        tlast = None
        tt = None
        for k in range(NKD):
            bank = "PT%d" % (k % 2)
            slot = PT[k % 2][:, 128 * ((k // 2) % 2):128 * ((k // 2) % 2) + nrow]
            tt = tpose(bank, slot, xntt[0:nrow, k * 128:(k + 1) * 128],
                       identbf[0:nrow, 0:nrow], tn)
            te = vcopy(dst_fm[:, k * colstride + dstcol0:k * colstride + dstcol0 + nrow],
                       slot, [tt])
            consume(bank, te)
            tlast = te
        return (tt, [tn[0], tn[1], tq0, tq1], tlast)

    def round_r(r, prev_end=None):
        xdram = dxr[r]
        lnprev = None
        for tt_ in range(4):
            ex = [tconst, tsetup_pe, tsetup_dv] if (r == 0 and tt_ == 0) else []
            if prev_end is not None and tt_ == 0:
                ex = ex + list(prev_end)
            lnprev = ln_tile(xdram, 3 + tt_ * 128, 128, xnb, TR, tt_ * 128,
                             ex, prev=lnprev)
        ln_tile(xdram, 0, 3, xnhalo, 3, 0, prev=lnprev)
        txn = p.tok(sDV)

        if debug:
            for k in range(NKD):
                sdma(dbg["xn"][r, k * 128:(k + 1) * 128, :], xnb[:, k * TR:(k + 1) * TR], [txn])
        if stage < 2:
            return

        def ct_info(ci):
            if ci == 0:
                return ("dt", PROJ // 128 - 1, None, None)
            if ci <= 16:
                gg = ci - 1
                half, g = gg // 8, gg % 8
                cti = (2 * INTER + half * NG * 128 + g * 128) // 128
                return ("bc", cti, cfmb if half else bfmb, g)
            k = ci - 17
            return ("x", (INTER + k * 128) // 128, xfm, k)

        tslab = [None] * NCT
        tcdsl = [None] * NCT
        tmmend = [None] * NCT
        tconvend = [None] * NCT
        tevac = [None] * NCT

        def slabdma(ci, wl, wl2):
            kind, cti, dst, idx = ct_info(ci)
            tslab[ci] = gdma(inslab[ci % 2][:, :], dwti[cti, :, :], wl)
            if kind != "dt":
                tcdsl[ci] = gdma(convdsl[ci % 2][:, :], dconvd[cti - INTER // 128, :, :], wl2)
        pe_guard = [tsetup_pe] + (list(prev_end) if prev_end is not None else [])
        slabdma(0, pe_guard, [])
        slabdma(1, pe_guard, [])
        for ci in range(NCT):
            kind, cti, dst, idx = ct_info(ci)
            bank, ps = "PA%d" % (ci % 2), PA[ci % 2]
            hbank = "PY%d" % (ci % 2)
            tk = None
            for k in range(NKD):
                tk = mm(bank, ps[:, :], inslab[ci % 2][:, k * 128:(k + 1) * 128],
                        xnb[:, k * TR:(k + 1) * TR], k == 0, k == NKD - 1,
                        [tslab[ci], txn], sPE if k == NKD - 1 else None)
                if kind != "dt":
                    hinc = sPE if k == NKD - 1 else None
                    thl = mm(hbank, PY[ci % 2][:, 0:3],
                             inslab[ci % 2][:, k * 128:(k + 1) * 128],
                             xnhalo[:, k * 3:(k + 1) * 3], k == 0, k == NKD - 1, [], hinc)
            tmmend[ci] = tk
            if kind == "dt":
                def mkz(ci=ci):
                    def go(v):
                        return v.tensor_scalar(dtsb[:, :], PA[ci % 2][:, :], dtbc[:, 0:1],
                                               None, ALU.add)
                    return go
                tz = E("vector", mkz(), [tmmend[ci]], sDV)
                consume(bank, tz)

                def mkabs(sc):
                    return sc.activation(negcs[:, :], dtsb[:, :], ACT.Abs)

                def mkexp(sc):
                    return sc.activation(negcs[:, :], negcs[:, :], ACT.Exp, scale=-1.0)

                def mkln(sc):
                    return sc.activation(negcs[:, :], negcs[:, :], ACT.Ln, bias=1.0)
                ta1 = E("scalar", mkabs, [tz], sAC)
                ta2 = E("scalar", mkexp, [ta1], sAC)
                ta3 = E("scalar", mkln, [ta2], sAC)

                def mkfin(v):
                    v.tensor_scalar(dtsb[:, :], dtsb[:, :], 0.0, None, ALU.max)
                    v.tensor_copy(gapb[0:1, 0:1], gapb[0:1, 1:2])
                    v.tensor_add(dtsb[:, :], dtsb[:, :], negcs[:, :])
                    v.tensor_copy(gapb[0:1, 0:1], gapb[0:1, 1:2])
                    return v.tensor_scalar(dtsb[:, :], dtsb[:, :], DT_MIN, DT_MAX,
                                           ALU.max, ALU.min)
                tevac[ci] = E("vector", mkfin, [ta3], sDV)
            else:
                wr = [tmmend[ci]] + ([tconvend[ci - 2]] if ci >= 2 else [])

                def mkraw(ci=ci):
                    def go(v):
                        return v.tensor_copy(rawsl[ci % 2][:, 3:3 + 512], PA[ci % 2][:, :])
                    return go
                tr0 = E("vector", mkraw(), wr, sDV)
                consume(bank, tr0)

                def mkrh(ci=ci):
                    def go(v):
                        return v.tensor_copy(rawsl[ci % 2][:, 0:3], PY[ci % 2][:, 0:3])
                    return go
                tr1 = E("vector", mkrh(), [thl], sDV)
                consume(hbank, tr1)
                cbank = "PC%d" % (ci % 2)
                tc = None
                for j in range(KTAP):
                    tc = mm(cbank, PC[ci % 2][:, :], convdsl[ci % 2][:, j * 128:(j + 1) * 128],
                            rawsl[ci % 2][:, j:j + 512], j == 0, j == KTAP - 1,
                            [tr0, tr1, tcdsl[ci]], sPE if j == KTAP - 1 else None)
                tconvend[ci] = tc
                if debug and ci in (17, 79):
                    sdma(dbg["raw"][0 if ci == 17 else 1, :, 0:515], rawsl[ci % 2][:, 0:515],
                         [tr0, tr1])

                def mksilu(ci=ci, dst=dst, idx=idx, cti=cti):
                    def go(sc):
                        return sc.activation(dst[:, idx * TR:(idx + 1) * TR], PC[ci % 2][:, :],
                                             ACT.Silu, bias=convbb[:, cti - 64:cti - 63])
                    return go
                tevac[ci] = E("scalar", mksilu(), [tc], sAC)
                consume(cbank, tevac[ci])
            if ci + 2 < NCT:
                slabdma(ci + 2, [tmmend[ci]], [tconvend[ci]] if tconvend[ci] else [])
        tph1ac = p.tok(sAC)
        tdtok = tevac[0]

        if debug:
            for k in range(NXT):
                sdma(dbg["xc"][r, k * 128:(k + 1) * 128, :], xfm[:, k * TR:(k + 1) * TR],
                     [tevac[17 + k]])
            for g in range(8):
                sdma(dbg["bc"][r, g * 128:(g + 1) * 128, :], bfmb[:, g * TR:(g + 1) * TR],
                     [tevac[1 + g]])
                sdma(dbg["bc"][r, 1024 + g * 128:1024 + (g + 1) * 128, :],
                     cfmb[:, g * TR:(g + 1) * TR], [tevac[9 + g]])
            sdma(dbg["dt"][r, :, :], dtsb[:, :], [tdtok])
        if stage < 3:
            return

        twl = [p.tok(sDW), tmmend[NCT - 1], tconvend[NCT - 1]]
        tRinit = E("gpsimd", lambda g: g.memset(rbuf[:, :], 0.0), twl, sGP)
        ptok = {}

        for c in range(NCHUNK):
            c0 = c * CHL

            def mkda(c=c):
                def go(v):
                    return v.tensor_scalar(dasb[:, :], dtsb[:, c * CHL:(c + 1) * CHL],
                                           acol[:, 0:1], None, ALU.mult)
                return go
            tda = E("vector", mkda(), [tdtok], sDV)
            ttda = tpose("PC0", PC[0][:, 0:128], dasb[:, :], ident32[:, :], [tda])
            tdat = vcopy(datsb[:, c * 128:(c + 1) * 128], PC[0][:, 0:128], [ttda])
            consume("PC0", tdat)
            tcst = None
            for cc in range(c + 1):
                tcst = mm("PC1", PC[1][:, 0:128], (lt1 if cc == c else on128)[:, :],
                          datsb[:, cc * 128:(cc + 1) * 128], cc == 0, cc == c,
                          [tdat], sPE if cc == c else None)

            def mkcst(v):
                v.tensor_copy(cstsb[:, :], PC[1][:, 0:128])
                return v.tensor_scalar(ncstsb[:, :], PC[1][:, 0:128], -1.0, None, ALU.mult)
            tcste = E("vector", mkcst, [tcst], sDV)
            consume("PC1", tcste)
            tcs1 = tpose("PC0", PC[0][:, 0:128], cstsb[:, :], ident32[:, :], [tcste])

            def mkcs(c=c):
                def go(v):
                    v.tensor_copy(csb[:, c * CHL:(c + 1) * CHL], PC[0][:, 0:128])
                    return v.tensor_scalar(negcs[:, c * CHL:(c + 1) * CHL], PC[0][:, 0:128],
                                           -1.0, None, ALU.mult)
                return go
            tcse = E("vector", mkcs(), [tcs1], sDV)
            consume("PC0", tcse)

            tdt1 = tpose("PC1", PC[1][:, 0:128], dtsb[:, c * CHL:(c + 1) * CHL], ident32[:, :],
                         [tdtok, tcse])
            tdtt = vcopy(dttsb[:, :], PC[1][:, 0:128], [tdt1])
            consume("PC1", tdtt)

            def mkdec(c=c):
                def go(sc):
                    return sc.activation(dechsb[:, :], csb[:, c * CHL:(c + 1) * CHL], ACT.Exp,
                                         bias=csb[:, (c + 1) * CHL - 1:(c + 1) * CHL], scale=-1.0)
                return go
            tdec = E("scalar", mkdec(), [tcse], sAC)
            tdec1 = tpose("PC0", PC[0][:, 0:128], dechsb[:, :], ident32[:, :], [tdec])
            tdect = vcopy(dectsb[:, :], PC[0][:, 0:128], [tdec1])
            consume("PC0", tdect)

            def mkexpl(c=c):
                def go(sc):
                    if c == 0:
                        return sc.activation(dechsb[:, :], csb[:, 0:CHL], ACT.Exp)
                    return sc.activation(dechsb[:, :], csb[:, c * CHL:(c + 1) * CHL], ACT.Exp,
                                         bias=negcs[:, c * CHL - 1:c * CHL])
                return go
            texl = E("scalar", mkexpl(), [tcse, tdec1], sAC)
            texl1 = tpose("PC1", PC[1][:, 0:128], dechsb[:, :], ident32[:, :], [texl])
            texlt = vcopy(expltsb[:, :], PC[1][:, 0:128], [texl1])
            consume("PC1", texlt)

            def mkexpg(c=c):
                def go(sc):
                    return sc.activation(dechsb[:, :], csb[:, c * CHL:(c + 1) * CHL], ACT.Exp)
                return go
            texg = E("scalar", mkexpg(), [texl1], sAC)
            texg1 = tpose("PC0", PC[0][:, 0:128], dechsb[:, :], ident32[:, :], [texg])
            texgt = vcopy(expgtsb[:, c * 128:(c + 1) * 128], PC[0][:, 0:128], [texg1])
            consume("PC0", texgt)

            def mkcd(c=c):
                def go(sc):
                    if c == 0:
                        return sc.activation(statc[:, 10:11], csb[:, CHL - 1:CHL], ACT.Exp)
                    return sc.activation(statc[:, 10:11], csb[:, (c + 1) * CHL - 1:(c + 1) * CHL],
                                         ACT.Exp, bias=negcs[:, c * CHL - 1:c * CHL])
                return go
            tcd = E("scalar", mkcd(), [tcse], sAC)
            tcd1 = tpose("PC1", PC[1][0:1, 0:128], statc[:, 10:11], ident32[:, :], [tcd])
            tcdr = vcopy(cdrowb[0:1, :], PC[1][0:1, 0:128], [tcd1])
            consume("PC1", tcdr)
            tq1 = mm("PC0", PC[0][:, 0:128], on1[0:1, :], cdrowb[0:1, :], True, True, [tcdr], sPE)
            tq = vcopy(qsb[:, :], PC[0][:, 0:128], [tq1])
            consume("PC0", tq)

            for pr in range(NXT):
                pi = c * NXT + pr
                g = pr // 8
                par = pi % 2
                tks = {}
                ptok[pi] = tks
                prev2 = ptok.get(pi - 2, {})
                prev4 = ptok.get(pi - 4, {})

                if pr % 8 == 0:
                    gb = "PC%d" % (g % 2)
                    tsc1 = mm(gb, PC[g % 2][:, 128:256], bfmb[:, g * TR + c0:g * TR + c0 + CHL],
                              cfmb[:, g * TR + c0:g * TR + c0 + CHL], True, True,
                              [tevac[1 + g], tevac[9 + g]], sPE)

                    def mkscm(g=g):
                        def go(v):
                            return v.tensor_tensor(scmsb[g % 2][:, :], PC[g % 2][:, 128:256],
                                                   utm[:, :], ALU.mult)
                        return go
                    tscm_g = E("vector", mkscm(), [tsc1], sDV)
                    consume(gb, tscm_g)
                    tb1 = tpose("PT%d" % (g % 2), PT[g % 2][:, 512:640],
                                bfmb[:, g * TR + c0:g * TR + c0 + CHL], identbf[:, :],
                                [tevac[1 + g]])
                    tbt_g = vcopy(btsb[g % 2][:, :], PT[g % 2][:, 512:640], [tb1])
                    consume("PT%d" % (g % 2), tbt_g)
                    if g == 0 or True:
                        pass
                    grp_tok = (tscm_g, tbt_g)
                tscm_g, tbt_g = grp_tok

                xbank = "PT%d" % (pi % 2)
                xslot = PT[pi % 2][:, 128 * ((pi // 2) % 2):128 * ((pi // 2) % 2) + 128]
                ttx = tpose(xbank, xslot, xfm[:, pr * TR + c0:pr * TR + c0 + CHL], identbf[:, :],
                            [tevac[17 + pr]])

                def mkxdt(pi=pi, xslot=xslot, pr=pr):
                    def go(v):
                        v.tensor_scalar(xdttsb[pi % 4][:, 0:64], xslot[:, 0:64],
                                        dttsb[:, 2 * pr:2 * pr + 1], None, ALU.mult)
                        return v.tensor_scalar(xdttsb[pi % 4][:, 64:128], xslot[:, 64:128],
                                               dttsb[:, 2 * pr + 1:2 * pr + 2], None, ALU.mult)
                    return go
                txdt = E("vector", mkxdt(), [ttx, tdtt, prev4.get("states")], sDV)
                consume(xbank, txdt)

                tscl = [None, None]
                for hi in range(2):
                    h = 2 * pr + hi
                    m = (2 * pi + hi) % 4
                    rb = "PC%d" % ((2 * pi + hi) % 2)
                    rbp = PC[(2 * pi + hi) % 2]

                    def mkbc(h=h, m=m):
                        def go(sc):
                            return sc.activation(bcastsb[m][:, :], zero128[:, :], ACT.Identity,
                                                 bias=cstsb[:, h:h + 1], scale=0.0)
                        return go
                    tbc = E("scalar", mkbc(), [tcste, prev2.get("rep%d" % hi)], sAC)
                    trep = tpose(rb, rbp[:, 256:384], bcastsb[m][:, :], ident32[:, :], [tbc])
                    tks["rep%d" % hi] = trep

                    def mkseg(h=h, m=m, rbp=rbp):
                        def go(v):
                            return v.tensor_scalar(segsb[m][:, :], rbp[:, 256:384],
                                                   ncstsb[:, h:h + 1], 0.0, ALU.add, ALU.min)
                        return go
                    tseg = E("vector", mkseg(), [trep, prev2.get("lm%d" % hi)], sDV)
                    consume(rb, tseg)

                    def mklm(m=m):
                        def go(sc):
                            return sc.activation(lmsb[m][:, :], segsb[m][:, :], ACT.Exp)
                        return go
                    tlm = E("scalar", mklm(), [tseg, prev2.get("scl%d" % hi)], sAC)
                    tks["lm%d" % hi] = tlm

                    def mkscl(m=m, g=g):
                        def go(v):
                            return v.tensor_mul(sclsb[m][:, :], lmsb[m][:, :], scmsb[g % 2][:, :])
                        return go
                    tscl[hi] = E("vector", mkscl(), [tlm, tscm_g, prev2.get("ydiagend")], sDV)
                    tks["scl%d" % hi] = tscl[hi]

                ybank = "PY%d" % (pi % 2)
                yps = PY[pi % 2]
                m0, m1 = (2 * pi) % 4, (2 * pi + 1) % 4
                ty1 = mm(ybank, yps[:, 0:64], sclsb[m0][:, :], xdttsb[pi % 4][:, 0:64],
                         True, False, [txdt, tscl[0], prev2.get("ycomb")])
                ty2 = mm(ybank, yps[:, 64:128], sclsb[m1][:, :], xdttsb[pi % 4][:, 64:128],
                         False, True, [tscl[1]], sPE)
                tks["ydiagend"] = ty2

                tbd = [None, None]
                for hi in range(2):
                    h = 2 * pr + hi

                    def mkbdt(h=h, m=(2 * pi + hi) % 4, g=g):
                        def go(v):
                            return v.tensor_scalar(bdtsb[m][:, :], btsb[g % 2][:, :],
                                                   dectsb[:, h:h + 1], None, ALU.mult)
                        return go
                    tbd[hi] = E("vector", mkbdt(), [tbt_g, tdect, prev2.get("states")], sDV)
                sbank = "PA%d" % par
                sps = PA[par]
                tst1 = mm(sbank, sps[:, 0:64], bdtsb[m0][:, :], xdttsb[pi % 4][:, 0:64],
                          True, False, [tbd[0], tbd[1], txdt, tscl[1]])
                tst2 = mm(sbank, sps[:, 64:128], bdtsb[m1][:, :], xdttsb[pi % 4][:, 64:128],
                          False, True, [], sPE)
                tks["states"] = tst2

                if c > 0:
                    trcp = E("vector", (lambda pr=pr, pi=pi: lambda v: v.tensor_copy(
                        rcpsb[pi % 2][:, :], rbuf[:, pr * 128:(pr + 1) * 128]))(),
                        [prev2.get("yoffmm"), ptok.get(pi - 64, {}).get("walk")], sDV)
                    tyo1 = mm(sbank, sps[:, 128:256], cfmb[:, g * TR + c0:g * TR + c0 + CHL],
                              rcpsb[pi % 2][:, :], True, True, [trcp, tevac[9 + g]], sPE)
                    tks["yoffmm"] = tyo1

                    def mkyoe(pi=pi, pr=pr, sps=sps):
                        def go(sc):
                            sc.activation(yofsb[pi % 2][:, 0:64], sps[:, 128:192], ACT.Identity,
                                          scale=expltsb[:, 2 * pr:2 * pr + 1])
                            return sc.activation(yofsb[pi % 2][:, 64:128], sps[:, 192:256],
                                                 ACT.Identity,
                                                 scale=expltsb[:, 2 * pr + 1:2 * pr + 2])
                        return go
                    tyoe = E("scalar", mkyoe(), [tyo1, texlt, prev2.get("ycomb")], sAC)
                    consume(sbank, tyoe)

                def mkwalk(pr=pr, par=par, sps=sps):
                    def go(v):
                        v.scalar_tensor_tensor(rbuf[:, pr * 128:pr * 128 + 64],
                                               rbuf[:, pr * 128:pr * 128 + 64],
                                               qsb[:, 2 * pr:2 * pr + 1], sps[:, 0:64],
                                               ALU.mult, ALU.add)
                        return v.scalar_tensor_tensor(rbuf[:, pr * 128 + 64:(pr + 1) * 128],
                                                      rbuf[:, pr * 128 + 64:(pr + 1) * 128],
                                                      qsb[:, 2 * pr + 1:2 * pr + 2],
                                                      sps[:, 64:128], ALU.mult, ALU.add)
                    return go
                wk = [tst2, tq, tRinit]
                if c > 0:
                    wk.append(tks["yoffmm"])
                    wk.append(tyoe)
                twalk = E("vector", mkwalk(), wk, sDV)
                consume(sbank, twalk)
                tks["walk"] = twalk

                if c > 0:
                    def mkcomb(pi=pi, yps=yps):
                        def go(v):
                            return v.tensor_add(ycsb[pi % 2][:, :], yps[:, 0:128],
                                                yofsb[pi % 2][:, :])
                        return go
                    tcomb = E("vector", mkcomb(), [ty2, tyoe, prev2.get("ytr")], sDV)
                else:
                    def mkcomb(pi=pi, yps=yps):
                        def go(v):
                            return v.tensor_copy(ycsb[pi % 2][:, :], yps[:, 0:128])
                        return go
                    tcomb = E("vector", mkcomb(), [ty2, prev2.get("ytr")], sDV)
                consume(ybank, tcomb)
                tks["ycomb"] = tcomb
                ytslot = PT[pi % 2][:, 640:768]
                tytr = tpose("PT%d" % (pi % 2), ytslot, ycsb[pi % 2][:, :], identbf[:, :],
                             [tcomb])
                tks["ytr"] = tytr

                def mkyev(pr=pr, pi=pi, c0=c0, ytslot=ytslot):
                    def go(v):
                        return v.scalar_tensor_tensor(
                            xfm[:, pr * TR + c0:pr * TR + c0 + CHL],
                            xfm[:, pr * TR + c0:pr * TR + c0 + CHL],
                            dcolb[:, pr:pr + 1], ytslot, ALU.mult, ALU.add)
                    return go
                tyev = E("vector", mkyev(), [tytr, txdt], sDV)
                consume("PT%d" % (pi % 2), tyev)
                tks["yevac"] = tyev

        tlastwalk = ptok[NCHUNK * NXT - 1]["walk"]
        tpay = sdma(dagin[r][:, 0:INTER], rbuf[:, :], [tlastwalk])

        def mkdtot(sc):
            return sc.activation(statc[:, 11:12], csb[:, TR - 1:TR], ACT.Exp)
        tdt_ = E("scalar", mkdtot, [p.tok(sDV)], sAC)
        tdr1 = tpose("PC1", PC[1][0:1, 0:128], statc[:, 11:12], ident32[:, :], [tdt_])
        tdrow = vcopy(drowb[0:1, :], PC[1][0:1, 0:128], [tdr1])
        consume("PC1", tdrow)
        tpz = sdma(dagin[r][:, INTER:INTER + 128], zero128[:, :], [tconst])
        tpay2 = sdma(dagin[r][0:1, INTER:INTER + 128], drowb[0:1, :], [tdrow, tpz])

        tdump = []
        if debug and stage == 3:
            for k in range(NXT):
                sdma(dbg["ya"][r, k * 128:(k + 1) * 128, :], xfm[:, k * TR:(k + 1) * TR],
                     [ptok[(NCHUNK - 1) * NXT + k]["yevac"]])
            tdump.append(sdma(dbg["pay"][r, :, 0:INTER], rbuf[:, :], [tlastwalk]))
            sdma(dbg["pay"][r, 0:1, INTER:INTER + 128], drowb[0:1, :], [tdrow])
            sdma(dbg["cs"][r, :, :], csb[:, :], [p.tok(sDV)])
        if stage < 4:
            return None

        tag = E("gpsimd", (lambda r=r: lambda g: g.collective_compute(
            "AllGather", ALU.bypass, replica_groups=[[0, 1, 2, 3], [4, 5, 6, 7]],
            ins=[dagin[r].ap().opt()], outs=[dagout[r].ap().opt()]))(),
            [tpay, tpay2], sGP)

        for ts_ in range(4):
            te_mm = None
            for k in range(NKD):
                te_mm = mm("PC%d" % (ts_ % 2), PC[ts_ % 2][:, 0:40],
                           xnb[:, k * TR + ts_ * 128:k * TR + (ts_ + 1) * 128],
                           enctb[:, k * 40:(k + 1) * 40], k == 0, k == NKD - 1,
                           [txn], sPE if k == NKD - 1 else None)
            tecp = vcopy(ebsb[:, :], PC[ts_ % 2][:, 0:40], [te_mm])
            consume("PC%d" % (ts_ % 2), tecp)
            tet = tpose("PC%d" % ((ts_ + 1) % 2), PC[(ts_ + 1) % 2][0:40, 0:128],
                        ebsb[:, :], ident32[:, :], [tecp])
            teev = vcopy(etall[0:40, ts_ * 128:(ts_ + 1) * 128],
                         PC[(ts_ + 1) % 2][0:40, 0:128], [tet])
            consume("PC%d" % ((ts_ + 1) % 2), teev)
        tet_done = p.tok(sDV)

        def combine_pass2(masked, seed_from_dram, out_extra_wait):
            jrange = list(range(3)) if masked else list(range(4))
            tqd = [None] * 4
            if seed_from_dram:
                tseed = sdma(ainbuf[:, :], dst2048[:, :], list(out_extra_wait) + [tag])
            else:
                tseed = E("gpsimd", (lambda: lambda g: g.memset(ainbuf[:, :], 0.0))(),
                          list(out_extra_wait) + [tag], sGP)
            for j in jrange:
                tdr = sdma(drowb[0:1, :], dagout[r][128 * j:128 * j + 1, INTER:INTER + 128],
                           [tag] + ([tqd[j - 2]] if j >= 2 else []))
                tqm = mm("PC%d" % (j % 2), PC[j % 2][:, 0:128], on1[0:1, :], drowb[0:1, :],
                         True, True, [tdr], sPE)

                def mkqd(j=j, masked=masked):
                    def go(v):
                        if masked:
                            return v.tensor_scalar(qdsb[j % 2][:, :], PC[j % 2][:, 0:128],
                                                   selmb[:, j:j + 1], selmb[:, 4 + j:5 + j],
                                                   ALU.mult, ALU.add)
                        return v.tensor_copy(qdsb[j % 2][:, :], PC[j % 2][:, 0:128])
                    return go
                tqd[j] = E("vector", mkqd(), [tqm], sDV)
                consume("PC%d" % (j % 2), tqd[j])
            tlast = None
            tprevq = {}
            tquarter = [None] * 4
            for quarter in range(4):
                for ji, j in enumerate(jrange):
                    qi = (quarter * len(jrange) + ji) % 2
                    tsj_new = sdma(sjbuf[qi][:, :],
                                   dagout[r][128 * j:128 * (j + 1),
                                             quarter * 2048:(quarter + 1) * 2048],
                                   [tag] + ([tprevq[qi]] if qi in tprevq else []))
                    tscl_s = None
                    if masked:
                        def mksm(j=j, qi=qi):
                            def go(v):
                                return v.tensor_scalar(sjbuf[qi][:, :], sjbuf[qi][:, :],
                                                       selmb[:, j:j + 1], None, ALU.mult)
                            return go
                        tscl_s = E("vector", mksm(), [tsj_new], sDV)
                    for hb in range(32):
                        h = quarter * 32 + hb

                        def mkc(j=j, qi=qi, h=h, hb=hb):
                            def go(v):
                                return v.scalar_tensor_tensor(
                                    ainbuf[:, h * 64:(h + 1) * 64],
                                    ainbuf[:, h * 64:(h + 1) * 64],
                                    qdsb[j % 2][:, h:h + 1],
                                    sjbuf[qi][:, hb * 64:(hb + 1) * 64],
                                    ALU.mult, ALU.add)
                            return go
                        tlast = E("vector", mkc(), [tsj_new, tscl_s, tqd[j], tseed], sDV)
                    tprevq[qi] = tlast
                tquarter[quarter] = tlast
            return tlast, tquarter

        textra = list(tdump)
        if r == 0:
            tfull, _ = combine_pass2(False, False, textra + [tpay])
            tst_dma = sdma(dst2048[:, :], ainbuf[:, :], [tfull])
            tain, tainq = combine_pass2(True, False, [tst_dma])
        else:
            tain, tainq = combine_pass2(True, True, textra + [tpay])

        btok = {}
        for pr in range(NXT):
            for c in range(NCHUNK):
                c0 = c * CHL
                pib = pr * NCHUNK + c
                g = pr // 8
                par = pib % 2
                prevb = btok.get(pib - 2, {})
                tks2 = {}
                btok[pib] = tks2

                def mkacp(pr=pr, pib=pib):
                    def go(v):
                        return v.tensor_copy(rcpsb[pib % 2][:, :],
                                             rbuf[:, pr * 128:(pr + 1) * 128])
                    return go
                tacp = E("vector", mkacp(), [tainq[pr // 16], prevb.get("bmm")], sDV)
                tks2["acp"] = tacp
                tbmm = mm("PA%d" % par, PA[par][:, 128:256],
                          cfmb[:, g * TR + c0:g * TR + c0 + CHL],
                          rcpsb[pib % 2][:, :], True, True,
                          [tacp, tevac[9 + g]], sPE)
                tks2["bmm"] = tbmm

                def mkbsc(pib=pib, pr=pr, par=par, c=c):
                    def go(v):
                        v.tensor_scalar(yofsb[pib % 2][:, 0:64], PA[par][:, 128:192],
                                        expgtsb[:, c * 128 + 2 * pr:c * 128 + 2 * pr + 1],
                                        None, ALU.mult)
                        return v.tensor_scalar(yofsb[pib % 2][:, 64:128], PA[par][:, 192:256],
                                               expgtsb[:, c * 128 + 2 * pr + 1:c * 128 + 2 * pr + 2],
                                               None, ALU.mult)
                    return go
                tbsc = E("vector", mkbsc(), [tbmm, prevb.get("btr")], sDV)
                consume("PA%d" % par, tbsc)
                tbtr = tpose("PY%d" % par, PY[par][:, 0:128], yofsb[pib % 2][:, :],
                             ident32[:, :], [tbsc])
                tks2["btr"] = tbtr

                def mkbadd(pr=pr, c0=c0, par=par):
                    def go(v):
                        return v.tensor_add(xfm[:, pr * TR + c0:pr * TR + c0 + CHL],
                                            xfm[:, pr * TR + c0:pr * TR + c0 + CHL],
                                            PY[par][:, 0:128])
                    return go
                tbadd = E("vector", mkbadd(),
                          [tbtr, ptok[c * NXT + pr]["yevac"]], sDV)
                consume("PY%d" % par, tbadd)
                tks2["badd"] = tbadd
        tphB = p.tok(sDV)

        if debug:
            for k in range(NXT):
                sdma(dbg["yb"][r, k * 128:(k + 1) * 128, :], xfm[:, k * TR:(k + 1) * TR],
                     [btok[k * NCHUNK + NCHUNK - 1]["badd"]])
        if stage < 5:
            return None

        tgsl = [None] * NXT
        tgmm = [None] * NXT
        tgate = [None] * NXT
        tsq = [None] * NXT
        tnmm = [None] * NXT

        def gslab(kk, wl):
            tgsl[kk] = gdma(inslab[kk % 2][:, :], dwti[kk, :, :], wl)
        gslab(0, [btok[15 * NCHUNK + 3]["acp"]])
        gslab(1, [btok[31 * NCHUNK + 3]["acp"]])
        for kk in range(NXT):
            bank, ps = "PA%d" % (kk % 2), PA[kk % 2]
            tk = None
            for k in range(NKD):
                tk = mm(bank, ps[:, :], inslab[kk % 2][:, k * 128:(k + 1) * 128],
                        xnb[:, k * TR:(k + 1) * TR], k == 0, k == NKD - 1,
                        [tgsl[kk]], sPE if k == NKD - 1 else None)
            tgmm[kk] = tk

            def mkgs(kk=kk):
                def go(sc):
                    return sc.activation(gatesb[kk % 2][:, :], PA[kk % 2][:, :], ACT.Silu)
                return go
            wg = [tk] + ([tgate[kk - 2]] if kk >= 2 else [])
            tsilu = E("scalar", mkgs(), wg, sAC)
            consume(bank, tsilu)

            def mkgm(kk=kk):
                def go(v):
                    return v.tensor_mul(xfm[:, kk * TR:(kk + 1) * TR],
                                        xfm[:, kk * TR:(kk + 1) * TR],
                                        gatesb[kk % 2][:, :])
                return go
            tgate[kk] = E("vector", mkgm(), [tsilu, btok[kk * NCHUNK + 3]["badd"]], sDV)

            def mksqg(kk=kk):
                def go(sc):
                    return sc.activation(ygsqsb[kk % 2][:, :], xfm[:, kk * TR:(kk + 1) * TR],
                                         ACT.Square)
                return go
            ws = [tgate[kk]] + ([tnmm[kk - 2]] if kk >= 2 else [])
            tsq[kk] = E("scalar", mksqg(), ws, sAC)
            tnmm[kk] = mm("PY0", PY[0][0:8, 0:512], gcolb[:, (kk // 8) * 8:(kk // 8) * 8 + 8],
                          ygsqsb[kk % 2][:, :], kk == 0, kk == NXT - 1, [tsq[kk]],
                          sPE if kk == NXT - 1 else None)
            if kk + 2 < NXT:
                gslab(kk + 2, [tgmm[kk]])

        def mkms(v):
            return v.tensor_scalar(rstd8[:, :], PY[0][0:8, 0:512], 1.0 / 1024, EPS,
                                   ALU.mult, ALU.add)
        tms = E("vector", mkms, [tnmm[NXT - 1]], sDV)
        consume("PY0", tms)
        tsq8 = E("scalar", lambda sc: sc.activation(rstd8[:, :], rstd8[:, :], ACT.Sqrt),
                 [tms], sAC)

        def mkrec(v):
            return v.reciprocal(rstd8[:, :], rstd8[:, :])
        trec = E("vector", mkrec, [tsq8], sDV)

        tscale = [None] * NXT
        tgrep = None
        for gg in range(8):
            trow = sdma(rstdrow[0:1, :], rstd8[gg:gg + 1, :],
                        [trec] + ([tgrep] if tgrep else []))
            tgrep_mm = mm("PC0", PC[0][:, 0:512], on1[0:1, :], rstdrow[0:1, :], True, True,
                          [trow], sPE)
            for kk in range(gg * 8, gg * 8 + 8):
                def mksc2(kk=kk):
                    def go(v):
                        return v.scalar_tensor_tensor(xfm[:, kk * TR:(kk + 1) * TR],
                                                      xfm[:, kk * TR:(kk + 1) * TR],
                                                      nwcolb[:, kk:kk + 1],
                                                      PC[0][:, 0:512], ALU.mult, ALU.mult)
                    return go
                tscale[kk] = E("vector", mksc2(), [tgrep_mm, tsq[kk]], sDV)
                consume("PC0", tscale[kk])
            tgrep = tscale[gg * 8 + 7]

        if debug:
            for k in range(NXT):
                sdma(dbg["ygn"][r, k * 128:(k + 1) * 128, :], xfm[:, k * TR:(k + 1) * TR],
                     [tscale[k]])
        if stage < 6:
            return None

        twsl = [None] * (2 * NKD)
        twb = [None] * NKD
        tdmm = [None] * NKD
        tdev = [None] * NKD
        txk = {}
        tflush = {}
        tadds = {}

        def wdma(dtl, half, wl):
            twsl[2 * dtl + half] = gdma(wotsl[half][:, :],
                                        dwto[dtl, :, half * 4096:(half + 1) * 4096], wl)

        def wbdma(dtl, wl):
            twb[dtl] = gdma(wbtsl[dtl % 2][:, :], dwbt[dtl, :, :], wl)

        wdma(0, 0, [tgmm[NXT - 2]])
        wdma(0, 1, [tgmm[NXT - 1]])
        wbdma(0, [tphB])
        wbdma(1, [tphB])

        def xtdma(dg, ts_, wl):
            txk[(dg, ts_)] = sdma(xtok[ts_][:, :],
                                  xdram[3 + ts_ * 128:3 + (ts_ + 1) * 128,
                                        dg * 512:(dg + 1) * 512], wl)
        for ts_ in range(4):
            xtdma(0, ts_, [tphB])

        for dtl in range(NKD):
            bank, ps = "PA%d" % (dtl % 2), PA[dtl % 2]
            dg = dtl // 4
            tk = None
            for k in range(NKD):
                tk = mm(bank, ps[:, :], wbtsl[dtl % 2][:, k * 128:(k + 1) * 128],
                        xnb[:, k * TR:(k + 1) * TR], k == 0, False,
                        [twb[dtl]], None)
            tk = mm(bank, ps[:, :], vtb[0:40, dtl * 128:(dtl + 1) * 128], etall[0:40, :],
                    False, False, [tet_done], None)
            for half in range(2):
                for k2 in range(NKD):
                    kk = half * NKD + k2
                    last = (half == 1 and k2 == NKD - 1)
                    inc2 = sPE if (last or k2 == NKD - 1) else None
                    tk = mm(bank, ps[:, :],
                            wotsl[half][:, k2 * 128:(k2 + 1) * 128],
                            xfm[:, kk * TR:(kk + 1) * TR], False, last,
                            [twsl[2 * dtl + half], tscale[kk]], inc2)
                if half == 0 and dtl + 1 < NKD:
                    wdma(dtl + 1, 0, [tk])
            tdmm[dtl] = tk
            if dtl + 2 < NKD:
                wbdma(dtl + 2, [tdmm[dtl]])
            if dtl + 1 < NKD:
                wdma(dtl + 1, 1, [tdmm[dtl]])

            def mkoev(dtl=dtl, par=dtl % 2):
                def go(v):
                    return v.tensor_scalar(outsb[dtl % 2][:, :], PA[par][:, :],
                                           bbcolb[:, dtl:dtl + 1], None, ALU.add)
                return go
            wv = [tdmm[dtl]] + ([tdev[dtl - 2]] if dtl >= 2 else [])
            tev_ = E("vector", mkoev(), wv, sDV)
            consume(bank, tev_)
            for ts_ in range(4):
                tb = "PY%d" % (ts_ % 2)
                tt_o = tpose(tb, PY[ts_ % 2][:, 128:256], outsb[dtl % 2][:, ts_ * 128:(ts_ + 1) * 128],
                             ident32[:, :], [tev_])

                def mkres(dtl=dtl, ts_=ts_, dg=dg):
                    def go(v):
                        return v.tensor_add(outst[ts_][:, (dtl % 4) * 128:(dtl % 4 + 1) * 128],
                                            PY[ts_ % 2][:, 128:256],
                                            xtok[ts_][:, (dtl % 4) * 128:(dtl % 4 + 1) * 128])
                    return go
                wl3 = [tt_o, txk[(dg, ts_)]]
                if (dg - 1, ts_) in tflush:
                    wl3.append(tflush[(dg - 1, ts_)])
                tadds[(dtl, ts_)] = E("vector", mkres(), wl3, sDV)
                consume(tb, tadds[(dtl, ts_)])
            tdev[dtl] = tadds[(dtl, 3)]
            if dtl % 4 == 3:
                for ts_ in range(4):
                    tflush[(dg, ts_)] = sdma(
                        dout[r * TR + ts_ * 128:r * TR + (ts_ + 1) * 128,
                             dg * 512:(dg + 1) * 512],
                        outst[ts_][:, :], [tadds[(dtl, ts_)]])
                if dg + 1 < 8:
                    for ts_ in range(4):
                        xtdma(dg + 1, ts_, [tadds[(dtl, ts_)]])
        tend_pe = p.tok(sPE)
        tend_dv = p.tok(sDV)
        tend_dx = p.tok(sDX)
        tend_dw = p.tok(sDW)
        tend_ac = p.tok(sAC)
        return (tend_pe, tend_dv, tend_dx, tend_dw, tend_ac)

    prev_end = None
    for r in range(2 if stage >= 6 else 1):
        prev_end = round_r(r, prev_end)

    E("sync", lambda s_: s_.nop(), [p.tok(sDX)])
    E("gpsimd", lambda g: g.nop(), [p.tok(sDW)])

    def gapfn(e):
        e.tensor_copy(gapb[0:1, 0:1], gapb[0:1, 1:2])

    with nc.Block() as block:
        p.replay(block, gapfn=gapfn)
    st.close()
    return nc


def kernel(**inputs):
    sh = host_shared(inputs)
    nc = build()
    in_maps = []
    for core in range(8):
        m = dict(sh)
        m.update(host_percore(inputs, core))
        in_maps.append(m)
    res = run_bass_kernel_spmd(nc, in_maps, core_ids=list(range(8)))
    out = np.empty((B, S, D), np.float32)
    for core in range(8):
        b, q = core // 4, core % 4
        o = res.results[core]["out"]
        out[b, 512 * q:512 * q + 512] = o[0:512]
        out[b, 2048 + 512 * q:2048 + 512 * q + 512] = o[512:1024]
    return out


# version 05
# speedup vs baseline: 1.0448x; 1.0013x over previous
import math
import contextlib
import numpy as np
import ml_dtypes
import concourse.bass as bass
import concourse.mybir as mybir
from concourse.bass_utils import run_bass_kernel_spmd

F32 = mybir.dt.float32
BF16 = mybir.dt.bfloat16
U8 = mybir.dt.uint8
ALU = mybir.AluOpType
ACT = mybir.ActivationFunctionType
AXL = mybir.AxisListType

B, S, D = 2, 4096, 4096
NH, HD, NG, KTAP = 128, 64, 8, 4
CHL = 128
INTER = NH * HD
CONVD = INTER + 2 * NG * 128
PROJ = INTER + CONVD + NH
TR = 512
NCHUNK = TR // CHL
NKD = D // 128
NXT = INTER // 128
DT_MIN, DT_MAX, EPS = 1e-3, 100.0, 1e-5
SCALING = 16.0 / 32.0
NCT = 81

STAGE = 6
DEBUG = False


class Pl:
    ENGINES = ("sync", "gpsimd", "scalar", "vector", "tensor")

    def __init__(self, nc):
        self.nc = nc
        self.items = {e: [] for e in self.ENGINES}
        self.cnt = {}
        self.hand = {}
        self.hiw = {}

    def sem(self, stack, name):
        self.hand[name] = stack.enter_context(self.nc.semaphore(name))
        self.cnt[name] = 0
        return name

    def tok(self, sem):
        return (sem, self.cnt[sem])

    def emit(self, eng, fn, waits=(), inc=None, amt=1):
        tokv = None
        if inc is not None:
            self.cnt[inc] += amt
            tokv = (inc, self.cnt[inc])
        wl = []
        for w in waits:
            if w is None:
                continue
            s, v = w
            if v <= 0 or self.hiw.get((eng, s), 0) >= v:
                continue
            self.hiw[(eng, s)] = v
            wl.append((s, v))
        self.items[eng].append((fn, wl, tokv, amt))
        return tokv

    @staticmethod
    def _spans(args):
        out = []
        for a in args:
            name = getattr(getattr(a, "tensor", None), "name", None)
            if name is None:
                memset = getattr(a, "memsetref", None) or ""
                name = memset[:-4] if memset.endswith("_set") else memset
            ap = getattr(a, "ap", None)
            off = getattr(a, "offset", 0)
            if ap is None:
                continue
            span = 1
            for (st_, cn) in list(ap)[1:]:
                span += abs(st_) * (cn - 1)
            out.append((str(name), off, off + span))
        return out

    def _vgaps(self, gapfn):
        nc = self.nc
        gaps = set()
        intra = []
        with nc.discard():
            e = nc.vector
            recs = []
            for idx, (fn, wl, tokv, amt) in enumerate(self.items["vector"]):
                insts = []
                with _InstCatcher(nc, insts):
                    fn(e)
                for j, ins in enumerate(insts):
                    rspan = self._spans(getattr(ins, "ins", []) or [])
                    wspan = self._spans(getattr(ins, "outs", []) or [])
                    recs.append((idx, j, rspan, wspan))
            for k in range(1, len(recs)):
                pidx, pj, pr, pw = recs[k - 1]
                idx, j, rr, rw = recs[k]
                hit = False
                for (wn, wl_, wh) in pw:
                    if wn.startswith("c_gap"):
                        continue
                    for (rn, rl, rh) in rr:
                        if wn == rn and rl < wh and wl_ < rh:
                            hit = True
                if not hit:
                    continue
                if pidx == idx:
                    intra.append((idx, pj, j))
                else:
                    gaps.add(idx)
        if intra:
            raise AssertionError(f"intra-item vector RAW chains (add gaps inside): {intra[:10]}")
        return gaps

    def replay(self, block, gapfn=None):
        hand = self.hand
        vgaps = self._vgaps(gapfn) if gapfn is not None else set()

        def mk(eng):
            items = self.items[eng]

            def go(e):
                for idx, (fn, wl, tokv, amt) in enumerate(items):
                    if eng == "vector" and idx in vgaps:
                        gapfn(e)
                    for (s, v) in wl:
                        e.wait_ge(hand[s], v)
                    inst = fn(e)
                    if tokv is not None:
                        if inst is None:
                            inst = e.nop()
                        inst.then_inc(hand[tokv[0]], amt)
            return go

        for eng in self.ENGINES:
            getattr(block, eng)(mk(eng))


class _InstCatcher:

    def __init__(self, nc, sink):
        self.nc = nc
        self.sink = sink

    def __enter__(self):
        self.cb = lambda ins: self.sink.append(ins)
        self.nc.push_inst_callback(self.cb)
        return self

    def __exit__(self, *a):
        self.nc.remove_inst_callback(self.cb)
        return False


_HC = {}


def fourier_mt():
    if "mt" not in _HC:
        n = D
        j = np.arange(n, dtype=np.float64)
        Mt = np.empty((n, n), dtype=np.float64)
        Mt[0, :] = 1.0 / math.sqrt(n)
        maxf = (n - 1) // 2
        f = np.arange(1, maxf + 1, dtype=np.float64)
        ang = 2.0 * np.pi * np.outer(f, j) / n
        c = math.sqrt(2.0 / n)
        Mt[1:2 * maxf:2, :] = c * np.cos(ang)
        Mt[2:2 * maxf + 1:2, :] = c * np.sin(ang)
        Mt[n - 1, :] = ((-1.0) ** j) / math.sqrt(n)
        _HC["mt"] = np.ascontiguousarray(Mt.astype(np.float32))
    return _HC["mt"]


def _slab_image(wT, c0):
    a = wT[:, c0:c0 + 128]
    return a.reshape(D // 128, 128, 128).transpose(1, 0, 2).reshape(128, D)


def host_shared(inp):
    i = np.arange(128)
    sh = {}
    wti = np.asarray(inp["L0_attn_in_proj_w"]).T
    sh["wti"] = np.stack([_slab_image(wti, c * 128) for c in range(PROJ // 128)])
    wto = np.asarray(inp["L0_attn_out_proj_w"]).T
    im = []
    for dtl in range(NKD):
        a = wto[:, dtl * 128:(dtl + 1) * 128]
        im.append(a.reshape(64, 128, 128).transpose(1, 0, 2).reshape(128, 8192))
    sh["wto"] = np.stack(im)
    wbt = np.asarray(inp["L0_ffn_W_base"]).T
    sh["wbt"] = np.stack([_slab_image(wbt, c * 128) for c in range(NKD)])
    sh["mt"] = fourier_mt()
    cw = np.asarray(inp["L0_attn_conv_w"]).reshape(CONVD, KTAP)
    convd = np.zeros((CONVD // 128, 128, KTAP, 128), np.float32)
    t = np.arange(CONVD)
    for jj in range(KTAP):
        convd[t // 128, t % 128, jj, t % 128] = cw[:, jj]
    sh["convd"] = convd.reshape(CONVD // 128, 128, KTAP * 128)
    sh["convb"] = np.ascontiguousarray(np.asarray(inp["L0_attn_conv_b"]).reshape(CONVD // 128, 128).T)
    sh["dtb"] = np.asarray(inp["L0_attn_dt_bias"]).reshape(128, 1).astype(np.float32)
    sh["acol"] = (-(i + 1).astype(np.float32)).reshape(128, 1)
    sh["dcol"] = np.ascontiguousarray(np.repeat(np.asarray(inp["L0_attn_D"]), HD).reshape(NXT, 128).T)
    sh["nwcol"] = np.ascontiguousarray(np.asarray(inp["L0_attn_norm_w"]).reshape(NXT, 128).T)
    sh["bbcol"] = np.ascontiguousarray(np.asarray(inp["L0_ffn_b_base"]).reshape(NKD, 128).T)
    sh["enct"] = np.ascontiguousarray(
        np.concatenate([np.asarray(inp["L0_ffn_enc0"]).T, np.asarray(inp["L0_ffn_enc1"]).T], axis=1))
    co = np.concatenate([np.transpose(np.asarray(inp["L0_ffn_coefs0"]), (1, 0, 2)),
                         np.transpose(np.asarray(inp["L0_ffn_coefs1"]), (1, 0, 2))],
                        axis=2).reshape(D, 120)
    sh["coef"] = np.ascontiguousarray(co)
    cm = np.concatenate([np.transpose(np.asarray(inp["L0_ffn_mask0"]), (1, 0, 2)),
                         np.transpose(np.asarray(inp["L0_ffn_mask1"]), (1, 0, 2))],
                        axis=2).reshape(D, 120)
    sh["cmask"] = np.ascontiguousarray(cm.astype(np.uint8))
    sh["ident32"] = np.eye(128, dtype=np.float32)
    sh["identbf"] = np.eye(128, dtype=ml_dtypes.bfloat16)
    sh["lt1"] = (i[:, None] <= i[None, :]).astype(np.float32)
    sh["ones128"] = np.ones((128, 128), np.float32)
    sh["utm"] = (i[:, None] <= i[None, :]).astype(ml_dtypes.bfloat16)
    sh["ones1"] = np.ones((1, 128), np.float32)
    gc = np.zeros((8, 128, 8), dtype=ml_dtypes.bfloat16)
    for g in range(8):
        gc[g, :, g] = 1
    sh["gcol"] = gc
    return sh


def host_percore(inp, core):
    b, q = core // 4, core % 4
    x = np.asarray(inp["L0_attn_hidden_states"])[b]
    pc = {}
    for r in range(2):
        t0 = 2048 * r + 512 * q
        main = x[t0:t0 + TR]
        halo = x[t0 - 3:t0] if t0 >= 3 else np.zeros((3, D), np.float32)
        pc[f"x{r}"] = np.ascontiguousarray(np.concatenate([halo, main], axis=0))
    sel = np.zeros((128, 8), np.float32)
    for j in range(3):
        sel[:, j] = 1.0 if j < q else 0.0
        sel[:, 4 + j] = 0.0 if j < q else 1.0
    pc["selm"] = sel
    return pc


def build(stage=STAGE, debug=DEBUG):
    nc = bass.Bass("TRN2", target_bir_lowering=False, detect_race_conditions=False)
    st = contextlib.ExitStack()

    dxr = [nc.dram_tensor(f"x{r}", [3 + TR, D], F32, kind="ExternalInput") for r in range(2)]
    dwti = nc.dram_tensor("wti", [PROJ // 128, 128, D], F32, kind="ExternalInput")
    dwto = nc.dram_tensor("wto", [NKD, 128, INTER], F32, kind="ExternalInput")
    dwbt = nc.dram_tensor("wbt", [NKD, 128, D], F32, kind="ExternalInput")
    dmt = nc.dram_tensor("mt", [D, D], F32, kind="ExternalInput")
    dconvd = nc.dram_tensor("convd", [CONVD // 128, 128, KTAP * 128], F32, kind="ExternalInput")
    dconvb = nc.dram_tensor("convb", [128, CONVD // 128], F32, kind="ExternalInput")
    ddtb = nc.dram_tensor("dtb", [128, 1], F32, kind="ExternalInput")
    dacol = nc.dram_tensor("acol", [128, 1], F32, kind="ExternalInput")
    ddcol = nc.dram_tensor("dcol", [128, NXT], F32, kind="ExternalInput")
    dnwcol = nc.dram_tensor("nwcol", [128, NXT], F32, kind="ExternalInput")
    dbbcol = nc.dram_tensor("bbcol", [128, NKD], F32, kind="ExternalInput")
    denct = nc.dram_tensor("enct", [D, 40], F32, kind="ExternalInput")
    dcoef = nc.dram_tensor("coef", [D, 120], F32, kind="ExternalInput")
    dcmask = nc.dram_tensor("cmask", [D, 120], U8, kind="ExternalInput")
    dselm = nc.dram_tensor("selm", [128, 8], F32, kind="ExternalInput")
    di32 = nc.dram_tensor("ident32", [128, 128], F32, kind="ExternalInput")
    dibf = nc.dram_tensor("identbf", [128, 128], BF16, kind="ExternalInput")
    dlt1 = nc.dram_tensor("lt1", [128, 128], F32, kind="ExternalInput")
    don128 = nc.dram_tensor("ones128", [128, 128], F32, kind="ExternalInput")
    dutm = nc.dram_tensor("utm", [128, 128], BF16, kind="ExternalInput")
    don1 = nc.dram_tensor("ones1", [1, 128], F32, kind="ExternalInput")
    dgcol = nc.dram_tensor("gcol", [8, 128, 8], BF16, kind="ExternalInput")
    dout = nc.dram_tensor("out", [2 * TR, D], F32, kind="ExternalOutput")
    dagin = [nc.dram_tensor(f"agin{r}", [128, INTER + 128], F32) for r in range(2)]
    dagout = [nc.dram_tensor(f"agout{r}", [512, INTER + 128], F32) for r in range(2)]
    dst2048 = nc.dram_tensor("st2048", [128, INTER], F32)

    dbg = {}
    if debug:
        for nm, shp, dt_ in [("xn", [2, D, TR], BF16), ("xc", [2, INTER, TR], BF16),
                             ("bc", [2, 2048, TR], BF16), ("dt", [2, 128, TR], F32),
                             ("cs", [2, 128, TR], F32), ("ya", [2, INTER, TR], BF16),
                             ("pay", [2, 128, INTER + 128], F32),
                             ("raw", [2, 128, 544], BF16),
                             ("yb", [2, INTER, TR], BF16), ("ygn", [2, INTER, TR], BF16)]:
            dbg[nm] = nc.dram_tensor("dbg_" + nm, shp, dt_, kind="ExternalOutput")

    def sb(name, shape, dtype):
        return nc.alloc_sbuf_tensor(name, shape, dtype)

    ident32 = sb("c_i32", [128, 128], F32)
    identbf = sb("c_ibf", [128, 128], BF16)
    lt1 = sb("c_lt1", [128, 128], F32)
    on128 = sb("c_on1x", [128, 128], F32)
    utm = sb("c_utm", [128, 128], BF16)
    on1 = sb("c_on1", [1, 128], F32)
    acol = sb("c_acol", [128, 1], F32)
    dtbc = sb("c_dtb", [128, 1], F32)
    dcolb = sb("c_dcol", [128, NXT], F32)
    nwcolb = sb("c_nw", [128, NXT], F32)
    bbcolb = sb("c_bb", [128, NKD], F32)
    selmb = sb("c_selm", [128, 8], F32)
    gcolb = sb("c_gcol", [128, 64], BF16)
    zero128 = sb("c_zero", [128, 128], F32)
    gapb = sb("c_gap", [1, 8], F32)
    convbb = sb("c_convb", [128, CONVD // 128], F32)
    enctb = sb("c_enct", [128, NKD * 40], BF16)
    vtb = sb("c_vt", [40, D], BF16)
    etall = sb("c_et", [40, TR], BF16)
    drowb = sb("c_drow", [1, 128], F32)
    cdrowb = sb("c_cdrow", [1, 128], F32)
    xnhalo = sb("c_xnh", [128, NKD * 3], BF16)
    statc = sb("c_stat", [128, 16], F32)

    xnb = sb("a_xn", [128, NKD * TR], BF16)
    xfm = sb("a_xfm", [128, NXT * TR], BF16)
    bfmb = sb("a_bfm", [128, NG * TR], BF16)
    cfmb = sb("a_cfm", [128, NG * TR], BF16)

    ov = sb("a_ov", [128, 36 * 1024 + 896], U8)
    ovoff = nc.lookup_mloc(ov).addr

    def at(name, shape, dtype, off):
        return nc.alloc_sbuf_tensor_at(name, shape, dtype, offset=ovoff + off)

    inslab = [at(f"w_in{i}", [128, NKD * 128], BF16, i * 8192) for i in range(2)]
    convdsl = [at(f"w_cd{i}", [128, KTAP * 128], BF16, 16384 + i * 1024) for i in range(2)]
    rawsl = [at(f"w_raw{i}", [128, 544], BF16, 18432 + i * 1088) for i in range(2)]
    rbuf = at("r_buf", [128, INTER], F32, 0)
    ainbuf = rbuf
    wotsl = [at(f"w_ot{i}", [128, NKD * 128], BF16, i * 8192) for i in range(2)]
    wbtsl = [at(f"w_bt{i}", [128, NKD * 128], BF16, 16384 + i * 8192) for i in range(2)]
    gatesb = [at(f"w_gt{i}", [128, 512], BF16, 32768 + i * 1024) for i in range(2)]
    ygsqsb = [at(f"w_yq{i}", [128, 512], BF16, 34816 + i * 1024) for i in range(2)]

    sa = sb("a_str", [128, 24 * 1024], U8)
    saoff = nc.lookup_mloc(sa).addr

    def ats(name, shape, dtype, off):
        return nc.alloc_sbuf_tensor_at(name, shape, dtype, offset=saoff + off)

    xhalf = [ats(f"s_xh{i}", [128, 2048], F32, i * 8192) for i in range(2)]
    xntt = ats("s_xntt", [128, D], BF16, 16384)
    coefw = ats("c_coefw", [128, NKD * 40], BF16, 0)
    coefbuf = ats("c_coefb", [128, 120], F32, 2560)
    cmaskbuf = ats("c_cmkb", [128, 120], F32, 3072)
    vacc = ats("c_vacc", [40, D], F32, 4096)
    xtok = [ats(f"s_xtok{i}", [128, 512], F32, i * 2048) for i in range(4)]
    outsb = [ats(f"s_osb{i}", [128, 512], F32, 8192 + i * 2048) for i in range(2)]
    outst = [ats(f"s_ost{i}", [128, 512], F32, 12288 + i * 2048) for i in range(4)]
    sjbuf = [ats(f"s_sj{i}", [128, 2048], F32, i * 8192) for i in range(2)]

    csb = sb("d_cs", [128, TR], F32)
    negcs = sb("d_ncs", [128, TR], F32)
    dtsb = sb("d_dt", [128, TR], F32)
    dasb = sb("d_da", [128, 128], F32)
    datsb = sb("d_dat", [128, 4 * 128], F32)
    cstsb = sb("d_cst", [128, 128], F32)
    ncstsb = sb("d_ncst", [128, 128], F32)
    dttsb = sb("d_dtt", [128, 128], F32)
    dectsb = sb("d_dect", [128, 128], F32)
    expltsb = sb("d_explt", [128, 128], F32)
    expgtsb = sb("d_expgt", [128, 4 * 128], F32)
    dechsb = sb("d_dech", [128, 128], F32)
    qsb = sb("d_q", [128, 128], F32)
    ebsb = sb("d_eb", [128, 40], F32)
    bcastsb = [ats(f"d_bc{i}", [128, 128], F32, i * 512) for i in range(4)]
    lmsb = [ats(f"d_lm{i}", [128, 128], BF16, 2048 + i * 256) for i in range(4)]
    sclsb = [ats(f"d_scl{i}", [128, 128], BF16, 3072 + i * 256) for i in range(4)]
    scmsb = [ats(f"d_scm{i}", [128, 128], BF16, 4096 + i * 256) for i in range(2)]
    btsb = [ats(f"d_bt{i}", [128, 128], BF16, 4608 + i * 256) for i in range(2)]
    bdtsb = [ats(f"d_bdt{i}", [128, 128], BF16, 5120 + i * 256) for i in range(4)]
    xdttsb = [ats(f"d_xdtt{i}", [128, 128], BF16, 6144 + i * 256) for i in range(4)]
    rcpsb = [ats(f"d_rcp{i}", [128, 128], BF16, 7168 + i * 256) for i in range(2)]
    qdsb = [sb(f"d_qd{i}", [128, 128], F32) for i in range(2)]
    yofsb = [ats(f"d_yof{i}", [128, 128], F32, 7680 + i * 512) for i in range(2)]
    segsb = [ats(f"d_seg{i}", [128, 128], F32, 8704 + i * 512) for i in range(4)]
    ycsb = [ats(f"d_yc{i}", [128, 128], BF16, 10752 + i * 256) for i in range(2)]
    rstd8 = ats("c_rstd8", [8, 512], F32, 20480)
    rstdrow = ats("c_rsrow", [1, 512], F32, 22528)

    assert nc.sbuf_bytes_remaining >= 0, f"SBUF over by {-nc.sbuf_bytes_remaining}"

    PA = [nc.alloc_psum_tensor(f"pa{i}", [128, 512], F32) for i in range(2)]
    PT = [nc.alloc_psum_tensor(f"ptp{i}", [128, 1024], BF16) for i in range(2)]
    PC = [nc.alloc_psum_tensor(f"pcc{i}", [128, 512], F32) for i in range(2)]
    PY = [nc.alloc_psum_tensor(f"pyy{i}", [128, 512], F32) for i in range(2)]

    p = Pl(nc)
    sDW = p.sem(st, "sDW")
    sDX = p.sem(st, "sDX")
    sPE = p.sem(st, "sPE")
    sDV = p.sem(st, "sDV")
    sAC = p.sem(st, "sAC")
    sGP = p.sem(st, "sGP")
    E = p.emit

    bankw = {}

    def consume(bank, tok):
        if tok is not None:
            bankw.setdefault(bank, []).append(tok)

    def bank_free(bank):
        return bankw.pop(bank, [])

    def gdma(out, in_, waits=()):
        return E("gpsimd", (lambda o, i: lambda g: g.dma_start(o, i))(out, in_), waits, sDW, 16)

    def sdma(out, in_, waits=()):
        return E("sync", (lambda o, i: lambda s_: s_.dma_start(o, i))(out, in_), waits, sDX, 16)

    def mm(bank, out, lhsT, rhs, start, stop, waits=(), inc=None, isT=False):
        w = list(waits) + (bank_free(bank) if start else [])
        fn = (lambda o, l, r, s0, s1, it: lambda t: t.matmul(
            o, l, r, start=s0, stop=s1, is_transpose=it))(out, lhsT, rhs, start, stop, isT)
        return E("tensor", fn, w, inc)

    def tpose(bank, out, in_, ident, waits=(), inc=sPE, start=True, stop=True):
        return mm(bank, out, in_, ident, start, stop, waits, inc, isT=True)

    def vcopy(out, in_, waits=(), inc=sDV):
        return E("vector", (lambda o, i: lambda v: v.tensor_copy(o, i))(out, in_), waits, inc)

    for dst, src in [(ident32, di32), (identbf, dibf), (lt1, dlt1), (on128, don128),
                     (utm, dutm), (on1, don1), (acol, dacol), (dtbc, ddtb),
                     (dcolb, ddcol), (nwcolb, dnwcol), (bbcolb, dbbcol), (selmb, dselm),
                     (convbb, dconvb)]:
        sdma(dst[:, :], src[:, :])
    for g in range(8):
        sdma(gcolb[:, g * 8:(g + 1) * 8], dgcol[g, :, :])
    for k in range(NKD):
        gdma(enctb[:, k * 40:(k + 1) * 40], denct[k * 128:(k + 1) * 128, :])
    tgap0 = E("gpsimd", lambda g: g.memset(gapb[:, :], 0.0), (), sGP)
    E("gpsimd", lambda g: g.memset(zero128[:, :], 0.0), (), sGP)
    E("gpsimd", lambda g: g.memset(vtb[:, :], 0.0), (), sGP)
    tzero = E("gpsimd", lambda g: g.memset(etall[:, :], 0.0), (), sGP)
    tconst = p.tok(sDX)
    tconstw = p.tok(sDW)

    twe = [None] * NKD
    for k in range(NKD):
        wl = [twe[k - 1]] if k >= 1 else []
        t1 = sdma(coefbuf[:, :], dcoef[k * 128:(k + 1) * 128, :], wl)
        t2 = gdma(cmaskbuf[:, :], dcmask[k * 128:(k + 1) * 128, :], wl)

        def mkwe(k=k):
            def go(v):
                v.tensor_mul(coefbuf[:, 0:40], coefbuf[:, 0:40], cmaskbuf[:, 0:40])
                v.tensor_mul(coefbuf[:, 40:80], coefbuf[:, 40:80], cmaskbuf[:, 40:80])
                v.tensor_mul(coefbuf[:, 80:120], coefbuf[:, 80:120], cmaskbuf[:, 80:120])
                v.tensor_add(coefbuf[:, 0:40], coefbuf[:, 0:40], coefbuf[:, 40:80])
                v.tensor_copy(gapb[0:1, 0:1], gapb[0:1, 1:2])
                return v.tensor_add(coefw[:, k * 40:(k + 1) * 40], coefbuf[:, 0:40],
                                    coefbuf[:, 80:120])
            return go
        twe[k] = E("vector", mkwe(), [t1, t2], sDV)

    tvz = E("gpsimd", lambda g: g.memset(vacc[:, :], 0.0), (), sGP)
    tmt = [None] * NKD
    tvo = [None] * NKD
    for o in range(2):
        tmt[o] = gdma(inslab[o % 2][:, :], dmt[o * 128:(o + 1) * 128, :], [])
    for o in range(NKD):
        for js in range(8):
            bank = "PC%d" % (js % 2)
            tk = mm(bank, PC[js % 2][0:40, 0:512], coefw[:, o * 40:(o + 1) * 40],
                    inslab[o % 2][:, js * 512:(js + 1) * 512], True, True,
                    [tmt[o], twe[o]], sPE)

            def mkadd(js=js, bank=bank):
                def go(v):
                    return v.tensor_add(vacc[0:40, js * 512:(js + 1) * 512],
                                        vacc[0:40, js * 512:(js + 1) * 512],
                                        PC[js % 2][0:40, 0:512])
                return go
            ta = E("vector", mkadd(), [tk, tvz], sDV)
            consume(bank, ta)
        tvo[o] = tk
        if o + 2 < NKD:
            tmt[o + 2] = gdma(inslab[o % 2][:, :], dmt[(o + 2) * 128:(o + 3) * 128, :], [tk])
    tvt = E("vector", lambda v: v.tensor_scalar(vtb[0:40, :], vacc[0:40, :], SCALING, None,
                                                ALU.mult), [p.tok(sDV)], sDV)
    tsetup_pe = p.tok(sPE)
    tsetup_dv = p.tok(sDV)

    def ln_tile(xdram, row0, nrow, dst_fm, colstride, dstcol0, extra=(), prev=None):
        pw = []
        if prev is not None:
            pw = [prev[0]] + list(prev[1])
        th = [sdma(xhalf[h][0:nrow, :], xdram[row0:row0 + nrow, h * 2048:(h + 1) * 2048],
                   list(extra) + pw) for h in range(2)]

        def mkst(h):
            def go(v):
                return v.tensor_reduce(statc[0:nrow, h:h + 1], xhalf[h][0:nrow, :], AXL.X, ALU.add)
            return go

        def mksq(h):
            def go(sc):
                return sc.activation(xntt[0:nrow, h * 2048:(h + 1) * 2048], xhalf[h][0:nrow, :],
                                     ACT.Square, accum_out=statc[0:nrow, 2 + h:3 + h])
            return go
        ts0 = E("vector", mkst(0), [th[0]], sDV)
        ts1 = E("vector", mkst(1), [th[1]], sDV)
        tq0 = E("scalar", mksq(0), [th[0]], sAC)
        tq1 = E("scalar", mksq(1), [th[1]], sAC)

        def mkmv(v):
            v.tensor_add(statc[0:nrow, 4:5], statc[0:nrow, 0:1], statc[0:nrow, 1:2])
            v.tensor_add(statc[0:nrow, 5:6], statc[0:nrow, 2:3], statc[0:nrow, 3:4])
            v.tensor_scalar(statc[0:nrow, 4:5], statc[0:nrow, 4:5], 1.0 / D, None, ALU.mult)
            v.tensor_scalar(statc[0:nrow, 5:6], statc[0:nrow, 5:6], 1.0 / D, None, ALU.mult)
            v.tensor_mul(statc[0:nrow, 6:7], statc[0:nrow, 4:5], statc[0:nrow, 4:5])
            v.tensor_copy(gapb[0:1, 0:1], gapb[0:1, 1:2])
            v.tensor_sub(statc[0:nrow, 5:6], statc[0:nrow, 5:6], statc[0:nrow, 6:7])
            v.tensor_copy(gapb[0:1, 0:1], gapb[0:1, 1:2])
            return v.tensor_scalar(statc[0:nrow, 5:6], statc[0:nrow, 5:6], EPS, None, ALU.add)
        tmv = E("vector", mkmv, [ts0, ts1, tq0, tq1], sDV)
        tsq = E("scalar", lambda sc: sc.activation(statc[0:nrow, 7:8], statc[0:nrow, 5:6],
                                                   ACT.Sqrt), [tmv], sAC)

        def mknewt(v):
            v.reciprocal(statc[0:nrow, 8:9], statc[0:nrow, 7:8])
            v.tensor_copy(gapb[0:1, 0:1], gapb[0:1, 1:2])
            v.tensor_mul(statc[0:nrow, 9:10], statc[0:nrow, 8:9], statc[0:nrow, 8:9])
            v.tensor_copy(gapb[0:1, 0:1], gapb[0:1, 1:2])
            v.tensor_mul(statc[0:nrow, 9:10], statc[0:nrow, 9:10], statc[0:nrow, 5:6])
            v.tensor_copy(gapb[0:1, 0:1], gapb[0:1, 1:2])
            v.tensor_scalar(statc[0:nrow, 9:10], statc[0:nrow, 9:10], -0.5, 1.5, ALU.mult, ALU.add)
            v.tensor_copy(gapb[0:1, 0:1], gapb[0:1, 1:2])
            return v.tensor_mul(statc[0:nrow, 8:9], statc[0:nrow, 8:9], statc[0:nrow, 9:10])
        tnw = E("vector", mknewt, [tsq], sDV)
        tn = []
        for h in range(2):
            def mknm(h=h):
                def go(v):
                    return v.tensor_scalar(xntt[0:nrow, h * 2048:(h + 1) * 2048],
                                           xhalf[h][0:nrow, :], statc[0:nrow, 4:5],
                                           statc[0:nrow, 8:9], ALU.subtract, ALU.mult)
                return go
            tn.append(E("vector", mknm(h), [tnw], sDV))
        tlast = None
        tt = None
        for k in range(NKD):
            bank = "PT%d" % (k % 2)
            slot = PT[k % 2][:, 128 * ((k // 2) % 2):128 * ((k // 2) % 2) + nrow]
            tt = tpose(bank, slot, xntt[0:nrow, k * 128:(k + 1) * 128],
                       identbf[0:nrow, 0:nrow], tn)
            te = vcopy(dst_fm[:, k * colstride + dstcol0:k * colstride + dstcol0 + nrow],
                       slot, [tt])
            consume(bank, te)
            tlast = te
        return (tt, [tn[0], tn[1], tq0, tq1], tlast)

    def round_r(r, prev_end=None):
        xdram = dxr[r]
        lnprev = None
        for tt_ in range(4):
            ex = [tconst, tsetup_pe, tsetup_dv] if (r == 0 and tt_ == 0) else []
            if prev_end is not None and tt_ == 0:
                ex = ex + list(prev_end)
            lnprev = ln_tile(xdram, 3 + tt_ * 128, 128, xnb, TR, tt_ * 128,
                             ex, prev=lnprev)
        ln_tile(xdram, 0, 3, xnhalo, 3, 0, prev=lnprev)
        txn = p.tok(sDV)

        if debug:
            for k in range(NKD):
                sdma(dbg["xn"][r, k * 128:(k + 1) * 128, :], xnb[:, k * TR:(k + 1) * TR], [txn])
        if stage < 2:
            return

        def ct_info(ci):
            if ci == 0:
                return ("dt", PROJ // 128 - 1, None, None)
            if ci <= 16:
                gg = ci - 1
                half, g = gg // 8, gg % 8
                cti = (2 * INTER + half * NG * 128 + g * 128) // 128
                return ("bc", cti, cfmb if half else bfmb, g)
            k = ci - 17
            return ("x", (INTER + k * 128) // 128, xfm, k)

        tslab = [None] * NCT
        tcdsl = [None] * NCT
        tmmend = [None] * NCT
        tconvend = [None] * NCT
        tevac = [None] * NCT

        def slabdma(ci, wl, wl2):
            kind, cti, dst, idx = ct_info(ci)
            tslab[ci] = gdma(inslab[ci % 2][:, :], dwti[cti, :, :], wl)
            if kind != "dt":
                tcdsl[ci] = gdma(convdsl[ci % 2][:, :], dconvd[cti - INTER // 128, :, :], wl2)
        pe_guard = [tsetup_pe] + (list(prev_end) if prev_end is not None else [])
        slabdma(0, pe_guard, [])
        slabdma(1, pe_guard, [])
        for ci in range(NCT):
            kind, cti, dst, idx = ct_info(ci)
            bank, ps = "PA%d" % (ci % 2), PA[ci % 2]
            hbank = "PY%d" % (ci % 2)
            tk = None
            for k in range(NKD):
                tk = mm(bank, ps[:, :], inslab[ci % 2][:, k * 128:(k + 1) * 128],
                        xnb[:, k * TR:(k + 1) * TR], k == 0, k == NKD - 1,
                        [tslab[ci], txn], sPE if k == NKD - 1 else None)
                if kind != "dt":
                    hinc = sPE if k == NKD - 1 else None
                    thl = mm(hbank, PY[ci % 2][:, 0:3],
                             inslab[ci % 2][:, k * 128:(k + 1) * 128],
                             xnhalo[:, k * 3:(k + 1) * 3], k == 0, k == NKD - 1, [], hinc)
            tmmend[ci] = tk
            if kind == "dt":
                def mkz(ci=ci):
                    def go(v):
                        return v.tensor_scalar(dtsb[:, :], PA[ci % 2][:, :], dtbc[:, 0:1],
                                               None, ALU.add)
                    return go
                tz = E("vector", mkz(), [tmmend[ci]], sDV)
                consume(bank, tz)

                def mkabs(sc):
                    return sc.activation(negcs[:, :], dtsb[:, :], ACT.Abs)

                def mkexp(sc):
                    return sc.activation(negcs[:, :], negcs[:, :], ACT.Exp, scale=-1.0)

                def mkln(sc):
                    return sc.activation(negcs[:, :], negcs[:, :], ACT.Ln, bias=1.0)
                ta1 = E("scalar", mkabs, [tz], sAC)
                ta2 = E("scalar", mkexp, [ta1], sAC)
                ta3 = E("scalar", mkln, [ta2], sAC)

                def mkfin(v):
                    v.tensor_scalar(dtsb[:, :], dtsb[:, :], 0.0, None, ALU.max)
                    v.tensor_copy(gapb[0:1, 0:1], gapb[0:1, 1:2])
                    v.tensor_add(dtsb[:, :], dtsb[:, :], negcs[:, :])
                    v.tensor_copy(gapb[0:1, 0:1], gapb[0:1, 1:2])
                    return v.tensor_scalar(dtsb[:, :], dtsb[:, :], DT_MIN, DT_MAX,
                                           ALU.max, ALU.min)
                tevac[ci] = E("vector", mkfin, [ta3], sDV)
            else:
                wr = [tmmend[ci]] + ([tconvend[ci - 2]] if ci >= 2 else [])

                def mkraw(ci=ci):
                    def go(v):
                        return v.tensor_copy(rawsl[ci % 2][:, 3:3 + 512], PA[ci % 2][:, :])
                    return go
                tr0 = E("vector", mkraw(), wr, sDV)
                consume(bank, tr0)

                def mkrh(ci=ci):
                    def go(v):
                        return v.tensor_copy(rawsl[ci % 2][:, 0:3], PY[ci % 2][:, 0:3])
                    return go
                tr1 = E("vector", mkrh(), [thl], sDV)
                consume(hbank, tr1)
                cbank = "PC%d" % (ci % 2)
                tc = None
                for j in range(KTAP):
                    tc = mm(cbank, PC[ci % 2][:, :], convdsl[ci % 2][:, j * 128:(j + 1) * 128],
                            rawsl[ci % 2][:, j:j + 512], j == 0, j == KTAP - 1,
                            [tr0, tr1, tcdsl[ci]], sPE if j == KTAP - 1 else None)
                tconvend[ci] = tc
                if debug and ci in (17, 79):
                    sdma(dbg["raw"][0 if ci == 17 else 1, :, 0:515], rawsl[ci % 2][:, 0:515],
                         [tr0, tr1])

                def mksilu(ci=ci, dst=dst, idx=idx, cti=cti):
                    def go(sc):
                        return sc.activation(dst[:, idx * TR:(idx + 1) * TR], PC[ci % 2][:, :],
                                             ACT.Silu, bias=convbb[:, cti - 64:cti - 63])
                    return go
                tevac[ci] = E("scalar", mksilu(), [tc], sAC)
                consume(cbank, tevac[ci])
            if ci + 2 < NCT:
                slabdma(ci + 2, [tmmend[ci]], [tconvend[ci]] if tconvend[ci] else [])
        tph1ac = p.tok(sAC)
        tdtok = tevac[0]

        if debug:
            for k in range(NXT):
                sdma(dbg["xc"][r, k * 128:(k + 1) * 128, :], xfm[:, k * TR:(k + 1) * TR],
                     [tevac[17 + k]])
            for g in range(8):
                sdma(dbg["bc"][r, g * 128:(g + 1) * 128, :], bfmb[:, g * TR:(g + 1) * TR],
                     [tevac[1 + g]])
                sdma(dbg["bc"][r, 1024 + g * 128:1024 + (g + 1) * 128, :],
                     cfmb[:, g * TR:(g + 1) * TR], [tevac[9 + g]])
            sdma(dbg["dt"][r, :, :], dtsb[:, :], [tdtok])
        if stage < 3:
            return

        twl = [p.tok(sDW), tmmend[NCT - 1], tconvend[NCT - 1]]
        tRinit = E("gpsimd", lambda g: g.memset(rbuf[:, :], 0.0), twl, sGP)
        ptok = {}

        for c in range(NCHUNK):
            c0 = c * CHL

            def mkda(c=c):
                def go(v):
                    return v.tensor_scalar(dasb[:, :], dtsb[:, c * CHL:(c + 1) * CHL],
                                           acol[:, 0:1], None, ALU.mult)
                return go
            tda = E("vector", mkda(), [tdtok], sDV)
            ttda = tpose("PC0", PC[0][:, 0:128], dasb[:, :], ident32[:, :], [tda])
            tdat = vcopy(datsb[:, c * 128:(c + 1) * 128], PC[0][:, 0:128], [ttda])
            consume("PC0", tdat)
            tcst = None
            for cc in range(c + 1):
                tcst = mm("PC1", PC[1][:, 0:128], (lt1 if cc == c else on128)[:, :],
                          datsb[:, cc * 128:(cc + 1) * 128], cc == 0, cc == c,
                          [tdat], sPE if cc == c else None)

            def mkcst(v):
                v.tensor_copy(cstsb[:, :], PC[1][:, 0:128])
                return v.tensor_scalar(ncstsb[:, :], PC[1][:, 0:128], -1.0, None, ALU.mult)
            tcste = E("vector", mkcst, [tcst], sDV)
            consume("PC1", tcste)
            tcs1 = tpose("PC0", PC[0][:, 0:128], cstsb[:, :], ident32[:, :], [tcste])

            def mkcs(c=c):
                def go(v):
                    v.tensor_copy(csb[:, c * CHL:(c + 1) * CHL], PC[0][:, 0:128])
                    return v.tensor_scalar(negcs[:, c * CHL:(c + 1) * CHL], PC[0][:, 0:128],
                                           -1.0, None, ALU.mult)
                return go
            tcse = E("vector", mkcs(), [tcs1], sDV)
            consume("PC0", tcse)

            tdt1 = tpose("PC1", PC[1][:, 0:128], dtsb[:, c * CHL:(c + 1) * CHL], ident32[:, :],
                         [tdtok, tcse])
            tdtt = vcopy(dttsb[:, :], PC[1][:, 0:128], [tdt1])
            consume("PC1", tdtt)

            def mkdec(c=c):
                def go(sc):
                    return sc.activation(dechsb[:, :], csb[:, c * CHL:(c + 1) * CHL], ACT.Exp,
                                         bias=csb[:, (c + 1) * CHL - 1:(c + 1) * CHL], scale=-1.0)
                return go
            tdec = E("scalar", mkdec(), [tcse], sAC)
            tdec1 = tpose("PC0", PC[0][:, 0:128], dechsb[:, :], ident32[:, :], [tdec])
            tdect = vcopy(dectsb[:, :], PC[0][:, 0:128], [tdec1])
            consume("PC0", tdect)

            def mkexpl(c=c):
                def go(sc):
                    if c == 0:
                        return sc.activation(dechsb[:, :], csb[:, 0:CHL], ACT.Exp)
                    return sc.activation(dechsb[:, :], csb[:, c * CHL:(c + 1) * CHL], ACT.Exp,
                                         bias=negcs[:, c * CHL - 1:c * CHL])
                return go
            texl = E("scalar", mkexpl(), [tcse, tdec1], sAC)
            texl1 = tpose("PC1", PC[1][:, 0:128], dechsb[:, :], ident32[:, :], [texl])
            texlt = vcopy(expltsb[:, :], PC[1][:, 0:128], [texl1])
            consume("PC1", texlt)

            def mkexpg(c=c):
                def go(sc):
                    return sc.activation(dechsb[:, :], csb[:, c * CHL:(c + 1) * CHL], ACT.Exp)
                return go
            texg = E("scalar", mkexpg(), [texl1], sAC)
            texg1 = tpose("PC0", PC[0][:, 0:128], dechsb[:, :], ident32[:, :], [texg])
            texgt = vcopy(expgtsb[:, c * 128:(c + 1) * 128], PC[0][:, 0:128], [texg1])
            consume("PC0", texgt)

            def mkcd(c=c):
                def go(sc):
                    if c == 0:
                        return sc.activation(statc[:, 10:11], csb[:, CHL - 1:CHL], ACT.Exp)
                    return sc.activation(statc[:, 10:11], csb[:, (c + 1) * CHL - 1:(c + 1) * CHL],
                                         ACT.Exp, bias=negcs[:, c * CHL - 1:c * CHL])
                return go
            tcd = E("scalar", mkcd(), [tcse], sAC)
            tcd1 = tpose("PC1", PC[1][0:1, 0:128], statc[:, 10:11], ident32[:, :], [tcd])
            tcdr = vcopy(cdrowb[0:1, :], PC[1][0:1, 0:128], [tcd1])
            consume("PC1", tcdr)
            tq1 = mm("PC0", PC[0][:, 0:128], on1[0:1, :], cdrowb[0:1, :], True, True, [tcdr], sPE)
            tq = vcopy(qsb[:, :], PC[0][:, 0:128], [tq1])
            consume("PC0", tq)

            for pr in range(NXT):
                pi = c * NXT + pr
                g = pr // 8
                par = pi % 2
                tks = {}
                ptok[pi] = tks
                prev2 = ptok.get(pi - 2, {})
                prev4 = ptok.get(pi - 4, {})

                if pr % 8 == 0:
                    gb = "PC%d" % (g % 2)
                    tsc1 = mm(gb, PC[g % 2][:, 128:256], bfmb[:, g * TR + c0:g * TR + c0 + CHL],
                              cfmb[:, g * TR + c0:g * TR + c0 + CHL], True, True,
                              [tevac[1 + g], tevac[9 + g]], sPE)

                    def mkscm(g=g):
                        def go(v):
                            return v.tensor_tensor(scmsb[g % 2][:, :], PC[g % 2][:, 128:256],
                                                   utm[:, :], ALU.mult)
                        return go
                    tscm_g = E("vector", mkscm(), [tsc1], sDV)
                    consume(gb, tscm_g)
                    tb1 = tpose("PT%d" % (g % 2), PT[g % 2][:, 512:640],
                                bfmb[:, g * TR + c0:g * TR + c0 + CHL], identbf[:, :],
                                [tevac[1 + g]])
                    tbt_g = vcopy(btsb[g % 2][:, :], PT[g % 2][:, 512:640], [tb1])
                    consume("PT%d" % (g % 2), tbt_g)
                    if g == 0 or True:
                        pass
                    grp_tok = (tscm_g, tbt_g)
                tscm_g, tbt_g = grp_tok

                xbank = "PT%d" % (pi % 2)
                xslot = PT[pi % 2][:, 128 * ((pi // 2) % 2):128 * ((pi // 2) % 2) + 128]
                ttx = tpose(xbank, xslot, xfm[:, pr * TR + c0:pr * TR + c0 + CHL], identbf[:, :],
                            [tevac[17 + pr]])

                def mkxdt(pi=pi, xslot=xslot, pr=pr):
                    def go(v):
                        v.tensor_scalar(xdttsb[pi % 4][:, 0:64], xslot[:, 0:64],
                                        dttsb[:, 2 * pr:2 * pr + 1], None, ALU.mult)
                        return v.tensor_scalar(xdttsb[pi % 4][:, 64:128], xslot[:, 64:128],
                                               dttsb[:, 2 * pr + 1:2 * pr + 2], None, ALU.mult)
                    return go
                txdt = E("vector", mkxdt(), [ttx, tdtt, prev4.get("states")], sDV)
                consume(xbank, txdt)

                tscl = [None, None]
                for hi in range(2):
                    h = 2 * pr + hi
                    m = (2 * pi + hi) % 4
                    rb = "PC%d" % ((2 * pi + hi) % 2)
                    rbp = PC[(2 * pi + hi) % 2]

                    def mkbc(h=h, m=m):
                        def go(sc):
                            return sc.activation(bcastsb[m][:, :], zero128[:, :], ACT.Identity,
                                                 bias=cstsb[:, h:h + 1], scale=0.0)
                        return go
                    tbc = E("scalar", mkbc(), [tcste, prev2.get("rep%d" % hi)], sAC)
                    trep = tpose(rb, rbp[:, 256:384], bcastsb[m][:, :], ident32[:, :], [tbc])
                    tks["rep%d" % hi] = trep

                    def mkseg(h=h, m=m, rbp=rbp):
                        def go(v):
                            return v.tensor_scalar(segsb[m][:, :], rbp[:, 256:384],
                                                   ncstsb[:, h:h + 1], 0.0, ALU.add, ALU.min)
                        return go
                    tseg = E("vector", mkseg(), [trep, prev2.get("lm%d" % hi)], sDV)
                    consume(rb, tseg)

                    def mklm(m=m):
                        def go(sc):
                            return sc.activation(lmsb[m][:, :], segsb[m][:, :], ACT.Exp)
                        return go
                    tlm = E("scalar", mklm(), [tseg, prev2.get("scl%d" % hi)], sAC)
                    tks["lm%d" % hi] = tlm

                    def mkscl(m=m, g=g):
                        def go(v):
                            return v.tensor_mul(sclsb[m][:, :], lmsb[m][:, :], scmsb[g % 2][:, :])
                        return go
                    tscl[hi] = E("vector", mkscl(), [tlm, tscm_g, prev2.get("ydiagend")], sDV)
                    tks["scl%d" % hi] = tscl[hi]

                ybank = "PY%d" % (pi % 2)
                yps = PY[pi % 2]
                m0, m1 = (2 * pi) % 4, (2 * pi + 1) % 4
                ty1 = mm(ybank, yps[:, 0:64], sclsb[m0][:, :], xdttsb[pi % 4][:, 0:64],
                         True, False, [txdt, tscl[0], prev2.get("ycomb")])
                ty2 = mm(ybank, yps[:, 64:128], sclsb[m1][:, :], xdttsb[pi % 4][:, 64:128],
                         False, True, [tscl[1]], sPE)
                tks["ydiagend"] = ty2

                tbd = [None, None]
                for hi in range(2):
                    h = 2 * pr + hi

                    def mkbdt(h=h, m=(2 * pi + hi) % 4, g=g):
                        def go(v):
                            return v.tensor_scalar(bdtsb[m][:, :], btsb[g % 2][:, :],
                                                   dectsb[:, h:h + 1], None, ALU.mult)
                        return go
                    tbd[hi] = E("vector", mkbdt(), [tbt_g, tdect, prev2.get("states")], sDV)
                sbank = "PA%d" % par
                sps = PA[par]
                tst1 = mm(sbank, sps[:, 0:64], bdtsb[m0][:, :], xdttsb[pi % 4][:, 0:64],
                          True, False, [tbd[0], tbd[1], txdt, tscl[1]])
                tst2 = mm(sbank, sps[:, 64:128], bdtsb[m1][:, :], xdttsb[pi % 4][:, 64:128],
                          False, True, [], sPE)
                tks["states"] = tst2

                if c > 0:
                    trcp = E("vector", (lambda pr=pr, pi=pi: lambda v: v.tensor_copy(
                        rcpsb[pi % 2][:, :], rbuf[:, pr * 128:(pr + 1) * 128]))(),
                        [prev2.get("yoffmm"), ptok.get(pi - 64, {}).get("walk")], sDV)
                    tyo1 = mm(sbank, sps[:, 128:256], cfmb[:, g * TR + c0:g * TR + c0 + CHL],
                              rcpsb[pi % 2][:, :], True, True, [trcp, tevac[9 + g]], sPE)
                    tks["yoffmm"] = tyo1

                    def mkyoe(pi=pi, pr=pr, sps=sps):
                        def go(sc):
                            sc.activation(yofsb[pi % 2][:, 0:64], sps[:, 128:192], ACT.Identity,
                                          scale=expltsb[:, 2 * pr:2 * pr + 1])
                            return sc.activation(yofsb[pi % 2][:, 64:128], sps[:, 192:256],
                                                 ACT.Identity,
                                                 scale=expltsb[:, 2 * pr + 1:2 * pr + 2])
                        return go
                    tyoe = E("scalar", mkyoe(), [tyo1, texlt, prev2.get("ycomb")], sAC)
                    consume(sbank, tyoe)

                def mkwalk(pr=pr, par=par, sps=sps):
                    def go(v):
                        v.scalar_tensor_tensor(rbuf[:, pr * 128:pr * 128 + 64],
                                               rbuf[:, pr * 128:pr * 128 + 64],
                                               qsb[:, 2 * pr:2 * pr + 1], sps[:, 0:64],
                                               ALU.mult, ALU.add)
                        return v.scalar_tensor_tensor(rbuf[:, pr * 128 + 64:(pr + 1) * 128],
                                                      rbuf[:, pr * 128 + 64:(pr + 1) * 128],
                                                      qsb[:, 2 * pr + 1:2 * pr + 2],
                                                      sps[:, 64:128], ALU.mult, ALU.add)
                    return go
                wk = [tst2, tq, tRinit]
                if c > 0:
                    wk.append(tks["yoffmm"])
                    wk.append(tyoe)
                twalk = E("vector", mkwalk(), wk, sDV)
                consume(sbank, twalk)
                tks["walk"] = twalk

                if c > 0:
                    def mkcomb(pi=pi, yps=yps):
                        def go(v):
                            return v.tensor_add(ycsb[pi % 2][:, :], yps[:, 0:128],
                                                yofsb[pi % 2][:, :])
                        return go
                    tcomb = E("vector", mkcomb(), [ty2, tyoe, prev2.get("ytr")], sDV)
                else:
                    def mkcomb(pi=pi, yps=yps):
                        def go(v):
                            return v.tensor_copy(ycsb[pi % 2][:, :], yps[:, 0:128])
                        return go
                    tcomb = E("vector", mkcomb(), [ty2, prev2.get("ytr")], sDV)
                consume(ybank, tcomb)
                tks["ycomb"] = tcomb
                ytslot = PT[pi % 2][:, 640:768]
                tytr = tpose("PT%d" % (pi % 2), ytslot, ycsb[pi % 2][:, :], identbf[:, :],
                             [tcomb])
                tks["ytr"] = tytr

                def mkyev(pr=pr, pi=pi, c0=c0, ytslot=ytslot):
                    def go(v):
                        return v.scalar_tensor_tensor(
                            xfm[:, pr * TR + c0:pr * TR + c0 + CHL],
                            xfm[:, pr * TR + c0:pr * TR + c0 + CHL],
                            dcolb[:, pr:pr + 1], ytslot, ALU.mult, ALU.add)
                    return go
                tyev = E("vector", mkyev(), [tytr, txdt], sDV)
                consume("PT%d" % (pi % 2), tyev)
                tks["yevac"] = tyev

        tlastwalk = ptok[NCHUNK * NXT - 1]["walk"]
        tpay = sdma(dagin[r][:, 0:INTER], rbuf[:, :], [tlastwalk])

        def mkdtot(sc):
            return sc.activation(statc[:, 11:12], csb[:, TR - 1:TR], ACT.Exp)
        tdt_ = E("scalar", mkdtot, [p.tok(sDV)], sAC)
        tdr1 = tpose("PC1", PC[1][0:1, 0:128], statc[:, 11:12], ident32[:, :], [tdt_])
        tdrow = vcopy(drowb[0:1, :], PC[1][0:1, 0:128], [tdr1])
        consume("PC1", tdrow)
        tpz = sdma(dagin[r][:, INTER:INTER + 128], zero128[:, :], [tconst])
        tpay2 = sdma(dagin[r][0:1, INTER:INTER + 128], drowb[0:1, :], [tdrow, tpz])

        tdump = []
        if debug and stage == 3:
            for k in range(NXT):
                sdma(dbg["ya"][r, k * 128:(k + 1) * 128, :], xfm[:, k * TR:(k + 1) * TR],
                     [ptok[(NCHUNK - 1) * NXT + k]["yevac"]])
            tdump.append(sdma(dbg["pay"][r, :, 0:INTER], rbuf[:, :], [tlastwalk]))
            sdma(dbg["pay"][r, 0:1, INTER:INTER + 128], drowb[0:1, :], [tdrow])
            sdma(dbg["cs"][r, :, :], csb[:, :], [p.tok(sDV)])
        if stage < 4:
            return None

        tag = E("gpsimd", (lambda r=r: lambda g: g.collective_compute(
            "AllGather", ALU.bypass, replica_groups=[[0, 1, 2, 3], [4, 5, 6, 7]],
            ins=[dagin[r].ap().opt()], outs=[dagout[r].ap().opt()]))(),
            [tpay, tpay2], sGP)

        for ts_ in range(4):
            te_mm = None
            for k in range(NKD):
                te_mm = mm("PC%d" % (ts_ % 2), PC[ts_ % 2][:, 0:40],
                           xnb[:, k * TR + ts_ * 128:k * TR + (ts_ + 1) * 128],
                           enctb[:, k * 40:(k + 1) * 40], k == 0, k == NKD - 1,
                           [txn], sPE if k == NKD - 1 else None)
            tecp = vcopy(ebsb[:, :], PC[ts_ % 2][:, 0:40], [te_mm])
            consume("PC%d" % (ts_ % 2), tecp)
            tet = tpose("PC%d" % ((ts_ + 1) % 2), PC[(ts_ + 1) % 2][0:40, 0:128],
                        ebsb[:, :], ident32[:, :], [tecp])
            teev = vcopy(etall[0:40, ts_ * 128:(ts_ + 1) * 128],
                         PC[(ts_ + 1) % 2][0:40, 0:128], [tet])
            consume("PC%d" % ((ts_ + 1) % 2), teev)
        tet_done = p.tok(sDV)

        def combine_pass2(masked, seed_from_dram, out_extra_wait):
            jrange = list(range(3)) if masked else list(range(4))
            tqd = [None] * 4
            if seed_from_dram:
                tseed = sdma(ainbuf[:, :], dst2048[:, :], list(out_extra_wait) + [tag])
            else:
                tseed = E("gpsimd", (lambda: lambda g: g.memset(ainbuf[:, :], 0.0))(),
                          list(out_extra_wait) + [tag], sGP)
            for j in jrange:
                tdr = sdma(drowb[0:1, :], dagout[r][128 * j:128 * j + 1, INTER:INTER + 128],
                           [tag] + ([tqd[j - 2]] if j >= 2 else []))
                tqm = mm("PC%d" % (j % 2), PC[j % 2][:, 0:128], on1[0:1, :], drowb[0:1, :],
                         True, True, [tdr], sPE)

                def mkqd(j=j, masked=masked):
                    def go(v):
                        if masked:
                            return v.tensor_scalar(qdsb[j % 2][:, :], PC[j % 2][:, 0:128],
                                                   selmb[:, j:j + 1], selmb[:, 4 + j:5 + j],
                                                   ALU.mult, ALU.add)
                        return v.tensor_copy(qdsb[j % 2][:, :], PC[j % 2][:, 0:128])
                    return go
                tqd[j] = E("vector", mkqd(), [tqm], sDV)
                consume("PC%d" % (j % 2), tqd[j])
            tlast = None
            tprevq = {}
            tquarter = [None] * 4
            for quarter in range(4):
                for ji, j in enumerate(jrange):
                    qi = (quarter * len(jrange) + ji) % 2
                    tsj_new = sdma(sjbuf[qi][:, :],
                                   dagout[r][128 * j:128 * (j + 1),
                                             quarter * 2048:(quarter + 1) * 2048],
                                   [tag] + ([tprevq[qi]] if qi in tprevq else []))
                    tscl_s = None
                    if masked:
                        def mksm(j=j, qi=qi):
                            def go(v):
                                return v.tensor_scalar(sjbuf[qi][:, :], sjbuf[qi][:, :],
                                                       selmb[:, j:j + 1], None, ALU.mult)
                            return go
                        tscl_s = E("vector", mksm(), [tsj_new], sDV)
                    for hb in range(32):
                        h = quarter * 32 + hb

                        def mkc(j=j, qi=qi, h=h, hb=hb):
                            def go(v):
                                return v.scalar_tensor_tensor(
                                    ainbuf[:, h * 64:(h + 1) * 64],
                                    ainbuf[:, h * 64:(h + 1) * 64],
                                    qdsb[j % 2][:, h:h + 1],
                                    sjbuf[qi][:, hb * 64:(hb + 1) * 64],
                                    ALU.mult, ALU.add)
                            return go
                        tlast = E("vector", mkc(), [tsj_new, tscl_s, tqd[j], tseed], sDV)
                    tprevq[qi] = tlast
                tquarter[quarter] = tlast
            return tlast, tquarter

        textra = list(tdump)
        if r == 0:
            tfull, _ = combine_pass2(False, False, textra + [tpay])
            tst_dma = sdma(dst2048[:, :], ainbuf[:, :], [tfull])
            tain, tainq = combine_pass2(True, False, [tst_dma])
        else:
            tain, tainq = combine_pass2(True, True, textra + [tpay])

        tgsl = [None] * NXT
        tgmm = [None] * NXT
        tgate = [None] * NXT
        tsq = [None] * NXT
        tnmm = [None] * NXT
        GLAG = 33

        def gslab(kk, wl):
            tgsl[kk] = gdma(inslab[kk % 2][:, :], dwti[kk, :, :], wl)

        def gate_tile(kk):
            bank, ps = "PA%d" % (kk % 2), PA[kk % 2]
            if kk == 0:
                gslab(0, [btok[15 * NCHUNK + 3]["acp"]])
                gslab(1, [btok[31 * NCHUNK + 3]["acp"]])
            tk = None
            for k in range(NKD):
                tk = mm(bank, ps[:, :], inslab[kk % 2][:, k * 128:(k + 1) * 128],
                        xnb[:, k * TR:(k + 1) * TR], k == 0, k == NKD - 1,
                        [tgsl[kk]], sPE if k == NKD - 1 else None)
            tgmm[kk] = tk

            def mkgs(kk=kk):
                def go(sc):
                    return sc.activation(gatesb[kk % 2][:, :], PA[kk % 2][:, :], ACT.Silu)
                return go
            wg = [tk] + ([tgate[kk - 2]] if kk >= 2 else [])
            tsilu = E("scalar", mkgs(), wg, sAC)
            consume(bank, tsilu)

            def mkgm(kk=kk):
                def go(v):
                    return v.tensor_mul(xfm[:, kk * TR:(kk + 1) * TR],
                                        xfm[:, kk * TR:(kk + 1) * TR],
                                        gatesb[kk % 2][:, :])
                return go
            tgate[kk] = E("vector", mkgm(), [tsilu, btok[kk * NCHUNK + 3]["badd"]], sDV)

            def mksqg(kk=kk):
                def go(sc):
                    return sc.activation(ygsqsb[kk % 2][:, :], xfm[:, kk * TR:(kk + 1) * TR],
                                         ACT.Square)
                return go
            ws = [tgate[kk]] + ([tnmm[kk - 2]] if kk >= 2 else [])
            tsq[kk] = E("scalar", mksqg(), ws, sAC)
            tnmm[kk] = mm("PY0", PY[0][0:8, 0:512], gcolb[:, (kk // 8) * 8:(kk // 8) * 8 + 8],
                          ygsqsb[kk % 2][:, :], kk == 0, kk == NXT - 1, [tsq[kk]],
                          sPE if kk == NXT - 1 else None)
            if kk + 2 < NXT:
                gslab(kk + 2, [tgmm[kk]])

        btok = {}
        for pr in range(NXT):
            for c in range(NCHUNK):
                c0 = c * CHL
                pib = pr * NCHUNK + c
                g = pr // 8
                par = pib % 2
                prevb = btok.get(pib - 2, {})
                tks2 = {}
                btok[pib] = tks2

                def mkacp(pr=pr, pib=pib):
                    def go(v):
                        return v.tensor_copy(rcpsb[pib % 2][:, :],
                                             rbuf[:, pr * 128:(pr + 1) * 128])
                    return go
                tacp = E("vector", mkacp(), [tainq[pr // 16], prevb.get("bmm")], sDV)
                tks2["acp"] = tacp
                tbmm = mm("PC%d" % par, PC[par][:, 128:256],
                          cfmb[:, g * TR + c0:g * TR + c0 + CHL],
                          rcpsb[pib % 2][:, :], True, True,
                          [tacp, tevac[9 + g]], sPE)
                tks2["bmm"] = tbmm

                def mkbsc(pib=pib, pr=pr, par=par, c=c):
                    def go(v):
                        v.tensor_scalar(yofsb[pib % 2][:, 0:64], PC[par][:, 128:192],
                                        expgtsb[:, c * 128 + 2 * pr:c * 128 + 2 * pr + 1],
                                        None, ALU.mult)
                        return v.tensor_scalar(yofsb[pib % 2][:, 64:128], PC[par][:, 192:256],
                                               expgtsb[:, c * 128 + 2 * pr + 1:c * 128 + 2 * pr + 2],
                                               None, ALU.mult)
                    return go
                tbsc = E("vector", mkbsc(), [tbmm, prevb.get("btr")], sDV)
                consume("PC%d" % par, tbsc)
                tbtr = tpose("PC%d" % par, PC[par][:, 256:384], yofsb[pib % 2][:, :],
                             ident32[:, :], [tbsc])
                tks2["btr"] = tbtr

                def mkbadd(pr=pr, c0=c0, par=par):
                    def go(v):
                        return v.tensor_add(xfm[:, pr * TR + c0:pr * TR + c0 + CHL],
                                            xfm[:, pr * TR + c0:pr * TR + c0 + CHL],
                                            PC[par][:, 256:384])
                    return go
                tbadd = E("vector", mkbadd(),
                          [tbtr, ptok[c * NXT + pr]["yevac"]], sDV)
                consume("PC%d" % par, tbadd)
                tks2["badd"] = tbadd
            if pr >= GLAG:
                gate_tile(pr - GLAG)
        tphB = p.tok(sDV)

        if debug:
            for k in range(NXT):
                sdma(dbg["yb"][r, k * 128:(k + 1) * 128, :], xfm[:, k * TR:(k + 1) * TR],
                     [btok[k * NCHUNK + NCHUNK - 1]["badd"]])
        if stage < 5:
            return None

        for kk in range(NXT - GLAG, NXT):
            gate_tile(kk)

        def mkms(v):
            return v.tensor_scalar(rstd8[:, :], PY[0][0:8, 0:512], 1.0 / 1024, EPS,
                                   ALU.mult, ALU.add)
        tms = E("vector", mkms, [tnmm[NXT - 1]], sDV)
        consume("PY0", tms)
        tsq8 = E("scalar", lambda sc: sc.activation(rstd8[:, :], rstd8[:, :], ACT.Sqrt),
                 [tms], sAC)

        def mkrec(v):
            return v.reciprocal(rstd8[:, :], rstd8[:, :])
        trec = E("vector", mkrec, [tsq8], sDV)

        tscale = [None] * NXT
        tgrep = None
        for gg in range(8):
            trow = sdma(rstdrow[0:1, :], rstd8[gg:gg + 1, :],
                        [trec] + ([tgrep] if tgrep else []))
            tgrep_mm = mm("PC0", PC[0][:, 0:512], on1[0:1, :], rstdrow[0:1, :], True, True,
                          [trow], sPE)
            for kk in range(gg * 8, gg * 8 + 8):
                def mksc2(kk=kk):
                    def go(v):
                        return v.scalar_tensor_tensor(xfm[:, kk * TR:(kk + 1) * TR],
                                                      xfm[:, kk * TR:(kk + 1) * TR],
                                                      nwcolb[:, kk:kk + 1],
                                                      PC[0][:, 0:512], ALU.mult, ALU.mult)
                    return go
                tscale[kk] = E("vector", mksc2(), [tgrep_mm, tsq[kk]], sDV)
                consume("PC0", tscale[kk])
            tgrep = tscale[gg * 8 + 7]

        if debug:
            for k in range(NXT):
                sdma(dbg["ygn"][r, k * 128:(k + 1) * 128, :], xfm[:, k * TR:(k + 1) * TR],
                     [tscale[k]])
        if stage < 6:
            return None

        twsl = [None] * (2 * NKD)
        twb = [None] * NKD
        tdmm = [None] * NKD
        tdev = [None] * NKD
        txk = {}
        tflush = {}
        tadds = {}

        def wdma(dtl, half, wl):
            twsl[2 * dtl + half] = gdma(wotsl[half][:, :],
                                        dwto[dtl, :, half * 4096:(half + 1) * 4096], wl)

        def wbdma(dtl, wl):
            twb[dtl] = gdma(wbtsl[dtl % 2][:, :], dwbt[dtl, :, :], wl)

        wdma(0, 0, [tgmm[NXT - 2]])
        wdma(0, 1, [tgmm[NXT - 1]])
        wbdma(0, [tphB])
        wbdma(1, [tphB])

        def xtdma(dg, ts_, wl):
            txk[(dg, ts_)] = sdma(xtok[ts_][:, :],
                                  xdram[3 + ts_ * 128:3 + (ts_ + 1) * 128,
                                        dg * 512:(dg + 1) * 512], wl)
        for ts_ in range(4):
            xtdma(0, ts_, [tphB])

        for dtl in range(NKD):
            bank, ps = "PA%d" % (dtl % 2), PA[dtl % 2]
            dg = dtl // 4
            tk = None
            for k in range(NKD):
                tk = mm(bank, ps[:, :], wbtsl[dtl % 2][:, k * 128:(k + 1) * 128],
                        xnb[:, k * TR:(k + 1) * TR], k == 0, False,
                        [twb[dtl]], None)
            tk = mm(bank, ps[:, :], vtb[0:40, dtl * 128:(dtl + 1) * 128], etall[0:40, :],
                    False, False, [tet_done], None)
            for half in range(2):
                for k2 in range(NKD):
                    kk = half * NKD + k2
                    last = (half == 1 and k2 == NKD - 1)
                    inc2 = sPE if (last or k2 == NKD - 1) else None
                    tk = mm(bank, ps[:, :],
                            wotsl[half][:, k2 * 128:(k2 + 1) * 128],
                            xfm[:, kk * TR:(kk + 1) * TR], False, last,
                            [twsl[2 * dtl + half], tscale[kk]], inc2)
                if half == 0 and dtl + 1 < NKD:
                    wdma(dtl + 1, 0, [tk])
            tdmm[dtl] = tk
            if dtl + 2 < NKD:
                wbdma(dtl + 2, [tdmm[dtl]])
            if dtl + 1 < NKD:
                wdma(dtl + 1, 1, [tdmm[dtl]])

            def mkoev(dtl=dtl, par=dtl % 2):
                def go(v):
                    return v.tensor_scalar(outsb[dtl % 2][:, :], PA[par][:, :],
                                           bbcolb[:, dtl:dtl + 1], None, ALU.add)
                return go
            wv = [tdmm[dtl]] + ([tdev[dtl - 2]] if dtl >= 2 else [])
            tev_ = E("vector", mkoev(), wv, sDV)
            consume(bank, tev_)
            for ts_ in range(4):
                tb = "PY%d" % (ts_ % 2)
                tt_o = tpose(tb, PY[ts_ % 2][:, 128:256], outsb[dtl % 2][:, ts_ * 128:(ts_ + 1) * 128],
                             ident32[:, :], [tev_])

                def mkres(dtl=dtl, ts_=ts_, dg=dg):
                    def go(v):
                        return v.tensor_add(outst[ts_][:, (dtl % 4) * 128:(dtl % 4 + 1) * 128],
                                            PY[ts_ % 2][:, 128:256],
                                            xtok[ts_][:, (dtl % 4) * 128:(dtl % 4 + 1) * 128])
                    return go
                wl3 = [tt_o, txk[(dg, ts_)]]
                if (dg - 1, ts_) in tflush:
                    wl3.append(tflush[(dg - 1, ts_)])
                tadds[(dtl, ts_)] = E("vector", mkres(), wl3, sDV)
                consume(tb, tadds[(dtl, ts_)])
            tdev[dtl] = tadds[(dtl, 3)]
            if dtl % 4 == 3:
                for ts_ in range(4):
                    tflush[(dg, ts_)] = sdma(
                        dout[r * TR + ts_ * 128:r * TR + (ts_ + 1) * 128,
                             dg * 512:(dg + 1) * 512],
                        outst[ts_][:, :], [tadds[(dtl, ts_)]])
                if dg + 1 < 8:
                    for ts_ in range(4):
                        xtdma(dg + 1, ts_, [tadds[(dtl, ts_)]])
        tend_pe = p.tok(sPE)
        tend_dv = p.tok(sDV)
        tend_dx = p.tok(sDX)
        tend_dw = p.tok(sDW)
        tend_ac = p.tok(sAC)
        return (tend_pe, tend_dv, tend_dx, tend_dw, tend_ac)

    prev_end = None
    for r in range(2 if stage >= 6 else 1):
        prev_end = round_r(r, prev_end)

    E("sync", lambda s_: s_.nop(), [p.tok(sDX)])
    E("gpsimd", lambda g: g.nop(), [p.tok(sDW)])

    def gapfn(e):
        e.tensor_copy(gapb[0:1, 0:1], gapb[0:1, 1:2])

    with nc.Block() as block:
        p.replay(block, gapfn=gapfn)
    st.close()
    return nc


def kernel(**inputs):
    sh = host_shared(inputs)
    nc = build()
    in_maps = []
    for core in range(8):
        m = dict(sh)
        m.update(host_percore(inputs, core))
        in_maps.append(m)
    res = run_bass_kernel_spmd(nc, in_maps, core_ids=list(range(8)))
    out = np.empty((B, S, D), np.float32)
    for core in range(8):
        b, q = core // 4, core % 4
        o = res.results[core]["out"]
        out[b, 512 * q:512 * q + 512] = o[0:512]
        out[b, 2048 + 512 * q:2048 + 512 * q + 512] = o[512:1024]
    return out
